# version 02
# baseline (speedup 1.0000x reference)
import math
import os
import numpy as np
import ml_dtypes
import concourse.bass as bass
import concourse.mybir as mybir
from concourse.bass import IndirectOffsetOnAxis
from concourse.bass_utils import run_bass_kernel_spmd
from contextlib import ExitStack

F32 = mybir.dt.float32
BF16 = mybir.dt.bfloat16
I32 = mybir.dt.int32
AF = mybir.ActivationFunctionType
ALU = mybir.AluOpType
AX = mybir.AxisListType

D = 2048
H = 16
DH = 128
B = 2
L = 8192
TOK = 2048
NT = 16
NBLK = 2
E = 8
NE = 9
EPS = 1e-5
ATT_SCALE = 1.0 / math.sqrt(DH)
NEG = -1e30
KILL = -1e38
NSLOT = 12
CROWS = 4104


def _sub_ranges(core):
    c = core % 4
    return c, 7 - c


KPH = int(os.environ.get("KPH", "9"))
KP6 = int(os.environ.get("KP6", "1"))
KDBG = int(os.environ.get("KDBG", "0"))
KNE = int(os.environ.get("KNE", str(NE)))
KEO = int(os.environ.get("KEO", "1"))


def build_nc():
    nc = bass.Bass("TRN2", target_bir_lowering=False, debug=False)

    def din(name, shape, dt=F32):
        return nc.dram_tensor(name, shape, dt, kind="ExternalInput").ap()

    x_own = din("x_own", [TOK, D])
    wqkv_t = din("wqkv_t", [32, 128, 16, 128])
    wv_t = din("wv_t", [16, 128, 2048])
    wo_t = din("wo_t", [16, 128, 2048])
    gate_t = din("gate_t", [16, 128, 8])
    w13_t = din("w13_t", [2, NE, 8, 128, 16, 128])
    w2_t = din("w2_t", [NE, 8, 128, 2048])
    cos_t = din("cos_t", [128, TOK])
    sin_t = din("sin_t", [128, TOK])
    trimask = din("trimask", [8, 128, 1024])
    padkill_in = din("padkill", [128, NSLOT])
    kvidx = din("kvidx", [128, 2 * NSLOT * H], I32)
    ident_in = din("ident", [128, 128])
    ropept_in = din("ropept", [128, 128])

    out = nc.dram_tensor("out", [TOK, D], F32, kind="ExternalOutput").ap()
    if KDBG:
        dbg_qT = nc.dram_tensor("dbg_qT", [H * 128, TOK], F32,
                                kind="ExternalOutput").ap()
        dbg_ct = nc.dram_tensor("dbg_ct", [CROWS, D], F32,
                                kind="ExternalOutput").ap()
        dbg_yT = nc.dram_tensor("dbg_yT", [NT * 128, H * 128], F32,
                                kind="ExternalOutput").ap()
        dbg_x2 = nc.dram_tensor("dbg_x2", [TOK, D], F32,
                                kind="ExternalOutput").ap()
        dbg_xnT = nc.dram_tensor("dbg_xnT", [16 * 128, TOK], F32,
                                 kind="ExternalOutput").ap()
        dbg_comb = nc.dram_tensor("dbg_comb", [128, 128], F32,
                                  kind="ExternalOutput").ap()
        dbg_q0 = nc.dram_tensor("dbg_q0", [128, 1024], F32,
                                kind="ExternalOutput").ap()
        dbg_rot = nc.dram_tensor("dbg_rot", [128, 1024], F32,
                                 kind="ExternalOutput").ap()
        dbg_rt1 = nc.dram_tensor("dbg_rt1", [128, 1024], F32,
                                 kind="ExternalOutput").ap()
        dbg_rt2 = nc.dram_tensor("dbg_rt2", [128, 1024], F32,
                                 kind="ExternalOutput").ap()
        dbg_cos = nc.dram_tensor("dbg_cos", [128, 1024], F32,
                                 kind="ExternalOutput").ap()
        dbg_sin = nc.dram_tensor("dbg_sin", [128, 1024], F32,
                                 kind="ExternalOutput").ap()
        dbg_g = nc.dram_tensor("dbg_g", [128, 8192], F32,
                               kind="ExternalOutput").ap()
        dbg_eo0 = nc.dram_tensor("dbg_eo0", [128, 16384], F32,
                                 kind="ExternalOutput").ap()
        dbg_eoF = nc.dram_tensor("dbg_eoF", [128, 16384], F32,
                                 kind="ExternalOutput").ap()

    xnT = nc.dram_tensor("xnT", [16, 128, TOK], BF16, kind="Internal").ap()
    qT = nc.dram_tensor("qT", [H, 128, TOK], BF16, kind="Internal").ap()
    contrib = nc.dram_tensor("contrib", [CROWS, D], BF16, kind="Internal").ap()
    gathered = nc.dram_tensor("gathered", [8 * CROWS, D], BF16, kind="Internal",
                              addr_space="Shared").ap()
    yT = nc.dram_tensor("yT", [NT, 128, H * 128], BF16, kind="Internal").ap()
    x2s = nc.dram_tensor("x2s", [TOK, D], F32, kind="Internal").ap()

    st = ExitStack()
    e_ = st.enter_context

    xnTb = nc.alloc_sbuf_tensor("xnTb", [128, 16 * 1024], BF16)
    stg = nc.alloc_sbuf_tensor("stg", [128, 2048], BF16)
    combsb = nc.alloc_sbuf_tensor("combsb", [128, 128], F32)
    padkill = nc.alloc_sbuf_tensor("padkill_sb", [128, NSLOT], F32)
    sidx = nc.alloc_sbuf_tensor("sidx", [128, 2 * NSLOT * H], I32)
    gate_sb = nc.alloc_sbuf_tensor("gate_sb", [128, 128], BF16)
    identf = nc.alloc_sbuf_tensor("identf", [128, 128], F32)
    ropept = nc.alloc_sbuf_tensor("ropept_sb", [128, 128], BF16)
    onescol = nc.alloc_sbuf_tensor("onescol", [128, 1], BF16)
    onesrow = nc.alloc_sbuf_tensor("onesrow", [1, 128], BF16)
    qn2 = nc.alloc_sbuf_tensor("qn2", [1, 16], F32)
    kn2own = nc.alloc_sbuf_tensor("kn2own", [1, 16], F32)
    kn2bf = nc.alloc_sbuf_tensor("kn2bf", [1, 16], BF16)
    knrow = nc.alloc_sbuf_tensor("knrow", [1, 128], BF16)
    kn2all = nc.alloc_sbuf_tensor("kn2all", [1, 16], F32)
    prodn = nc.alloc_sbuf_tensor("prodn", [1, 16], F32)
    cvec = nc.alloc_sbuf_tensor("cvec", [1, 16], F32)
    cbf = nc.alloc_sbuf_tensor("cbf", [1, 16], BF16)
    negCall = nc.alloc_sbuf_tensor("negCall", [128, 16], F32)
    biastab = nc.alloc_sbuf_tensor("biastab", [128, NSLOT], F32)
    nsc = [nc.alloc_sbuf_tensor(f"nsc{i}", [128, 1], F32) for i in range(8)]
    n1a = nc.alloc_sbuf_tensor("n1a", [1, 1], F32)
    n1b = nc.alloc_sbuf_tensor("n1b", [1, 1], F32)
    lrow = nc.alloc_sbuf_tensor("lrow", [1, 1024], F32)
    linv = nc.alloc_sbuf_tensor("linv", [1, 1024], F32)
    linvbf = nc.alloc_sbuf_tensor("linvbf", [1, 1024], BF16)
    epsb = nc.alloc_sbuf_tensor("epsb", [128, 1], F32)

    ARENA_KB = 136
    arena = nc.alloc_sbuf_tensor("arena", [128, ARENA_KB * 1024],
                                 mybir.dt.uint8)
    abase = nc.lookup_mloc(arena).addr

    def at(name, shape, dt, off_kb):
        return nc.alloc_sbuf_tensor_at(name, shape, dt,
                                       offset=abase + off_kb * 1024)

    wvsb = at("wvsb", [128, 16 * 2048], BF16, 0)
    cossb = at("cossb", [128, 2048], F32, 64)
    sinsb = at("sinsb", [128, 2048], F32, 72)
    wqbuf = [at(f"wqbuf{i}", [128, 2048], BF16, 80 + 4 * i) for i in range(2)]
    ropet1 = at("ropet1", [128, 1024], F32, 88)
    ropet2 = at("ropet2", [128, 1024], F32, 92)
    qkbf = [at(f"qkbf{i}", [128, 1024], BF16, 96 + 2 * i) for i in range(2)]
    ropedbf = [at(f"ropedbf{i}", [128, 1024], BF16, 100 + 2 * i)
               for i in range(2)]
    sqbf = at("sqbf", [128, 1024], BF16, 104)
    voutb = [at(f"voutb{i}", [128, 2048], BF16, 106 + 4 * i) for i in range(2)]
    masksb = at("masksb", [128, 8 * 1024], F32, 0)
    qbuf = at("qbuf", [128, 1024], BF16, 32)
    kbuf = [at(f"kbuf{i}", [128, 1024], BF16, 34 + 2 * i) for i in range(2)]
    vbuf = [at(f"vbuf{i}", [128, 1024], BF16, 38 + 2 * i) for i in range(2)]
    pbuf = [at(f"pbuf{i}", [128, 1024], BF16, 42 + 2 * i) for i in range(2)]
    ybuf = at("ybuf", [128, 1024], BF16, 46)
    linvb = at("linvb", [128, 1024], F32, 48)
    wosb = at("wosb", [128, 16 * 2048], BF16, 0)
    ytb = [at(f"ytb{i}", [128, 2048], BF16, 64 + 4 * i) for i in range(2)]
    xtile = [at(f"xtile{i}", [128, 2048], F32, 72 + 8 * i) for i in range(2)]
    xf = at("xf", [128, 2048], F32, 88)
    xnf = at("xnf", [128, 2048], F32, 96)
    aout = at("aout", [128, 2048], F32, 104)
    x2t = at("x2t", [128, 2048], F32, 112)
    eoacc = at("eoacc", [128, 8 * 2048], F32, 0)
    w2res = at("w2res", [128, 8 * 2048], BF16, 64)
    gbuf = at("gbuf", [128, 8 * 1024], BF16, 96)
    w1buf = [at(f"w1buf{i}", [128, 2048], BF16, 112 + 4 * i) for i in range(2)]
    w3buf = [at(f"w3buf{i}", [128, 2048], BF16, 120 + 4 * i) for i in range(2)]
    sgl = at("sgl", [128, 1024], F32, 128)
    eotmp = at("eotmp", [128, 1024], F32, 132)
    dbgbuf = at("dbgbuf", [128, 1024], F32, 114)
    p6a = at("p6a", [128, 2048], F32, 64)
    p6x2 = at("p6x2", [128, 2048], F32, 72)
    p6o = at("p6o", [128, 2048], F32, 80)

    pA = nc.alloc_psum_tensor("pA", [128, 1024], F32)
    pB = nc.alloc_psum_tensor("pB", [128, 1024], F32)
    pC = nc.alloc_psum_tensor("pC", [128, 1024], F32)
    pD = nc.alloc_psum_tensor("pD", [128, 512], F32)
    pE = nc.alloc_psum_tensor("pE", [128, 512], F32)
    psA = [pA, pB]

    QSP = e_(nc.semaphore("QSP"))
    QPL = e_(nc.semaphore("QPL"))
    CC = e_(nc.semaphore("CC"))
    SPE = e_(nc.semaphore("SPE"))
    SACT = e_(nc.semaphore("SACT"))
    SDVE = e_(nc.semaphore("SDVE"))
    SEMS = {"QSP": QSP, "QPL": QPL, "CC": CC, "PE": SPE, "ACT": SACT,
            "DVE": SDVE}

    recs = []
    cnt = dict(QSP=0, QPL=0, CC=0, PE=0, ACT=0, DVE=0)

    def rec(eng, emit, waits=(), inc=None):
        ws = tuple((s, v) for (s, v) in waits if v)
        for s, v in ws:
            assert v <= cnt[s], f"forward wait {s} {v} > {cnt[s]}"
        if inc:
            cnt[inc[0]] += inc[1]
        recs.append((eng, ws, emit, inc))
        return cnt[inc[0]] if inc else None

    def dma(q, out_ap, in_ap, waits=(), **kw):
        sem = "QSP" if q == "sp" else "QPL"
        return rec(q, lambda e: e.dma_start(out=out_ap, in_=in_ap, **kw),
                   waits, (sem, 16))

    def pe(emit, waits=(), bump=True):
        return rec("pe", emit, waits, ("PE", 1) if bump else None)

    def act(emit, waits=()):
        return rec("act", emit, waits, ("ACT", 1))

    def dve(emit, waits=()):
        return rec("dve", emit, waits, ("DVE", 1))

    def barrier():
        snap = dict(cnt)
        own = {"sp": "QSP", "pool": "QPL", "pe": "PE", "act": "ACT",
               "dve": "DVE"}
        for engname in ("sp", "pool", "pe", "act", "dve"):
            waits = tuple((s, v) for s, v in snap.items()
                          if s != own[engname] and s != "CC" and v > 0)
            recs.append((engname, waits, None, None))

    def finish():
        if KDBG:
            dbw = (("QPL", cnt["QPL"]), ("QSP", cnt["QSP"]),
                   ("PE", cnt["PE"]), ("ACT", cnt["ACT"]),
                   ("DVE", cnt["DVE"]))
            pairs = [(dbg_qT, qT.rearrange("h p t -> (h p) t")),
                     (dbg_ct, contrib[:, :]),
                     (dbg_yT, yT.rearrange("t p c -> (t p) c")),
                     (dbg_x2, x2s[:, :]),
                     (dbg_xnT, xnT.rearrange("a p t -> (a p) t")),
                     (dbg_comb, combsb[:, :])]
            for oi, (dst, srcap) in enumerate(pairs):
                dma("pool", dst[:, :], srcap, dbw if oi == 0 else ())
        recs.append(("pool", (("QPL", cnt["QPL"]),), None, None))
        recs.append(("sp", (("QSP", cnt["QSP"]),), None, None))
        with nc.Block() as block:
            def runner(engname):
                def _run(eng):
                    for rengine, waits, emit, inc in recs:
                        if rengine != engname:
                            continue
                        for s_, v in waits:
                            eng.wait_ge(SEMS[s_], v)
                        if emit is None:
                            continue
                        ins = emit(eng)
                        if inc:
                            ins.then_inc(SEMS[inc[0]], inc[1])
                return _run

            block.sync(runner("sp"))
            block.gpsimd(runner("pool"))
            block.tensor(runner("pe"))
            block.scalar(runner("act"))
            block.vector(runner("dve"))
        st.close()
        return nc

    dma("sp", padkill[:, :], padkill_in[:, :])
    dma("sp", sidx[:, :], kvidx[:, :])
    t_ident = dma("sp", identf[:, :], ident_in[:, :])
    dma("pool", ropept[:, :], ropept_in[:, :])
    for dc in range(16):
        dma("pool", gate_sb[:, dc * 8:(dc + 1) * 8], gate_t[dc, :, :])
    t_sin = 0
    dve(lambda e: e.memset(onescol[:, :], 1.0))
    dve(lambda e: e.memset(epsb[:, :], EPS))
    dve(lambda e: e.memset(onesrow[:, :], 1.0))
    dve(lambda e: e.memset(qn2[:, :], 0.0))
    dve(lambda e: e.memset(kn2own[:, :], 0.0))

    def emit_ln_tile(x_src, tok_src, src_sem, stg_war_qpl, prev_dve=0):
        w = ((src_sem, tok_src),)
        wp = w + ((("DVE", prev_dve),) if prev_dve else ())
        t1 = dve(lambda e: e.reduce_sum(nsc[0][:, :], x_src[:, :], axis=AX.X),
                 w)
        t2 = act(lambda e: e.activation(xf[:, :], x_src[:, :], AF.Square,
                                        accum_out=nsc[1][:, :]), wp)
        t3 = dve(lambda e: e.tensor_scalar_mul(nsc[2][:, :], nsc[0][:, :],
                                               1.0 / D), (("DVE", t1),))
        t4 = dve(lambda e: e.tensor_scalar_mul(nsc[3][:, :], nsc[1][:, :],
                                               1.0 / D), (("ACT", t2),))
        t5 = dve(lambda e: e.tensor_mul(nsc[4][:, :], nsc[2][:, :],
                                        nsc[2][:, :]), (("DVE", t3),))
        t6 = dve(lambda e: e.tensor_sub(nsc[5][:, :], nsc[3][:, :],
                                        nsc[4][:, :]),
                 (("DVE", t5), ("DVE", t4)))
        t7 = act(lambda e: e.activation(nsc[6][:, :], nsc[5][:, :], AF.Ln,
                                        bias=epsb[:, :]), (("DVE", t6),))
        t8 = act(lambda e: e.activation(nsc[7][:, :], nsc[6][:, :], AF.Exp,
                                        scale=-0.5), (("ACT", t7),))
        t9 = dve(lambda e: e.tensor_scalar(xf[:, :], x_src[:, :], nsc[2][:, :],
                                           None, ALU.subtract),
                 (("DVE", t3), ("ACT", t2)))
        t10 = dve(lambda e: e.tensor_scalar(xnf[:, :], xf[:, :], nsc[7][:, :],
                                            None, ALU.mult),
                  (("DVE", t9), ("ACT", t8)))
        tok_stg = 0
        for half in range(2):
            tok_pe = None
            for j in range(8):
                dcb = half * 8 + j
                wj = ()
                if j == 0:
                    wj = (("DVE", t10),) if half == 0 else (("DVE", tok_stg),)
                tok_pe = pe(
                    (lambda dcb, j: lambda e: e.transpose(
                        pC[:, j * 128:(j + 1) * 128],
                        xnf[:, dcb * 128:(dcb + 1) * 128],
                        identf[:, :]))(dcb, j), wj)
            sw = [("PE", tok_pe)]
            if half == 0 and stg_war_qpl:
                sw.append(("QPL", stg_war_qpl))
            tok_stg = dve(
                (lambda half: lambda e: e.tensor_copy(
                    stg[:, half * 1024:(half + 1) * 1024], pC[:, :]))(half),
                tuple(sw))
        return tok_stg

    t_xuse = [0, 0]
    t_stg_dma = 0
    t_p0prev = 0
    recs.append(("pe", (("QSP", t_ident),), None, None))
    for t in range(NT):
        wr = [("DVE", t_xuse[t % 2])] if t_xuse[t % 2] else []
        tok_x = dma("sp", xtile[t % 2][:, :], x_own[t * 128:(t + 1) * 128, :],
                    tuple(wr))
        tok_stg = emit_ln_tile(xtile[t % 2], tok_x, "QSP", t_stg_dma,
                               prev_dve=t_p0prev)
        t_xuse[t % 2] = tok_stg
        t_p0prev = tok_stg
        t_stg_dma = dma(
            "pool",
            xnT[:, :, t * 128:(t + 1) * 128].rearrange("dc p tj -> p dc tj"),
            stg.rearrange("p (dc tj) -> p dc tj", dc=16),
            (("DVE", tok_stg),))
    xnT_done = t_stg_dma
    barrier()

    if KPH < 1:
        return finish()
    dma("pool", cossb[:, :], cos_t[:, :])
    t_sin = dma("pool", sinsb[:, :], sin_t[:, :])
    t_wv = 0
    for dc in range(16):
        t_wv = dma("pool", wvsb[:, dc * 2048:(dc + 1) * 2048], wv_t[dc, :, :])
    t_wq = [0, 0]
    t_qkev = [0, 0]
    t_rope = [0, 0]
    t_mmtok = [0, 0]
    t_spill = [0, 0]
    t_tsq = [0, 0]
    t_pCr = 0
    t_norm_mm = 0
    t_normred = 0
    t_xnTb = 0
    for blk in range(NBLK):
        for dc in range(16):
            t_xnTb = dma("sp", xnTb[:, dc * 1024:(dc + 1) * 1024],
                         xnT[dc, :, blk * 1024:(blk + 1) * 1024],
                         (("QPL", xnT_done),) if (dc == 0 and blk == 0) else
                         ((("PE", cnt["PE"]),) if dc == 0 else ()))
        for m in range(32):
            i = m % 2
            t_wq[i] = dma("pool", wqbuf[i][:, :],
                          wqkv_t[m].rearrange("p dc fj -> p (dc fj)"),
                          (("PE", t_mmtok[i]),) if t_mmtok[i] else ())
            mmw = [("QPL", t_wq[i]), ("QSP", t_xnTb)]
            if t_qkev[i]:
                mmw.append(("DVE", t_qkev[i]))
            tok_mm = None
            for dc in range(16):
                for ns in range(2):
                    tok_mm = pe((lambda i, dc, ns: lambda e: e.matmul(
                        psA[i][:, ns * 512:(ns + 1) * 512],
                        wqbuf[i][:, dc * 128:(dc + 1) * 128],
                        xnTb[:, dc * 1024 + ns * 512:dc * 1024 + (ns + 1) * 512],
                        start=(dc == 0), stop=(dc == 15),
                        skip_group_check=True))(i, dc, ns),
                        tuple(mmw) if (dc == 0 and ns == 0) else ())
            t_mmtok[i] = tok_mm
            ev_w = [("PE", tok_mm)]
            if t_rope[i]:
                ev_w.append(("DVE", t_rope[i]))
            t_qkev[i] = dve((lambda i: lambda e: e.tensor_copy(
                qkbf[i][:, :], psA[i][:, :]))(i), tuple(ev_w))
            rw = [("DVE", t_qkev[i])]
            if t_pCr:
                rw.append(("DVE", t_pCr))
            tok_rot = None
            for ns in range(2):
                tok_rot = pe((lambda i, ns: lambda e: e.matmul(
                    pC[:, ns * 512:(ns + 1) * 512], ropept[:, :],
                    qkbf[i][:, ns * 512:(ns + 1) * 512],
                    start=True, stop=True, skip_group_check=True))(i, ns),
                    tuple(rw) if ns == 0 else ())
            cosw = (("QPL", t_sin),) if (blk == 0 and m == 0) else ()
            t_pCr = dve((lambda blk: lambda e: e.tensor_mul(
                ropet1[:, :], pC[:, :],
                sinsb[:, blk * 1024:(blk + 1) * 1024]))(blk),
                (("PE", tok_rot),) + cosw)
            tr2 = dve((lambda i, blk: lambda e: e.tensor_mul(
                ropet2[:, :], qkbf[i][:, :],
                cossb[:, blk * 1024:(blk + 1) * 1024]))(i, blk),
                (("DVE", t_pCr),))
            if KDBG and blk == 0 and m == 0:
                dma("pool", dbg_q0[:, :], qkbf[i][:, :], (("DVE", tr2),))
                t_pCr = dve(lambda e: e.tensor_copy(dbgbuf[:, :], pC[:, :]),
                            (("DVE", t_pCr),))
                dma("pool", dbg_rot[:, :], dbgbuf[:, :],
                    (("DVE", cnt["DVE"]),))
                dma("pool", dbg_rt1[:, :], ropet1[:, :], (("DVE", tr2),))
                dma("pool", dbg_rt2[:, :], ropet2[:, :], (("DVE", tr2),))
                dma("pool", dbg_cos[:, :], cossb[:, 0:1024],
                    (("QPL", t_sin),))
                dma("pool", dbg_sin[:, :], sinsb[:, 0:1024],
                    (("QPL", t_sin),))
            rope_w = [("DVE", tr2)]
            rope_w.append(("QPL", cnt["QPL"])) if (KDBG and blk == 0 and
                                                   m == 0) else None
            if t_spill[i]:
                rope_w.append(("QPL", t_spill[i]))
            if t_tsq[i]:
                rope_w.append(("ACT", t_tsq[i]))
            t_rope[i] = dve((lambda i: lambda e: e.tensor_add(
                ropedbf[i][:, :], ropet1[:, :], ropet2[:, :]))(i),
                tuple(rope_w))
            sq_w = [("DVE", t_rope[i])]
            if t_norm_mm:
                sq_w.append(("PE", t_norm_mm))
            t_tsq[i] = act((lambda i: lambda e: e.activation(
                sqbf[:, :], ropedbf[i][:, :], AF.Square))(i), tuple(sq_w))
            nmw = [("ACT", t_tsq[i])]
            if t_normred:
                nmw.append(("DVE", t_normred))
            pe(lambda e: e.matmul(pD[0:1, :], onescol[:, :], sqbf[:, 0:512],
                                  start=True, stop=True,
                                  skip_group_check=True), tuple(nmw))
            t_norm_mm = pe(lambda e: e.matmul(
                pE[0:1, :], onescol[:, :], sqbf[:, 512:1024],
                start=True, stop=True, skip_group_check=True))
            ta = dve(lambda e: e.reduce_max(n1a[:, :], pD[0:1, :], axis=AX.X),
                     (("PE", t_norm_mm),))
            tb = dve(lambda e: e.reduce_max(n1b[:, :], pE[0:1, :], axis=AX.X),
                     (("DVE", ta),))
            tc = dve(lambda e: e.tensor_max(n1a[:, :], n1a[:, :], n1b[:, :]),
                     (("DVE", tb),))
            dst = qn2 if m < 16 else kn2own
            hh = m % 16
            t_normred = dve((lambda dst, hh: lambda e: e.tensor_max(
                dst[0:1, hh:hh + 1], dst[0:1, hh:hh + 1], n1a[0:1, :]))(
                    dst, hh), (("DVE", tc),))
            if m < 16:
                t_spill[i] = dma("pool",
                                 qT[m, :, blk * 1024:(blk + 1) * 1024],
                                 ropedbf[i][:, :], (("DVE", t_rope[i]),))
            else:
                t_spill[i] = dma("pool",
                                 contrib[(m - 16) * 128:(m - 15) * 128,
                                         blk * 1024:(blk + 1) * 1024],
                                 ropedbf[i][:, :], (("DVE", t_rope[i]),))
        t_vev = [0, 0]
        t_vsp = [0, 0]
        for vt in range(8):
            t = blk * 8 + vt
            i = vt % 2
            vmw = [("QPL", t_wv), ("QSP", t_xnTb),
                   ("DVE", max(t_qkev)), ("DVE", max(t_vev))]
            tok_mm = None
            for dc in range(16):
                for ns in range(4):
                    ppp = psA[ns // 2]
                    tok_mm = pe((lambda dc, ns, vt, ppp: lambda e: e.matmul(
                        ppp[:, (ns % 2) * 512:((ns % 2) + 1) * 512],
                        xnTb[:, dc * 1024 + vt * 128:dc * 1024 + (vt + 1) * 128],
                        wvsb[:, dc * 2048 + ns * 512:dc * 2048 + (ns + 1) * 512],
                        start=(dc == 0), stop=(dc == 15),
                        skip_group_check=True))(dc, ns, vt, ppp),
                        tuple(vmw) if (dc == 0 and ns == 0) else ())
            evw = [("PE", tok_mm)]
            if t_vsp[i]:
                evw.append(("QPL", t_vsp[i]))
            dve((lambda i: lambda e: e.tensor_copy(
                voutb[i][:, 0:1024], pA[:, :]))(i), tuple(evw))
            t_vev[i] = dve((lambda i: lambda e: e.tensor_copy(
                voutb[i][:, 1024:2048], pB[:, :]))(i))
            t_vsp[i] = dma(
                "pool",
                contrib[2048:4096, :].rearrange(
                    "(h p) (tl d) -> p h tl d", p=128, d=128)[:, :, t, :],
                voutb[i].rearrange("p (h d) -> p h d", h=16),
                (("DVE", t_vev[i]),))
        t_qkev = [cnt["DVE"], cnt["DVE"]]
    tkb = dve(lambda e: e.tensor_copy(kn2bf[:, :], kn2own[:, :]),
              (("DVE", t_normred),))
    dma("pool", contrib[4096:4097, 0:16], kn2bf[0:1, :], (("DVE", tkb),))
    barrier()

    if KPH < 2:
        return finish()
    rec("pool", lambda e: e.collective_compute(
        "AllGather", ALU.bypass, replica_groups=[list(range(8))],
        ins=[contrib[:, :].opt()], outs=[gathered[:, :].opt()]),
        ((("QPL", cnt["QPL"])),), ("CC", 1))
    cc_tok = cnt["CC"]
    barrier()

    if KPH < 3:
        return finish()
    t_mask = 0
    for ck in range(8):
        t_mask = dma("sp", masksb[:, ck * 1024:(ck + 1) * 1024], trimask[ck])
    t_knr = 0
    for r in range(8):
        t_knr = dma("sp", knrow[0:1, r * 16:(r + 1) * 16],
                    gathered[4096 + r * CROWS:4097 + r * CROWS, 0:16],
                    (("CC", cc_tok),) if r == 0 else ())
    tk1 = dve(lambda e: e.reduce_max(
        kn2all[0:1, :],
        knrow[0:1, :].rearrange("a (r h) -> a h r", r=8), axis=AX.X),
        (("QSP", t_knr),))
    tk2 = dve(lambda e: e.tensor_mul(prodn[:, :], qn2[:, :], kn2all[:, :]),
              (("DVE", tk1),))
    tk3 = act(lambda e: e.activation(cvec[:, :], prodn[:, :], AF.Ln,
                                     scale=1.0 / DH), (("DVE", tk2),))
    tk4 = act(lambda e: e.activation(cvec[:, :], cvec[:, :], AF.Exp,
                                     scale=0.5), (("ACT", tk3),))
    tk5 = dve(lambda e: e.tensor_scalar_mul(cvec[:, :], cvec[:, :], 1.02),
              (("ACT", tk4),))
    tk6 = dve(lambda e: e.tensor_copy(cbf[:, :], cvec[:, :]), (("DVE", tk5),))
    tk7 = pe(lambda e: e.matmul(pD[:, 0:16], onesrow[:, :], cbf[0:1, :],
                                start=True, stop=True, skip_group_check=True),
             (("DVE", tk6),))
    tk8 = dve(lambda e: e.tensor_scalar_mul(negCall[:, :], pD[:, 0:16], -1.0),
              (("PE", tk7),))

    gview = gathered.rearrange("r c -> (r c)")[0:128 * 1024].rearrange(
        "(p c) -> p c", c=1024)

    t_kld = [0, 0]
    t_vld = [0, 0]
    t_pb = [0, 0]
    t_sps = [0, 0]
    t_ypc = 0
    t_yspill = 0
    t_pdpe = tk8
    for h in range(H):
        t_bias = dve((lambda h: lambda e: e.tensor_scalar(
            biastab[:, :], padkill[:, :], negCall[:, h:h + 1], None,
            ALU.add))(h), (("DVE", tk8),))
        for s in range(2):
            t_qld = dma("sp", qbuf[:, :], qT[h, :, s * 1024:(s + 1) * 1024],
                        (("PE", cnt["PE"]),))
            nslots = 4 if s == 0 else 8
            first = True
            tok_l = 0
            tok_pv = 0
            for j in range(nslots):
                slot = j if s == 0 else 4 + j
                jj = j % 2
                colk = (0 * NSLOT + slot) * H + h
                colv = (1 * NSLOT + slot) * H + h
                kw = [("CC", cc_tok)]
                if t_pb[jj]:
                    kw.append(("PE", t_pb[jj]))
                t_kld[jj] = rec("pool", (lambda jj, colk: lambda e:
                                e.indirect_dma_start(
                                    kbuf[jj][:, :], None, gview,
                                    IndirectOffsetOnAxis(
                                        sidx[:, colk:colk + 1], 0)))(jj, colk),
                                tuple(kw), ("QPL", 16))
                t_vld[jj] = rec("pool", (lambda jj, colv: lambda e:
                                e.indirect_dma_start(
                                    vbuf[jj][:, :], None, gview,
                                    IndirectOffsetOnAxis(
                                        sidx[:, colv:colv + 1], 0)))(jj, colv),
                                (), ("QPL", 16))
                is_diag = (s == 0 and j == 3) or (s == 1 and j == 7)
                for ck in range(8):
                    par = ck % 2
                    sw = [("QPL", t_kld[jj]), ("QSP", t_qld)]
                    if t_sps[par]:
                        sw.append(("ACT", t_sps[par]))
                    if first and ck < 2:
                        sw.append(("DVE", t_ypc))
                    tok_s = None
                    for ns in range(2):
                        tok_s = pe((lambda par, jj, ck, ns: lambda e: e.matmul(
                            psA[par][:, ns * 512:(ns + 1) * 512],
                            kbuf[jj][:, ck * 128:(ck + 1) * 128],
                            qbuf[:, ns * 512:(ns + 1) * 512],
                            start=True, stop=True,
                            skip_group_check=True))(par, jj, ck, ns),
                            tuple(sw) if ns == 0 else ())
                    ew = [("PE", tok_s)]
                    if is_diag:
                        tmsk = dve((lambda par, ck: lambda e: e.tensor_add(
                            psA[par][:, :], psA[par][:, :],
                            masksb[:, ck * 1024:(ck + 1) * 1024]))(par, ck),
                            (("PE", tok_s), ("QSP", t_mask)))
                        ew = [("DVE", tmsk)]
                    pw = ew + [("DVE", t_bias)]
                    if t_pb[jj]:
                        pw.append(("PE", t_pb[jj]))
                    t_sps[par] = act((lambda par, jj, slot: lambda e:
                                      e.activation(
                                          pbuf[jj][:, :], psA[par][:, :],
                                          AF.Exp,
                                          bias=biastab[:, slot:slot + 1],
                                          scale=ATT_SCALE))(par, jj, slot),
                                     tuple(pw))
                    pvw = [("ACT", t_sps[par]), ("QPL", t_vld[jj])]
                    if first:
                        pvw.append(("DVE", t_ypc))
                    st_f = first
                    sp_f = (j == nslots - 1 and ck == 7)
                    for ns in range(2):
                        tok_pv = pe((lambda jj, ck, ns, st_f, sp_f:
                                     lambda e: e.matmul(
                                         pC[:, ns * 512:(ns + 1) * 512],
                                         vbuf[jj][:, ck * 128:(ck + 1) * 128],
                                         pbuf[jj][:, ns * 512:(ns + 1) * 512],
                                         start=st_f, stop=sp_f,
                                         skip_group_check=True))(
                                             jj, ck, ns, st_f, sp_f),
                                    tuple(pvw) if ns == 0 else ())
                    lw = (("DVE", t_pdpe),) if first else ()
                    lps = [pD, pE]
                    for ns in range(2):
                        tok_l = pe((lambda jj, ns, st_f, sp_f, lp:
                                    lambda e: e.matmul(
                                        lp[0:1, :], onescol[:, :],
                                        pbuf[jj][:, ns * 512:(ns + 1) * 512],
                                        start=st_f, stop=sp_f,
                                        skip_group_check=True))(
                                            jj, ns, st_f, sp_f, lps[ns]),
                                   lw if ns == 0 else (), bump=(ns == 1))
                    t_pb[jj] = tok_l
                    first = False
            e1 = dve(lambda e: e.tensor_copy(lrow[0:1, 0:512], pD[0:1, :]),
                     (("PE", tok_l),))
            e2 = dve(lambda e: e.tensor_copy(lrow[0:1, 512:1024], pE[0:1, :]),
                     (("DVE", e1),))
            t_pdpe = e2
            e3 = dve(lambda e: e.reciprocal(linv[0:1, :], lrow[0:1, :]),
                     (("DVE", e2),))
            e4 = dve(lambda e: e.tensor_copy(linvbf[0:1, :], linv[0:1, :]),
                     (("DVE", e3),))
            bcw = [("DVE", e4), ("ACT", t_sps[0])]
            bc = None
            for ns in range(2):
                bc = pe((lambda ns: lambda e: e.matmul(
                    pA[:, ns * 512:(ns + 1) * 512], onesrow[:, :],
                    linvbf[0:1, ns * 512:(ns + 1) * 512],
                    start=True, stop=True, skip_group_check=True))(ns),
                    tuple(bcw) if ns == 0 else ())
            e5 = dve(lambda e: e.tensor_copy(linvb[:, :], pA[:, :]),
                     (("PE", bc),))
            yw = [("DVE", e5), ("PE", tok_pv)]
            if t_yspill:
                yw.append(("QPL", t_yspill))
            t_ypc = dve(lambda e: e.tensor_mul(ybuf[:, :], pC[:, :],
                                               linvb[:, :]), tuple(yw))
            t_sps[0] = cnt["ACT"]
            t_yspill = dma(
                "pool",
                yT[:, :, h * 128:(h + 1) * 128].rearrange(
                    "tl p d -> p tl d")[:, s * 8:(s + 1) * 8, :],
                ybuf.rearrange("p (tl d) -> p tl d", d=128),
                (("DVE", t_ypc),))
    barrier()

    if KPH < 4:
        return finish()
    t_wo = 0
    for dc in range(16):
        t_wo = dma("pool", wosb[:, dc * 2048:(dc + 1) * 2048], wo_t[dc])
    t_yt = [0, 0]
    t_aev = 0
    t_gate_dve = 0
    xnT2_wtok = [0] * NT
    t_x2w = [0] * NT
    t_stg2_dma = 0
    t_p4prev = 0
    for t in range(NT):
        i = t % 2
        t_ytld = dma("sp", ytb[i][:, :], yT[t],
                     (("PE", t_yt[i]),) if t_yt[i] else ())
        t_xld = dma("sp", xtile[i][:, :], x_own[t * 128:(t + 1) * 128, :],
                    (("DVE", t_xuse[i]),) if t_xuse[i] else ())
        mw = [("QPL", t_wo), ("QSP", t_ytld)]
        if t_aev:
            mw.append(("DVE", t_aev))
        tok_mm = None
        for hh in range(16):
            for ns in range(4):
                ppp = psA[ns // 2]
                tok_mm = pe((lambda i, hh, ns, ppp: lambda e: e.matmul(
                    ppp[:, (ns % 2) * 512:((ns % 2) + 1) * 512],
                    ytb[i][:, hh * 128:(hh + 1) * 128],
                    wosb[:, hh * 2048 + ns * 512:hh * 2048 + (ns + 1) * 512],
                    start=(hh == 0), stop=(hh == 15),
                    skip_group_check=True))(i, hh, ns, ppp),
                    tuple(mw) if (hh == 0 and ns == 0) else ())
        t_yt[i] = tok_mm
        a1 = dve(lambda e: e.tensor_copy(aout[:, 0:1024], pA[:, :]),
                 (("PE", tok_mm),))
        t_aev = dve(lambda e: e.tensor_copy(aout[:, 1024:2048], pB[:, :]),
                    (("DVE", a1),))
        l1 = dve(lambda e: e.reduce_sum(nsc[0][:, :], aout[:, :], axis=AX.X),
                 (("DVE", t_aev),))
        l2 = act(lambda e: e.activation(xf[:, :], aout[:, :], AF.Square,
                                        accum_out=nsc[1][:, :]),
                 (("DVE", t_aev),))
        l3 = dve(lambda e: e.tensor_scalar_mul(nsc[2][:, :], nsc[0][:, :],
                                               1.0 / D), (("DVE", l1),))
        l4 = dve(lambda e: e.tensor_scalar_mul(nsc[3][:, :], nsc[1][:, :],
                                               1.0 / D), (("ACT", l2),))
        l5 = dve(lambda e: e.tensor_mul(nsc[4][:, :], nsc[2][:, :],
                                        nsc[2][:, :]), (("DVE", l3),))
        l6 = dve(lambda e: e.tensor_sub(nsc[5][:, :], nsc[3][:, :],
                                        nsc[4][:, :]),
                 (("DVE", l5), ("DVE", l4)))
        l7 = act(lambda e: e.activation(nsc[6][:, :], nsc[5][:, :], AF.Ln,
                                        bias=epsb[:, :]), (("DVE", l6),))
        l8 = act(lambda e: e.activation(nsc[7][:, :], nsc[6][:, :], AF.Exp,
                                        scale=-0.5), (("ACT", l7),))
        l9 = dve(lambda e: e.tensor_scalar(xf[:, :], aout[:, :], nsc[2][:, :],
                                           None, ALU.subtract),
                 (("DVE", l3), ("ACT", l2)))
        l10w = [("DVE", l9), ("ACT", l8), ("QSP", t_xld)]
        if t > 0:
            l10w.append(("QPL", t_x2w[t - 1]))
        l10 = dve((lambda i: lambda e: e.scalar_tensor_tensor(
            x2t[:, :], xf[:, :], nsc[7][:, :], xtile[i][:, :],
            ALU.mult, ALU.add))(i), tuple(l10w))
        t_xuse[i] = l10
        t_x2w[t] = dma("pool", x2s[t * 128:(t + 1) * 128, :], x2t[:, :],
                       (("DVE", l10),))
        tok_stg2 = emit_ln_tile(x2t, l10, "DVE", t_stg2_dma,
                                prev_dve=t_p4prev)
        t_stg2_dma = dma(
            "pool",
            xnT[:, :, t * 128:(t + 1) * 128].rearrange("dc p tj -> p dc tj"),
            stg.rearrange("p (dc tj) -> p dc tj", dc=16),
            (("DVE", tok_stg2),))
        xnT2_wtok[t] = t_stg2_dma
        t_p4prev = tok_stg2
        gw = [("DVE", tok_stg2)]
        if t_gate_dve:
            gw.append(("DVE", t_gate_dve))
        tok_g = None
        for dc in range(16):
            tok_g = pe((lambda dc: lambda e: e.matmul(
                pD[:, 0:8], stg[:, dc * 128:(dc + 1) * 128],
                gate_sb[:, dc * 8:(dc + 1) * 8],
                start=(dc == 0), stop=(dc == 15),
                skip_group_check=True))(dc),
                tuple(gw) if dc == 0 else ())
        g1 = dve(lambda e: e.reduce_max(nsc[0][:, :], pD[:, 0:8], axis=AX.X),
                 (("PE", tok_g),))
        g2 = dve(lambda e: e.tensor_scalar_mul(nsc[1][:, :], nsc[0][:, :],
                                               -1.0), (("DVE", g1),))
        g3 = act(lambda e: e.activation(xnf[:, 0:8], pD[:, 0:8], AF.Exp,
                                        bias=nsc[1][:, :]), (("DVE", g2),))
        g4 = dve(lambda e: e.reduce_sum(nsc[2][:, :], xnf[:, 0:8], axis=AX.X),
                 (("ACT", g3),))
        g5 = dve(lambda e: e.reciprocal(nsc[3][:, :], nsc[2][:, :]),
                 (("DVE", g4),))
        g6 = dve(lambda e: e.tensor_scalar(xnf[:, 8:16], xnf[:, 0:8],
                                           nsc[3][:, :], None, ALU.mult),
                 (("DVE", g5),))
        g7 = dve(lambda e: e.reduce_max(nsc[4][:, :], xnf[:, 8:16],
                                        axis=AX.X), (("DVE", g6),))
        g8 = dve(lambda e: e.tensor_scalar(xnf[:, 16:24], xnf[:, 8:16],
                                           nsc[4][:, :], None, ALU.is_ge),
                 (("DVE", g7),))
        g9 = dve(lambda e: e.scalar_tensor_tensor(
            xnf[:, 24:32], xnf[:, 16:24], KILL, xnf[:, 8:16],
            ALU.mult, ALU.add), (("DVE", g8),))
        g10 = dve(lambda e: e.reduce_max(nsc[5][:, :], xnf[:, 24:32],
                                         axis=AX.X), (("DVE", g9),))
        g11 = dve(lambda e: e.tensor_scalar(xnf[:, 16:24], xnf[:, 8:16],
                                            nsc[5][:, :], None, ALU.is_ge),
                  (("DVE", g10),))
        t_gate_dve = dve((lambda t: lambda e: e.tensor_mul(
            combsb[:, t * 8:(t + 1) * 8], xnf[:, 16:24], xnf[:, 8:16]))(t),
            (("DVE", g11),))
    barrier()

    if KPH < 5:
        return finish()
    t_gdve = [0, 0]
    t_hmm = [0, 0]
    t_w2res = 0
    t_eo = 0
    t_eodve = 0
    t_outprev = 0
    for blk in range(NBLK):
        for dc in range(16):
            t_xnTb = dma("sp", xnTb[:, dc * 1024:(dc + 1) * 1024],
                         xnT[dc, :, blk * 1024:(blk + 1) * 1024],
                         (("QPL", xnT2_wtok[NT - 1]), ("PE", cnt["PE"]))
                         if dc == 0 else ())
        for ee in range(KNE):
            t_sil = 0
            for ic in range(8):
                i = ic % 2
                w1w = [("PE", t_hmm[i])] if t_hmm[i] else []
                t_w1 = dma("pool", w1buf[i][:, :],
                           w13_t[0, ee, ic].rearrange("p dc ij -> p (dc ij)"),
                           tuple(w1w))
                t_w3 = dma("pool", w3buf[i][:, :],
                           w13_t[1, ee, ic].rearrange("p dc ij -> p (dc ij)"))
                hw = [("QPL", t_w3), ("QSP", t_xnTb)]
                if t_sil:
                    hw.append(("ACT", t_sil))
                if ic == 0 and t_eodve:
                    hw.append(("DVE", t_eodve))
                tok_h = None
                for dc in range(16):
                    for ns in range(2):
                        tok_h = pe((lambda i, dc, ns: lambda e: e.matmul(
                            pA[:, ns * 512:(ns + 1) * 512],
                            w1buf[i][:, dc * 128:(dc + 1) * 128],
                            xnTb[:, dc * 1024 + ns * 512:
                                 dc * 1024 + (ns + 1) * 512],
                            start=(dc == 0), stop=(dc == 15),
                            skip_group_check=True))(i, dc, ns),
                            tuple(hw) if (dc == 0 and ns == 0) else ())
                h3w = [("DVE", t_gdve[1 - i])] if t_gdve[1 - i] else []
                if ic == 0 and t_eodve:
                    h3w.append(("DVE", t_eodve))
                tok_h3 = None
                for dc in range(16):
                    for ns in range(2):
                        tok_h3 = pe((lambda i, dc, ns: lambda e: e.matmul(
                            pB[:, ns * 512:(ns + 1) * 512],
                            w3buf[i][:, dc * 128:(dc + 1) * 128],
                            xnTb[:, dc * 1024 + ns * 512:
                                 dc * 1024 + (ns + 1) * 512],
                            start=(dc == 0), stop=(dc == 15),
                            skip_group_check=True))(i, dc, ns),
                            tuple(h3w) if (dc == 0 and ns == 0) else ())
                t_hmm[i] = tok_h3
                silw = [("PE", tok_h)]
                if t_gdve[1 - i]:
                    silw.append(("DVE", t_gdve[1 - i]))
                t_sil = act(lambda e: e.activation(sgl[:, :], pA[:, :],
                                                   AF.Silu), tuple(silw))
                t_gdve[i] = dve((lambda ic: lambda e: e.tensor_mul(
                    gbuf[:, ic * 1024:(ic + 1) * 1024], sgl[:, :],
                    pB[:, :]))(ic), (("ACT", t_sil), ("PE", tok_h3)))
            if not KEO:
                continue
            w2w = [("DVE", cnt["DVE"])]
            if t_eo:
                w2w.append(("PE", t_eo))
            for ic in range(8):
                t_w2res = dma("pool", w2res[:, ic * 2048:(ic + 1) * 2048],
                              w2_t[ee, ic], tuple(w2w) if ic == 0 else ())
            last_g = max(t_gdve)
            if KDBG and blk == 0 and ee == 0:
                dma("pool", dbg_g[:, :], gbuf[:, :], (("DVE", last_g),))
            for tt in range(8):
                t = blk * 8 + tt
                ew = [("QPL", t_w2res), ("DVE", last_g)]
                if t_eodve:
                    ew.append(("DVE", t_eodve))
                tok_eo = None
                for ic in range(8):
                    for ds in range(4):
                        ppp = psA[ds // 2]
                        tok_eo = pe((lambda ic, ds, tt, ppp: lambda e:
                                     e.matmul(
                                         ppp[:, (ds % 2) * 512:
                                             ((ds % 2) + 1) * 512],
                                         gbuf[:, ic * 1024 + tt * 128:
                                              ic * 1024 + (tt + 1) * 128],
                                         w2res[:, ic * 2048 + ds * 512:
                                               ic * 2048 + (ds + 1) * 512],
                                         start=(ic == 0), stop=(ic == 7),
                                         skip_group_check=True))(
                                             ic, ds, tt, ppp),
                                    tuple(ew) if (ic == 0 and ds == 0) else ())
                t_eo = tok_eo
                accw = (("PE", tok_eo),)
                for half in range(2):
                    ppp = psA[half]
                    dst = eoacc[:, tt * 2048 + half * 1024:
                                tt * 2048 + (half + 1) * 1024]
                    cs = combsb[:, t * 8 + min(ee, 7):t * 8 + min(ee, 7) + 1]
                    if ee == 0:
                        t_eodve = dve((lambda dst, ppp, cs: lambda e:
                                       e.tensor_scalar(dst, ppp[:, :], cs,
                                                       None, ALU.mult))(
                                           dst, ppp, cs),
                                      accw if half == 0 else ())
                    elif ee < E:
                        dve((lambda ppp, cs: lambda e:
                             e.tensor_scalar(eotmp[:, :], ppp[:, :], cs,
                                             None, ALU.mult))(ppp, cs),
                            accw if half == 0 else ())
                        t_eodve = dve((lambda dst: lambda e:
                                       e.tensor_add(dst, dst,
                                                    eotmp[:, :]))(dst))
                    else:
                        t_eodve = dve((lambda dst, ppp: lambda e:
                                       e.tensor_add(dst, dst, ppp[:, :]))(
                                           dst, ppp),
                                      accw if half == 0 else ())
            if KDBG and blk == 0 and ee == 0 and tt == 7:
                dma("pool", dbg_eo0[:, :], eoacc[:, :], (("DVE", t_eodve),))
        if KDBG and blk == 0:
            dma("pool", dbg_eoF[:, :], eoacc[:, :], (("DVE", t_eodve),))
        t_p6prev = 0
        for tt in range(8 if KP6 else 0):
            t = blk * 8 + tt
            eot = eoacc[:, tt * 2048:(tt + 1) * 2048]
            f1 = dve((lambda eot: lambda e: e.reduce_sum(
                nsc[0][:, :], eot, axis=AX.X))(eot), (("DVE", t_eodve),))
            f2 = act((lambda eot: lambda e: e.activation(
                p6a[:, :], eot, AF.Square, accum_out=nsc[1][:, :]))(eot),
                (("DVE", t_eodve), ("QPL", t_w2res),
                 ("DVE", t_p6prev)))
            f3 = dve(lambda e: e.tensor_scalar_mul(nsc[2][:, :], nsc[0][:, :],
                                                   1.0 / D), (("DVE", f1),))
            f4 = dve(lambda e: e.tensor_scalar_mul(nsc[3][:, :], nsc[1][:, :],
                                                   1.0 / D), (("ACT", f2),))
            f5 = dve(lambda e: e.tensor_mul(nsc[4][:, :], nsc[2][:, :],
                                            nsc[2][:, :]), (("DVE", f3),))
            f6 = dve(lambda e: e.tensor_sub(nsc[5][:, :], nsc[3][:, :],
                                            nsc[4][:, :]),
                     (("DVE", f5), ("DVE", f4)))
            f7 = act(lambda e: e.activation(nsc[6][:, :], nsc[5][:, :], AF.Ln,
                                            bias=epsb[:, :]), (("DVE", f6),))
            f8 = act(lambda e: e.activation(nsc[7][:, :], nsc[6][:, :],
                                            AF.Exp, scale=-0.5),
                     (("ACT", f7),))
            t_x2r = dma("sp", p6x2[:, :], x2s[t * 128:(t + 1) * 128, :],
                        (("QPL", t_x2w[t]), ("DVE", t_eodve),
                         ("DVE", t_p6prev)))
            f9 = dve((lambda eot: lambda e: e.tensor_scalar(
                p6a[:, :], eot, nsc[2][:, :], None, ALU.subtract))(eot),
                (("DVE", f3), ("ACT", f2)))
            f10w = [("DVE", f9), ("DVE", f6), ("ACT", f8), ("QSP", t_x2r)]
            if t_outprev:
                f10w.append(("QPL", t_outprev))
            f10 = dve(lambda e: e.scalar_tensor_tensor(
                p6o[:, :], p6a[:, :], nsc[7][:, :], p6x2[:, :],
                ALU.mult, ALU.add), tuple(f10w))
            t_outprev = dma("pool", out[t * 128:(t + 1) * 128, :], p6o[:, :],
                            (("DVE", f10),))
            t_p6prev = f10

    return finish()



def _host_prepare(inputs):
    W_qkv = np.asarray(inputs["L0_attn_W_qkv"], np.float32)
    W_o = np.asarray(inputs["L0_attn_W_o"], np.float32)
    gate_w = np.asarray(inputs["L0_ffn_gate_w"], np.float32)
    w1 = np.asarray(inputs["L0_ffn_w1"], np.float32)
    w2 = np.asarray(inputs["L0_ffn_w2"], np.float32)
    w3 = np.asarray(inputs["L0_ffn_w3"], np.float32)
    ws1 = np.asarray(inputs["L0_ffn_ws1"], np.float32)
    ws2 = np.asarray(inputs["L0_ffn_ws2"], np.float32)
    ws3 = np.asarray(inputs["L0_ffn_ws3"], np.float32)

    shared = {}
    shared["wqkv_t"] = np.ascontiguousarray(
        W_qkv[:4096].reshape(32, 128, 16, 128).transpose(0, 3, 2, 1))
    shared["wv_t"] = np.ascontiguousarray(
        W_qkv[4096:].reshape(2048, 16, 128).transpose(1, 2, 0))
    shared["wo_t"] = np.ascontiguousarray(
        W_o.reshape(2048, 16, 128).transpose(1, 2, 0))
    shared["gate_t"] = np.ascontiguousarray(
        gate_w.reshape(8, 16, 128).transpose(1, 2, 0))
    w13 = np.zeros((2, NE, 8, 128, 16, 128), np.float32)
    for wi, warr, wsarr in ((0, w1, ws1), (1, w3, ws3)):
        a = warr.reshape(8, 8, 128, 16, 128)
        w13[wi, :8] = a.transpose(0, 1, 4, 3, 2)
        b = wsarr.reshape(8, 128, 16, 128)
        w13[wi, 8] = b.transpose(0, 3, 2, 1)
    shared["w13_t"] = np.ascontiguousarray(w13)
    w2t = np.zeros((NE, 8, 128, 2048), np.float32)
    a = w2.reshape(8, 4, 512, 8, 128)
    w2t[:8] = a.transpose(0, 3, 4, 1, 2).reshape(8, 8, 128, 2048)
    b = ws2.reshape(4, 512, 8, 128)
    w2t[8] = b.transpose(2, 3, 0, 1).reshape(8, 128, 2048)
    shared["w2_t"] = np.ascontiguousarray(w2t)
    tm = np.zeros((8, 128, 1024), np.float32)
    qq = np.arange(1024)[None, :]
    for ck in range(8):
        kk = ck * 128 + np.arange(128)[:, None]
        tm[ck] = np.where(qq >= kk, 0.0, NEG)
    shared["trimask"] = tm
    shared["ident"] = np.eye(128, dtype=np.float32)
    Pm = np.zeros((128, 128), np.float32)
    for j in range(64):
        Pm[2 * j, 2 * j + 1] = -1.0
        Pm[2 * j + 1, 2 * j] = 1.0
    shared["ropept"] = np.ascontiguousarray(Pm.T)

    invf = 1.0 / (10000.0 ** (np.arange(0, DH, 2, dtype=np.float64) / DH))
    pos = np.arange(L, dtype=np.float64)
    ang = pos[None, :] * np.repeat(invf, 2)[:, None]
    cos_all = np.cos(ang).astype(np.float32)
    sin_all = np.sin(ang).astype(np.float32)

    per_core = []
    x = np.asarray(inputs["L0_attn_x"], np.float32)
    for core in range(8):
        b, c = core // 4, core % 4
        lo, hi = _sub_ranges(core)
        rows = np.r_[1024 * lo:1024 * (lo + 1), 1024 * hi:1024 * (hi + 1)]
        d = dict(shared)
        d["x_own"] = np.ascontiguousarray(x[b].reshape(L, D)[rows])
        d["cos_t"] = np.ascontiguousarray(cos_all[:, rows])
        d["sin_t"] = np.ascontiguousarray(sin_all[:, rows])
        pk = np.zeros((128, NSLOT), np.float32)
        for j in range(3):
            if j >= c:
                pk[:, j] = KILL
        for j in range(7):
            if j >= 7 - c:
                pk[:, 4 + j] = KILL
        d["padkill"] = pk
        idx = np.zeros((128, 2 * NSLOT * H), np.int32)
        p = np.arange(128)

        def kt_of(s, j):
            if s == 0:
                return c if j == 3 else j
            return (7 - c) if j == 7 else j

        for s in range(2):
            nsl = 4 if s == 0 else 8
            for j in range(nsl):
                slot = j if s == 0 else 4 + j
                kt = kt_of(s, j)
                rho = 4 * b + (kt if kt <= 3 else 7 - kt)
                halfbit = 1 if kt > 3 else 0
                for h in range(H):
                    kr = rho * CROWS + h * 128 + p
                    vr = rho * CROWS + 2048 + h * 128 + p
                    idx[:, (0 * NSLOT + slot) * H + h] = kr * 2 + halfbit
                    idx[:, (1 * NSLOT + slot) * H + h] = vr * 2 + halfbit
        d["kvidx"] = idx
        per_core.append(d)
    return per_core


_NC_CACHE = {}


def kernel(**inputs):
    in_maps = _host_prepare(inputs)
    if "nc" not in _NC_CACHE:
        _NC_CACHE["nc"] = build_nc()
    nc = _NC_CACHE["nc"]
    res = run_bass_kernel_spmd(nc, in_maps, core_ids=list(range(8)))
    out = np.empty((B, L, D), np.float32)
    for core in range(8):
        b = core // 4
        lo, hi = _sub_ranges(core)
        o = np.asarray(res.results[core]["out"], np.float32)
        out[b, 1024 * lo:1024 * (lo + 1)] = o[:1024]
        out[b, 1024 * hi:1024 * (hi + 1)] = o[1024:]
    return out


# version 03
# speedup vs baseline: 1.1657x; 1.1657x over previous
import math
import os
import numpy as np
import ml_dtypes
import concourse.bass as bass
import concourse.mybir as mybir
from concourse.bass import IndirectOffsetOnAxis
from concourse.bass_utils import run_bass_kernel_spmd
from contextlib import ExitStack

F32 = mybir.dt.float32
BF16 = mybir.dt.bfloat16
I32 = mybir.dt.int32
AF = mybir.ActivationFunctionType
ALU = mybir.AluOpType
AX = mybir.AxisListType

D = 2048
H = 16
DH = 128
B = 2
L = 8192
TOK = 2048
NT = 16
NBLK = 2
E = 8
NE = 9
EPS = 1e-5
ATT_SCALE = 1.0 / math.sqrt(DH)
NEG = -1e30
KILL = -1e38
NSLOT = 12
CROWS = 4104


def _sub_ranges(core):
    c = core % 4
    return c, 7 - c


KPH = int(os.environ.get("KPH", "9"))
KP6 = int(os.environ.get("KP6", "1"))
KDBG = int(os.environ.get("KDBG", "0"))
KNE = int(os.environ.get("KNE", str(NE)))
KEO = int(os.environ.get("KEO", "1"))


def build_nc():
    nc = bass.Bass("TRN2", target_bir_lowering=False, debug=False)

    def din(name, shape, dt=F32):
        return nc.dram_tensor(name, shape, dt, kind="ExternalInput").ap()

    x_own = din("x_own", [TOK, D])
    wqkv_t = din("wqkv_t", [32, 128, 16, 128])
    wv_t = din("wv_t", [16, 128, 2048])
    wo_t = din("wo_t", [16, 128, 2048])
    gate_t = din("gate_t", [16, 128, 8])
    w13_t = din("w13_t", [2, NE, 8, 128, 16, 128])
    w2_t = din("w2_t", [NE, 8, 128, 2048])
    cos_t = din("cos_t", [128, TOK])
    sin_t = din("sin_t", [128, TOK])
    trimask = din("trimask", [8, 128, 1024])
    padkill_in = din("padkill", [128, NSLOT])
    kvidx = din("kvidx", [128, 2 * NSLOT * H], I32)
    ident_in = din("ident", [128, 128])
    ropept_in = din("ropept", [128, 128])

    out = nc.dram_tensor("out", [TOK, D], F32, kind="ExternalOutput").ap()
    if KDBG:
        dbg_qT = nc.dram_tensor("dbg_qT", [H * 128, TOK], F32,
                                kind="ExternalOutput").ap()
        dbg_ct = nc.dram_tensor("dbg_ct", [CROWS, D], F32,
                                kind="ExternalOutput").ap()
        dbg_yT = nc.dram_tensor("dbg_yT", [NT * 128, H * 128], F32,
                                kind="ExternalOutput").ap()
        dbg_x2 = nc.dram_tensor("dbg_x2", [TOK, D], F32,
                                kind="ExternalOutput").ap()
        dbg_xnT = nc.dram_tensor("dbg_xnT", [16 * 128, TOK], F32,
                                 kind="ExternalOutput").ap()
        dbg_comb = nc.dram_tensor("dbg_comb", [128, 128], F32,
                                  kind="ExternalOutput").ap()
        dbg_q0 = nc.dram_tensor("dbg_q0", [128, 1024], F32,
                                kind="ExternalOutput").ap()
        dbg_rot = nc.dram_tensor("dbg_rot", [128, 1024], F32,
                                 kind="ExternalOutput").ap()
        dbg_rt1 = nc.dram_tensor("dbg_rt1", [128, 1024], F32,
                                 kind="ExternalOutput").ap()
        dbg_rt2 = nc.dram_tensor("dbg_rt2", [128, 1024], F32,
                                 kind="ExternalOutput").ap()
        dbg_cos = nc.dram_tensor("dbg_cos", [128, 1024], F32,
                                 kind="ExternalOutput").ap()
        dbg_sin = nc.dram_tensor("dbg_sin", [128, 1024], F32,
                                 kind="ExternalOutput").ap()
        dbg_g = nc.dram_tensor("dbg_g", [128, 8192], F32,
                               kind="ExternalOutput").ap()
        dbg_eo0 = nc.dram_tensor("dbg_eo0", [128, 16384], F32,
                                 kind="ExternalOutput").ap()
        dbg_eoF = nc.dram_tensor("dbg_eoF", [128, 16384], F32,
                                 kind="ExternalOutput").ap()

    xnT = nc.dram_tensor("xnT", [16, 128, TOK], BF16, kind="Internal").ap()
    qT = nc.dram_tensor("qT", [H, 128, TOK], BF16, kind="Internal").ap()
    contrib = nc.dram_tensor("contrib", [CROWS, D], BF16, kind="Internal").ap()
    gathered = nc.dram_tensor("gathered", [8 * CROWS, D], BF16, kind="Internal",
                              addr_space="Shared").ap()
    yT = nc.dram_tensor("yT", [NT, 128, H * 128], BF16, kind="Internal").ap()
    x2s = nc.dram_tensor("x2s", [TOK, D], F32, kind="Internal").ap()

    st = ExitStack()
    e_ = st.enter_context

    xnTb = nc.alloc_sbuf_tensor("xnTb", [128, 16 * 1024], BF16)
    stg = nc.alloc_sbuf_tensor("stg", [128, 2048], BF16)
    combsb = nc.alloc_sbuf_tensor("combsb", [128, 128], F32)
    padkill = nc.alloc_sbuf_tensor("padkill_sb", [128, NSLOT], F32)
    sidx = nc.alloc_sbuf_tensor("sidx", [128, 2 * NSLOT * H], I32)
    gate_sb = nc.alloc_sbuf_tensor("gate_sb", [128, 128], BF16)
    identf = nc.alloc_sbuf_tensor("identf", [128, 128], F32)
    ropept = nc.alloc_sbuf_tensor("ropept_sb", [128, 128], BF16)
    onescol = nc.alloc_sbuf_tensor("onescol", [128, 1], BF16)
    onesrow = nc.alloc_sbuf_tensor("onesrow", [1, 128], BF16)
    qn2 = nc.alloc_sbuf_tensor("qn2", [1, 16], F32)
    kn2own = nc.alloc_sbuf_tensor("kn2own", [1, 16], F32)
    kn2bf = nc.alloc_sbuf_tensor("kn2bf", [1, 16], BF16)
    knrow = nc.alloc_sbuf_tensor("knrow", [1, 128], BF16)
    kn2all = nc.alloc_sbuf_tensor("kn2all", [1, 16], F32)
    prodn = nc.alloc_sbuf_tensor("prodn", [1, 16], F32)
    cvec = nc.alloc_sbuf_tensor("cvec", [1, 16], F32)
    cbf = nc.alloc_sbuf_tensor("cbf", [1, 16], BF16)
    negCall = nc.alloc_sbuf_tensor("negCall", [128, 16], F32)
    biastab = nc.alloc_sbuf_tensor("biastab", [128, NSLOT], F32)
    nsc = [nc.alloc_sbuf_tensor(f"nsc{i}", [128, 1], F32) for i in range(8)]
    n1a = nc.alloc_sbuf_tensor("n1a", [1, 1], F32)
    n1b = nc.alloc_sbuf_tensor("n1b", [1, 1], F32)
    lrow = nc.alloc_sbuf_tensor("lrow", [1, 1024], F32)
    linv = nc.alloc_sbuf_tensor("linv", [1, 1024], F32)
    linvbf = nc.alloc_sbuf_tensor("linvbf", [1, 1024], BF16)
    epsb = nc.alloc_sbuf_tensor("epsb", [128, 1], F32)

    ARENA_KB = 136
    arena = nc.alloc_sbuf_tensor("arena", [128, ARENA_KB * 1024],
                                 mybir.dt.uint8)
    abase = nc.lookup_mloc(arena).addr

    def at(name, shape, dt, off_kb):
        return nc.alloc_sbuf_tensor_at(name, shape, dt,
                                       offset=abase + off_kb * 1024)

    wvsb = at("wvsb", [128, 16 * 2048], BF16, 0)
    cossb = at("cossb", [128, 2048], F32, 64)
    sinsb = at("sinsb", [128, 2048], F32, 72)
    wqbuf = [at(f"wqbuf{i}", [128, 2048], BF16, 80 + 4 * i) for i in range(2)]
    ropet1 = at("ropet1", [128, 1024], F32, 88)
    ropet2 = at("ropet2", [128, 1024], F32, 92)
    qkbf = [at(f"qkbf{i}", [128, 1024], BF16, 96 + 2 * i) for i in range(2)]
    ropedbf = [at(f"ropedbf{i}", [128, 1024], BF16, 100 + 2 * i)
               for i in range(2)]
    sqbf = at("sqbf", [128, 1024], BF16, 104)
    voutb = [at(f"voutb{i}", [128, 2048], BF16, 106 + 4 * i) for i in range(2)]
    masksb = at("masksb", [128, 8 * 1024], F32, 0)
    qbuf2 = [at(f"qbuf2_{i}", [128, 1024], BF16, [32, 52][i])
             for i in range(2)]
    kbuf = [at(f"kbuf{i}", [128, 1024], BF16, 34 + 2 * i) for i in range(2)]
    vbuf = [at(f"vbuf{i}", [128, 1024], BF16, 38 + 2 * i) for i in range(2)]
    pbuf = [at(f"pbuf{i}", [128, 1024], BF16, 42 + 2 * i) for i in range(2)]
    ybuf = at("ybuf", [128, 1024], BF16, 46)
    linvb = at("linvb", [128, 1024], F32, 48)
    wosb = at("wosb", [128, 16 * 2048], BF16, 0)
    ytb = [at(f"ytb{i}", [128, 2048], BF16, 64 + 4 * i) for i in range(2)]
    xtile = [at(f"xtile{i}", [128, 2048], F32, 72 + 8 * i) for i in range(2)]
    xf = at("xf", [128, 2048], F32, 88)
    xnf = at("xnf", [128, 2048], F32, 96)
    aout = at("aout", [128, 2048], F32, 104)
    x2t = at("x2t", [128, 2048], F32, 112)
    eoacc = at("eoacc", [128, 8 * 2048], F32, 0)
    w2res = at("w2res", [128, 8 * 2048], BF16, 64)
    gbuf = at("gbuf", [128, 8 * 1024], BF16, 96)
    w1buf = [at(f"w1buf{i}", [128, 2048], BF16, 112 + 4 * i) for i in range(2)]
    w3buf = [at(f"w3buf{i}", [128, 2048], BF16, 120 + 4 * i) for i in range(2)]
    sgl = at("sgl", [128, 1024], F32, 128)
    eotmp = at("eotmp", [128, 1024], F32, 132)
    dbgbuf = at("dbgbuf", [128, 1024], F32, 114)
    p6a = at("p6a", [128, 2048], F32, 64)
    p6x2 = at("p6x2", [128, 2048], F32, 72)
    p6o = at("p6o", [128, 2048], F32, 80)

    pA = nc.alloc_psum_tensor("pA", [128, 1024], F32)
    pB = nc.alloc_psum_tensor("pB", [128, 1024], F32)
    pC = nc.alloc_psum_tensor("pC", [128, 1024], F32)
    pD = nc.alloc_psum_tensor("pD", [128, 512], F32)
    pE = nc.alloc_psum_tensor("pE", [128, 512], F32)
    psA = [pA, pB]

    QSP = e_(nc.semaphore("QSP"))
    QPL = e_(nc.semaphore("QPL"))
    CC = e_(nc.semaphore("CC"))
    SPE = e_(nc.semaphore("SPE"))
    SACT = e_(nc.semaphore("SACT"))
    SDVE = e_(nc.semaphore("SDVE"))
    SEMS = {"QSP": QSP, "QPL": QPL, "CC": CC, "PE": SPE, "ACT": SACT,
            "DVE": SDVE}

    recs = []
    cnt = dict(QSP=0, QPL=0, CC=0, PE=0, ACT=0, DVE=0)

    def rec(eng, emit, waits=(), inc=None):
        ws = tuple((s, v) for (s, v) in waits if v)
        for s, v in ws:
            assert v <= cnt[s], f"forward wait {s} {v} > {cnt[s]}"
        if inc:
            cnt[inc[0]] += inc[1]
        recs.append((eng, ws, emit, inc))
        return cnt[inc[0]] if inc else None

    def dma(q, out_ap, in_ap, waits=(), **kw):
        sem = "QSP" if q == "sp" else "QPL"
        return rec(q, lambda e: e.dma_start(out=out_ap, in_=in_ap, **kw),
                   waits, (sem, 16))

    def pe(emit, waits=(), bump=True):
        return rec("pe", emit, waits, ("PE", 1) if bump else None)

    def act(emit, waits=()):
        return rec("act", emit, waits, ("ACT", 1))

    def dve(emit, waits=()):
        return rec("dve", emit, waits, ("DVE", 1))

    def barrier():
        snap = dict(cnt)
        own = {"sp": "QSP", "pool": "QPL", "pe": "PE", "act": "ACT",
               "dve": "DVE"}
        for engname in ("sp", "pool", "pe", "act", "dve"):
            waits = tuple((s, v) for s, v in snap.items()
                          if s != own[engname] and s != "CC" and v > 0)
            recs.append((engname, waits, None, None))

    def finish():
        if KDBG:
            dbw = (("QPL", cnt["QPL"]), ("QSP", cnt["QSP"]),
                   ("PE", cnt["PE"]), ("ACT", cnt["ACT"]),
                   ("DVE", cnt["DVE"]))
            pairs = [(dbg_qT, qT.rearrange("h p t -> (h p) t")),
                     (dbg_ct, contrib[:, :]),
                     (dbg_yT, yT.rearrange("t p c -> (t p) c")),
                     (dbg_x2, x2s[:, :]),
                     (dbg_xnT, xnT.rearrange("a p t -> (a p) t")),
                     (dbg_comb, combsb[:, :])]
            for oi, (dst, srcap) in enumerate(pairs):
                dma("pool", dst[:, :], srcap, dbw if oi == 0 else ())
        recs.append(("pool", (("QPL", cnt["QPL"]),), None, None))
        recs.append(("sp", (("QSP", cnt["QSP"]),), None, None))
        with nc.Block() as block:
            def runner(engname):
                def _run(eng):
                    for rengine, waits, emit, inc in recs:
                        if rengine != engname:
                            continue
                        for s_, v in waits:
                            eng.wait_ge(SEMS[s_], v)
                        if emit is None:
                            continue
                        ins = emit(eng)
                        if inc:
                            ins.then_inc(SEMS[inc[0]], inc[1])
                return _run

            block.sync(runner("sp"))
            block.gpsimd(runner("pool"))
            block.tensor(runner("pe"))
            block.scalar(runner("act"))
            block.vector(runner("dve"))
        st.close()
        return nc

    dma("sp", padkill[:, :], padkill_in[:, :])
    dma("sp", sidx[:, :], kvidx[:, :])
    t_ident = dma("sp", identf[:, :], ident_in[:, :])
    dma("pool", ropept[:, :], ropept_in[:, :])
    for dc in range(16):
        dma("pool", gate_sb[:, dc * 8:(dc + 1) * 8], gate_t[dc, :, :])
    t_sin = 0
    dve(lambda e: e.memset(onescol[:, :], 1.0))
    dve(lambda e: e.memset(epsb[:, :], EPS))
    dve(lambda e: e.memset(onesrow[:, :], 1.0))
    dve(lambda e: e.memset(qn2[:, :], 0.0))
    dve(lambda e: e.memset(kn2own[:, :], 0.0))

    def emit_ln_tile(x_src, tok_src, src_sem, stg_war_qpl, prev_dve=0):
        w = ((src_sem, tok_src),)
        wp = w + ((("DVE", prev_dve),) if prev_dve else ())
        t1 = dve(lambda e: e.reduce_sum(nsc[0][:, :], x_src[:, :], axis=AX.X),
                 w)
        t2 = act(lambda e: e.activation(xf[:, :], x_src[:, :], AF.Square,
                                        accum_out=nsc[1][:, :]), wp)
        t3 = dve(lambda e: e.tensor_scalar_mul(nsc[2][:, :], nsc[0][:, :],
                                               1.0 / D), (("DVE", t1),))
        t4 = dve(lambda e: e.tensor_scalar_mul(nsc[3][:, :], nsc[1][:, :],
                                               1.0 / D), (("ACT", t2),))
        t5 = dve(lambda e: e.tensor_mul(nsc[4][:, :], nsc[2][:, :],
                                        nsc[2][:, :]), (("DVE", t3),))
        t6 = dve(lambda e: e.tensor_sub(nsc[5][:, :], nsc[3][:, :],
                                        nsc[4][:, :]),
                 (("DVE", t5), ("DVE", t4)))
        t7 = act(lambda e: e.activation(nsc[6][:, :], nsc[5][:, :], AF.Ln,
                                        bias=epsb[:, :]), (("DVE", t6),))
        t8 = act(lambda e: e.activation(nsc[7][:, :], nsc[6][:, :], AF.Exp,
                                        scale=-0.5), (("ACT", t7),))
        t9 = dve(lambda e: e.tensor_scalar(xf[:, :], x_src[:, :], nsc[2][:, :],
                                           None, ALU.subtract),
                 (("DVE", t3), ("ACT", t2)))
        t10 = dve(lambda e: e.tensor_scalar(xnf[:, :], xf[:, :], nsc[7][:, :],
                                            None, ALU.mult),
                  (("DVE", t9), ("ACT", t8)))
        tok_stg = 0
        for half in range(2):
            tok_pe = None
            for j in range(8):
                dcb = half * 8 + j
                wj = ()
                if j == 0:
                    wj = (("DVE", t10),) if half == 0 else (("DVE", tok_stg),)
                tok_pe = pe(
                    (lambda dcb, j: lambda e: e.transpose(
                        pC[:, j * 128:(j + 1) * 128],
                        xnf[:, dcb * 128:(dcb + 1) * 128],
                        identf[:, :]))(dcb, j), wj)
            sw = [("PE", tok_pe)]
            if half == 0 and stg_war_qpl:
                sw.append(("QPL", stg_war_qpl))
            tok_stg = dve(
                (lambda half: lambda e: e.tensor_copy(
                    stg[:, half * 1024:(half + 1) * 1024], pC[:, :]))(half),
                tuple(sw))
        return tok_stg

    t_xuse = [0, 0]
    t_stg_dma = 0
    t_p0prev = 0
    recs.append(("pe", (("QSP", t_ident),), None, None))
    for t in range(NT):
        wr = [("DVE", t_xuse[t % 2])] if t_xuse[t % 2] else []
        tok_x = dma("sp", xtile[t % 2][:, :], x_own[t * 128:(t + 1) * 128, :],
                    tuple(wr))
        tok_stg = emit_ln_tile(xtile[t % 2], tok_x, "QSP", t_stg_dma,
                               prev_dve=t_p0prev)
        t_xuse[t % 2] = tok_stg
        t_p0prev = tok_stg
        t_stg_dma = dma(
            "pool",
            xnT[:, :, t * 128:(t + 1) * 128].rearrange("dc p tj -> p dc tj"),
            stg.rearrange("p (dc tj) -> p dc tj", dc=16),
            (("DVE", tok_stg),))
    xnT_done = t_stg_dma
    barrier()

    if KPH < 1:
        return finish()
    dma("pool", cossb[:, :], cos_t[:, :])
    t_sin = dma("pool", sinsb[:, :], sin_t[:, :])
    t_wv = 0
    for dc in range(16):
        t_wv = dma("pool", wvsb[:, dc * 2048:(dc + 1) * 2048], wv_t[dc, :, :])
    t_wq = [0, 0]
    t_qkev = [0, 0]
    t_rope = [0, 0]
    t_mmtok = [0, 0]
    t_spill = [0, 0]
    t_tsq = [0, 0]
    t_pCr = 0
    t_norm_mm = 0
    t_normred = 0
    t_xnTb = 0
    for blk in range(NBLK):
        for dc in range(16):
            t_xnTb = dma("sp", xnTb[:, dc * 1024:(dc + 1) * 1024],
                         xnT[dc, :, blk * 1024:(blk + 1) * 1024],
                         (("QPL", xnT_done),) if (dc == 0 and blk == 0) else
                         ((("PE", cnt["PE"]),) if dc == 0 else ()))
        for m in range(32):
            i = m % 2
            t_wq[i] = dma("pool", wqbuf[i][:, :],
                          wqkv_t[m].rearrange("p dc fj -> p (dc fj)"),
                          (("PE", t_mmtok[i]),) if t_mmtok[i] else ())
            mmw = [("QPL", t_wq[i]), ("QSP", t_xnTb)]
            if t_qkev[i]:
                mmw.append(("DVE", t_qkev[i]))
            tok_mm = None
            for dc in range(16):
                for ns in range(2):
                    tok_mm = pe((lambda i, dc, ns: lambda e: e.matmul(
                        psA[i][:, ns * 512:(ns + 1) * 512],
                        wqbuf[i][:, dc * 128:(dc + 1) * 128],
                        xnTb[:, dc * 1024 + ns * 512:dc * 1024 + (ns + 1) * 512],
                        start=(dc == 0), stop=(dc == 15),
                        skip_group_check=True))(i, dc, ns),
                        tuple(mmw) if (dc == 0 and ns == 0) else ())
            t_mmtok[i] = tok_mm
            ev_w = [("PE", tok_mm)]
            if t_rope[i]:
                ev_w.append(("DVE", t_rope[i]))
            t_qkev[i] = dve((lambda i: lambda e: e.tensor_copy(
                qkbf[i][:, :], psA[i][:, :]))(i), tuple(ev_w))
            rw = [("DVE", t_qkev[i])]
            if t_pCr:
                rw.append(("DVE", t_pCr))
            tok_rot = None
            for ns in range(2):
                tok_rot = pe((lambda i, ns: lambda e: e.matmul(
                    pC[:, ns * 512:(ns + 1) * 512], ropept[:, :],
                    qkbf[i][:, ns * 512:(ns + 1) * 512],
                    start=True, stop=True, skip_group_check=True))(i, ns),
                    tuple(rw) if ns == 0 else ())
            cosw = (("QPL", t_sin),) if (blk == 0 and m == 0) else ()
            t_pCr = dve((lambda blk: lambda e: e.tensor_mul(
                ropet1[:, :], pC[:, :],
                sinsb[:, blk * 1024:(blk + 1) * 1024]))(blk),
                (("PE", tok_rot),) + cosw)
            tr2 = dve((lambda i, blk: lambda e: e.tensor_mul(
                ropet2[:, :], qkbf[i][:, :],
                cossb[:, blk * 1024:(blk + 1) * 1024]))(i, blk),
                (("DVE", t_pCr),))
            if KDBG and blk == 0 and m == 0:
                dma("pool", dbg_q0[:, :], qkbf[i][:, :], (("DVE", tr2),))
                t_pCr = dve(lambda e: e.tensor_copy(dbgbuf[:, :], pC[:, :]),
                            (("DVE", t_pCr),))
                dma("pool", dbg_rot[:, :], dbgbuf[:, :],
                    (("DVE", cnt["DVE"]),))
                dma("pool", dbg_rt1[:, :], ropet1[:, :], (("DVE", tr2),))
                dma("pool", dbg_rt2[:, :], ropet2[:, :], (("DVE", tr2),))
                dma("pool", dbg_cos[:, :], cossb[:, 0:1024],
                    (("QPL", t_sin),))
                dma("pool", dbg_sin[:, :], sinsb[:, 0:1024],
                    (("QPL", t_sin),))
            rope_w = [("DVE", tr2)]
            rope_w.append(("QPL", cnt["QPL"])) if (KDBG and blk == 0 and
                                                   m == 0) else None
            if t_spill[i]:
                rope_w.append(("QPL", t_spill[i]))
            if t_tsq[i]:
                rope_w.append(("ACT", t_tsq[i]))
            t_rope[i] = dve((lambda i: lambda e: e.tensor_add(
                ropedbf[i][:, :], ropet1[:, :], ropet2[:, :]))(i),
                tuple(rope_w))
            sq_w = [("DVE", t_rope[i])]
            if t_norm_mm:
                sq_w.append(("PE", t_norm_mm))
            t_tsq[i] = act((lambda i: lambda e: e.activation(
                sqbf[:, :], ropedbf[i][:, :], AF.Square))(i), tuple(sq_w))
            nmw = [("ACT", t_tsq[i])]
            if t_normred:
                nmw.append(("DVE", t_normred))
            pe(lambda e: e.matmul(pD[0:1, :], onescol[:, :], sqbf[:, 0:512],
                                  start=True, stop=True,
                                  skip_group_check=True), tuple(nmw))
            t_norm_mm = pe(lambda e: e.matmul(
                pE[0:1, :], onescol[:, :], sqbf[:, 512:1024],
                start=True, stop=True, skip_group_check=True))
            ta = dve(lambda e: e.reduce_max(n1a[:, :], pD[0:1, :], axis=AX.X),
                     (("PE", t_norm_mm),))
            tb = dve(lambda e: e.reduce_max(n1b[:, :], pE[0:1, :], axis=AX.X),
                     (("DVE", ta),))
            tc = dve(lambda e: e.tensor_max(n1a[:, :], n1a[:, :], n1b[:, :]),
                     (("DVE", tb),))
            dst = qn2 if m < 16 else kn2own
            hh = m % 16
            t_normred = dve((lambda dst, hh: lambda e: e.tensor_max(
                dst[0:1, hh:hh + 1], dst[0:1, hh:hh + 1], n1a[0:1, :]))(
                    dst, hh), (("DVE", tc),))
            if m < 16:
                t_spill[i] = dma("pool",
                                 qT[m, :, blk * 1024:(blk + 1) * 1024],
                                 ropedbf[i][:, :], (("DVE", t_rope[i]),))
            else:
                t_spill[i] = dma("pool",
                                 contrib[(m - 16) * 128:(m - 15) * 128,
                                         blk * 1024:(blk + 1) * 1024],
                                 ropedbf[i][:, :], (("DVE", t_rope[i]),))
        t_vev = [0, 0]
        t_vsp = [0, 0]
        for vt in range(8):
            t = blk * 8 + vt
            i = vt % 2
            vmw = [("QPL", t_wv), ("QSP", t_xnTb),
                   ("DVE", max(t_qkev)), ("DVE", max(t_vev))]
            tok_mm = None
            for dc in range(16):
                for ns in range(4):
                    ppp = psA[ns // 2]
                    tok_mm = pe((lambda dc, ns, vt, ppp: lambda e: e.matmul(
                        ppp[:, (ns % 2) * 512:((ns % 2) + 1) * 512],
                        xnTb[:, dc * 1024 + vt * 128:dc * 1024 + (vt + 1) * 128],
                        wvsb[:, dc * 2048 + ns * 512:dc * 2048 + (ns + 1) * 512],
                        start=(dc == 0), stop=(dc == 15),
                        skip_group_check=True))(dc, ns, vt, ppp),
                        tuple(vmw) if (dc == 0 and ns == 0) else ())
            evw = [("PE", tok_mm)]
            if t_vsp[i]:
                evw.append(("QPL", t_vsp[i]))
            dve((lambda i: lambda e: e.tensor_copy(
                voutb[i][:, 0:1024], pA[:, :]))(i), tuple(evw))
            t_vev[i] = dve((lambda i: lambda e: e.tensor_copy(
                voutb[i][:, 1024:2048], pB[:, :]))(i))
            t_vsp[i] = dma(
                "pool",
                contrib[2048:4096, :].rearrange(
                    "(h p) (tl d) -> p h tl d", p=128, d=128)[:, :, t, :],
                voutb[i].rearrange("p (h d) -> p h d", h=16),
                (("DVE", t_vev[i]),))
        t_qkev = [cnt["DVE"], cnt["DVE"]]
    tkb = dve(lambda e: e.tensor_copy(kn2bf[:, :], kn2own[:, :]),
              (("DVE", t_normred),))
    dma("pool", contrib[4096:4097, 0:16], kn2bf[0:1, :], (("DVE", tkb),))
    barrier()

    if KPH < 2:
        return finish()
    rec("pool", lambda e: e.collective_compute(
        "AllGather", ALU.bypass, replica_groups=[list(range(8))],
        ins=[contrib[:, :].opt()], outs=[gathered[:, :].opt()]),
        ((("QPL", cnt["QPL"])),), ("CC", 1))
    cc_tok = cnt["CC"]
    barrier()

    if KPH < 3:
        return finish()
    t_mask = 0
    for ck in range(8):
        t_mask = dma("sp", masksb[:, ck * 1024:(ck + 1) * 1024], trimask[ck])
    t_knr = 0
    for r in range(8):
        t_knr = dma("sp", knrow[0:1, r * 16:(r + 1) * 16],
                    gathered[4096 + r * CROWS:4097 + r * CROWS, 0:16],
                    (("CC", cc_tok),) if r == 0 else ())
    tk1 = dve(lambda e: e.reduce_max(
        kn2all[0:1, :],
        knrow[0:1, :].rearrange("a (r h) -> a h r", r=8), axis=AX.X),
        (("QSP", t_knr),))
    tk2 = dve(lambda e: e.tensor_mul(prodn[:, :], qn2[:, :], kn2all[:, :]),
              (("DVE", tk1),))
    tk3 = act(lambda e: e.activation(cvec[:, :], prodn[:, :], AF.Ln,
                                     scale=1.0 / DH), (("DVE", tk2),))
    tk4 = act(lambda e: e.activation(cvec[:, :], cvec[:, :], AF.Exp,
                                     scale=0.5), (("ACT", tk3),))
    tk5 = dve(lambda e: e.tensor_scalar_mul(cvec[:, :], cvec[:, :], 1.02),
              (("ACT", tk4),))
    tk6 = dve(lambda e: e.tensor_copy(cbf[:, :], cvec[:, :]), (("DVE", tk5),))
    tk7 = pe(lambda e: e.matmul(pD[:, 0:16], onesrow[:, :], cbf[0:1, :],
                                start=True, stop=True, skip_group_check=True),
             (("DVE", tk6),))
    tk8 = dve(lambda e: e.tensor_scalar_mul(negCall[:, :], pD[:, 0:16], -1.0),
              (("PE", tk7),))

    gview = gathered.rearrange("r c -> (r c)")[0:128 * 1024].rearrange(
        "(p c) -> p c", c=1024)

    t_kld = [0, 0]
    t_vld = [0, 0]
    t_pb = [0, 0]
    t_sps = [0, 0]
    t_ypc = 0
    t_yspill = 0
    t_pdpe = tk8
    t_quse = [0, 0]
    for h in range(H):
        t_bias = dve((lambda h: lambda e: e.tensor_scalar(
            biastab[:, :], padkill[:, :], negCall[:, h:h + 1], None,
            ALU.add))(h), (("DVE", tk8),))
        for s in range(2):
            qp = (h * 2 + s) % 2
            qw = [("PE", t_quse[qp])] if t_quse[qp] else []
            t_qld = dma("sp", qbuf2[qp][:, :],
                        qT[h, :, s * 1024:(s + 1) * 1024], tuple(qw))
            nslots = 4 if s == 0 else 8
            chunks = [(j, ck) for j in range(nslots) for ck in range(8)]
            loaded = set()
            s_toks = {}

            def ensure_loaded(j, s=s, h=h):
                if j in loaded:
                    return
                loaded.add(j)
                slot = j if s == 0 else 4 + j
                jj = j % 2
                colk = slot * H + h
                colv = (NSLOT + slot) * H + h
                kw = [("CC", cc_tok)]
                if t_pb[jj]:
                    kw.append(("PE", t_pb[jj]))
                t_kld[jj] = rec("pool", (lambda jj, colk: lambda e:
                                e.indirect_dma_start(
                                    kbuf[jj][:, :], None, gview,
                                    IndirectOffsetOnAxis(
                                        sidx[:, colk:colk + 1], 0)))(jj, colk),
                                tuple(kw), ("QPL", 16))
                t_vld[jj] = rec("pool", (lambda jj, colv: lambda e:
                                e.indirect_dma_start(
                                    vbuf[jj][:, :], None, gview,
                                    IndirectOffsetOnAxis(
                                        sidx[:, colv:colv + 1], 0)))(jj, colv),
                                (), ("QPL", 16))

            def issue_S(idx, qp=qp):
                j, ck = chunks[idx]
                ensure_loaded(j)
                jj = j % 2
                par = idx % 2
                sw = [("QPL", t_kld[jj]), ("QSP", t_qld)]
                if t_sps[par]:
                    sw.append(("ACT", t_sps[par]))
                tok = None
                for ns in range(2):
                    tok = pe((lambda par, jj, ck, ns, qp: lambda e: e.matmul(
                        psA[par][:, ns * 512:(ns + 1) * 512],
                        kbuf[jj][:, ck * 128:(ck + 1) * 128],
                        qbuf2[qp][:, ns * 512:(ns + 1) * 512],
                        start=True, stop=True,
                        skip_group_check=True))(par, jj, ck, ns, qp),
                        tuple(sw) if ns == 0 else ())
                s_toks[idx] = tok

            issue_S(0)
            tok_l = 0
            tok_pv = 0
            for idx, (j, ck) in enumerate(chunks):
                jj = j % 2
                par = idx % 2
                slot = j if s == 0 else 4 + j
                is_diag = (s == 0 and j == 3) or (s == 1 and j == 7)
                tok_s = s_toks.pop(idx)
                ew = [("PE", tok_s)]
                if is_diag:
                    tmsk = dve((lambda par, ck: lambda e: e.tensor_add(
                        psA[par][:, :], psA[par][:, :],
                        masksb[:, ck * 1024:(ck + 1) * 1024]))(par, ck),
                        (("PE", tok_s), ("QSP", t_mask)))
                    ew = [("DVE", tmsk)]
                pw = ew + [("DVE", t_bias)]
                if t_pb[jj]:
                    pw.append(("PE", t_pb[jj]))
                t_sps[par] = act((lambda par, jj, slot: lambda e:
                                  e.activation(
                                      pbuf[jj][:, :], psA[par][:, :],
                                      AF.Exp,
                                      bias=biastab[:, slot:slot + 1],
                                      scale=ATT_SCALE))(par, jj, slot),
                                 tuple(pw))
                if idx + 1 < len(chunks):
                    issue_S(idx + 1)
                pvw = [("ACT", t_sps[par]), ("QPL", t_vld[jj])]
                if idx == 0:
                    pvw.append(("DVE", t_ypc))
                st_f = (idx == 0)
                sp_f = (idx == len(chunks) - 1)
                for ns in range(2):
                    tok_pv = pe((lambda jj, ck, ns, st_f, sp_f:
                                 lambda e: e.matmul(
                                     pC[:, ns * 512:(ns + 1) * 512],
                                     vbuf[jj][:, ck * 128:(ck + 1) * 128],
                                     pbuf[jj][:, ns * 512:(ns + 1) * 512],
                                     start=st_f, stop=sp_f,
                                     skip_group_check=True))(
                                         jj, ck, ns, st_f, sp_f),
                                tuple(pvw) if ns == 0 else ())
                lw = (("DVE", t_pdpe),) if idx == 0 else ()
                lps = [pD, pE]
                for ns in range(2):
                    tok_l = pe((lambda jj, ns, st_f, sp_f, lp:
                                lambda e: e.matmul(
                                    lp[0:1, :], onescol[:, :],
                                    pbuf[jj][:, ns * 512:(ns + 1) * 512],
                                    start=st_f, stop=sp_f,
                                    skip_group_check=True))(
                                        jj, ns, st_f, sp_f, lps[ns]),
                               lw if ns == 0 else (), bump=(ns == 1))
                t_pb[jj] = tok_l
            t_quse[qp] = tok_l
            e1 = dve(lambda e: e.tensor_copy(lrow[0:1, 0:512], pD[0:1, :]),
                     (("PE", tok_l),))
            e2 = dve(lambda e: e.tensor_copy(lrow[0:1, 512:1024], pE[0:1, :]),
                     (("DVE", e1),))
            e3 = dve(lambda e: e.reciprocal(linv[0:1, :], lrow[0:1, :]),
                     (("DVE", e2),))
            e4 = dve(lambda e: e.tensor_copy(linvbf[0:1, :], linv[0:1, :]),
                     (("DVE", e3),))
            bc = None
            for ns in range(2):
                bc = pe((lambda ns, lp: lambda e: e.matmul(
                    lp[:, :], onesrow[:, :],
                    linvbf[0:1, ns * 512:(ns + 1) * 512],
                    start=True, stop=True, skip_group_check=True))(
                        ns, [pD, pE][ns]),
                    (("DVE", e4),) if ns == 0 else ())
            e5a = dve(lambda e: e.tensor_copy(linvb[:, 0:512], pD[:, :]),
                      (("PE", bc),))
            e5 = dve(lambda e: e.tensor_copy(linvb[:, 512:1024], pE[:, :]),
                     (("DVE", e5a),))
            t_pdpe = e5
            yw = [("DVE", e5), ("PE", tok_pv)]
            if t_yspill:
                yw.append(("QPL", t_yspill))
            t_ypc = dve(lambda e: e.tensor_mul(ybuf[:, :], pC[:, :],
                                               linvb[:, :]), tuple(yw))
            t_yspill = dma(
                "pool",
                yT[:, :, h * 128:(h + 1) * 128].rearrange(
                    "tl p d -> p tl d")[:, s * 8:(s + 1) * 8, :],
                ybuf.rearrange("p (tl d) -> p tl d", d=128),
                (("DVE", t_ypc),))
    barrier()

    if KPH < 4:
        return finish()
    t_wo = 0
    for dc in range(16):
        t_wo = dma("pool", wosb[:, dc * 2048:(dc + 1) * 2048], wo_t[dc])
    t_yt = [0, 0]
    t_aev = 0
    t_gate_dve = 0
    xnT2_wtok = [0] * NT
    t_x2w = [0] * NT
    t_stg2_dma = 0
    t_p4prev = 0
    for t in range(NT):
        i = t % 2
        t_ytld = dma("sp", ytb[i][:, :], yT[t],
                     (("PE", t_yt[i]),) if t_yt[i] else ())
        t_xld = dma("sp", xtile[i][:, :], x_own[t * 128:(t + 1) * 128, :],
                    (("DVE", t_xuse[i]),) if t_xuse[i] else ())
        mw = [("QPL", t_wo), ("QSP", t_ytld)]
        if t_aev:
            mw.append(("DVE", t_aev))
        tok_mm = None
        for hh in range(16):
            for ns in range(4):
                ppp = psA[ns // 2]
                tok_mm = pe((lambda i, hh, ns, ppp: lambda e: e.matmul(
                    ppp[:, (ns % 2) * 512:((ns % 2) + 1) * 512],
                    ytb[i][:, hh * 128:(hh + 1) * 128],
                    wosb[:, hh * 2048 + ns * 512:hh * 2048 + (ns + 1) * 512],
                    start=(hh == 0), stop=(hh == 15),
                    skip_group_check=True))(i, hh, ns, ppp),
                    tuple(mw) if (hh == 0 and ns == 0) else ())
        t_yt[i] = tok_mm
        a1 = dve(lambda e: e.tensor_copy(aout[:, 0:1024], pA[:, :]),
                 (("PE", tok_mm),))
        t_aev = dve(lambda e: e.tensor_copy(aout[:, 1024:2048], pB[:, :]),
                    (("DVE", a1),))
        l1 = dve(lambda e: e.reduce_sum(nsc[0][:, :], aout[:, :], axis=AX.X),
                 (("DVE", t_aev),))
        l2 = act(lambda e: e.activation(xf[:, :], aout[:, :], AF.Square,
                                        accum_out=nsc[1][:, :]),
                 (("DVE", t_aev),))
        l3 = dve(lambda e: e.tensor_scalar_mul(nsc[2][:, :], nsc[0][:, :],
                                               1.0 / D), (("DVE", l1),))
        l4 = dve(lambda e: e.tensor_scalar_mul(nsc[3][:, :], nsc[1][:, :],
                                               1.0 / D), (("ACT", l2),))
        l5 = dve(lambda e: e.tensor_mul(nsc[4][:, :], nsc[2][:, :],
                                        nsc[2][:, :]), (("DVE", l3),))
        l6 = dve(lambda e: e.tensor_sub(nsc[5][:, :], nsc[3][:, :],
                                        nsc[4][:, :]),
                 (("DVE", l5), ("DVE", l4)))
        l7 = act(lambda e: e.activation(nsc[6][:, :], nsc[5][:, :], AF.Ln,
                                        bias=epsb[:, :]), (("DVE", l6),))
        l8 = act(lambda e: e.activation(nsc[7][:, :], nsc[6][:, :], AF.Exp,
                                        scale=-0.5), (("ACT", l7),))
        l9 = dve(lambda e: e.tensor_scalar(xf[:, :], aout[:, :], nsc[2][:, :],
                                           None, ALU.subtract),
                 (("DVE", l3), ("ACT", l2)))
        l10w = [("DVE", l9), ("ACT", l8), ("QSP", t_xld)]
        if t > 0:
            l10w.append(("QPL", t_x2w[t - 1]))
        l10 = dve((lambda i: lambda e: e.scalar_tensor_tensor(
            x2t[:, :], xf[:, :], nsc[7][:, :], xtile[i][:, :],
            ALU.mult, ALU.add))(i), tuple(l10w))
        t_xuse[i] = l10
        t_x2w[t] = dma("pool", x2s[t * 128:(t + 1) * 128, :], x2t[:, :],
                       (("DVE", l10),))
        tok_stg2 = emit_ln_tile(x2t, l10, "DVE", t_stg2_dma,
                                prev_dve=t_p4prev)
        t_stg2_dma = dma(
            "pool",
            xnT[:, :, t * 128:(t + 1) * 128].rearrange("dc p tj -> p dc tj"),
            stg.rearrange("p (dc tj) -> p dc tj", dc=16),
            (("DVE", tok_stg2),))
        xnT2_wtok[t] = t_stg2_dma
        t_p4prev = tok_stg2
        gw = [("DVE", tok_stg2)]
        if t_gate_dve:
            gw.append(("DVE", t_gate_dve))
        tok_g = None
        for dc in range(16):
            tok_g = pe((lambda dc: lambda e: e.matmul(
                pD[:, 0:8], stg[:, dc * 128:(dc + 1) * 128],
                gate_sb[:, dc * 8:(dc + 1) * 8],
                start=(dc == 0), stop=(dc == 15),
                skip_group_check=True))(dc),
                tuple(gw) if dc == 0 else ())
        g1 = dve(lambda e: e.reduce_max(nsc[0][:, :], pD[:, 0:8], axis=AX.X),
                 (("PE", tok_g),))
        g2 = dve(lambda e: e.tensor_scalar_mul(nsc[1][:, :], nsc[0][:, :],
                                               -1.0), (("DVE", g1),))
        g3 = act(lambda e: e.activation(xnf[:, 0:8], pD[:, 0:8], AF.Exp,
                                        bias=nsc[1][:, :]), (("DVE", g2),))
        g4 = dve(lambda e: e.reduce_sum(nsc[2][:, :], xnf[:, 0:8], axis=AX.X),
                 (("ACT", g3),))
        g5 = dve(lambda e: e.reciprocal(nsc[3][:, :], nsc[2][:, :]),
                 (("DVE", g4),))
        g6 = dve(lambda e: e.tensor_scalar(xnf[:, 8:16], xnf[:, 0:8],
                                           nsc[3][:, :], None, ALU.mult),
                 (("DVE", g5),))
        g7 = dve(lambda e: e.reduce_max(nsc[4][:, :], xnf[:, 8:16],
                                        axis=AX.X), (("DVE", g6),))
        g8 = dve(lambda e: e.tensor_scalar(xnf[:, 16:24], xnf[:, 8:16],
                                           nsc[4][:, :], None, ALU.is_ge),
                 (("DVE", g7),))
        g9 = dve(lambda e: e.scalar_tensor_tensor(
            xnf[:, 24:32], xnf[:, 16:24], KILL, xnf[:, 8:16],
            ALU.mult, ALU.add), (("DVE", g8),))
        g10 = dve(lambda e: e.reduce_max(nsc[5][:, :], xnf[:, 24:32],
                                         axis=AX.X), (("DVE", g9),))
        g11 = dve(lambda e: e.tensor_scalar(xnf[:, 16:24], xnf[:, 8:16],
                                            nsc[5][:, :], None, ALU.is_ge),
                  (("DVE", g10),))
        t_gate_dve = dve((lambda t: lambda e: e.tensor_mul(
            combsb[:, t * 8:(t + 1) * 8], xnf[:, 16:24], xnf[:, 8:16]))(t),
            (("DVE", g11),))
    barrier()

    if KPH < 5:
        return finish()
    t_gdve = [0, 0]
    t_hmm = [0, 0]
    t_w2res = 0
    t_eo = 0
    t_eodve = 0
    t_outprev = 0
    for blk in range(NBLK):
        for dc in range(16):
            t_xnTb = dma("sp", xnTb[:, dc * 1024:(dc + 1) * 1024],
                         xnT[dc, :, blk * 1024:(blk + 1) * 1024],
                         (("QPL", xnT2_wtok[NT - 1]), ("PE", cnt["PE"]))
                         if dc == 0 else ())
        for ee in range(KNE):
            t_sil = 0
            for ic in range(8):
                i = ic % 2
                w1w = [("PE", t_hmm[i])] if t_hmm[i] else []
                t_w1 = dma("pool", w1buf[i][:, :],
                           w13_t[0, ee, ic].rearrange("p dc ij -> p (dc ij)"),
                           tuple(w1w))
                t_w3 = dma("pool", w3buf[i][:, :],
                           w13_t[1, ee, ic].rearrange("p dc ij -> p (dc ij)"))
                hw = [("QPL", t_w3), ("QSP", t_xnTb)]
                if t_sil:
                    hw.append(("ACT", t_sil))
                if ic == 0 and t_eodve:
                    hw.append(("DVE", t_eodve))
                tok_h = None
                for dc in range(16):
                    for ns in range(2):
                        tok_h = pe((lambda i, dc, ns: lambda e: e.matmul(
                            pA[:, ns * 512:(ns + 1) * 512],
                            w1buf[i][:, dc * 128:(dc + 1) * 128],
                            xnTb[:, dc * 1024 + ns * 512:
                                 dc * 1024 + (ns + 1) * 512],
                            start=(dc == 0), stop=(dc == 15),
                            skip_group_check=True))(i, dc, ns),
                            tuple(hw) if (dc == 0 and ns == 0) else ())
                h3w = [("DVE", t_gdve[1 - i])] if t_gdve[1 - i] else []
                if ic == 0 and t_eodve:
                    h3w.append(("DVE", t_eodve))
                tok_h3 = None
                for dc in range(16):
                    for ns in range(2):
                        tok_h3 = pe((lambda i, dc, ns: lambda e: e.matmul(
                            pB[:, ns * 512:(ns + 1) * 512],
                            w3buf[i][:, dc * 128:(dc + 1) * 128],
                            xnTb[:, dc * 1024 + ns * 512:
                                 dc * 1024 + (ns + 1) * 512],
                            start=(dc == 0), stop=(dc == 15),
                            skip_group_check=True))(i, dc, ns),
                            tuple(h3w) if (dc == 0 and ns == 0) else ())
                t_hmm[i] = tok_h3
                silw = [("PE", tok_h)]
                if t_gdve[1 - i]:
                    silw.append(("DVE", t_gdve[1 - i]))
                t_sil = act(lambda e: e.activation(sgl[:, :], pA[:, :],
                                                   AF.Silu), tuple(silw))
                t_gdve[i] = dve((lambda ic: lambda e: e.tensor_mul(
                    gbuf[:, ic * 1024:(ic + 1) * 1024], sgl[:, :],
                    pB[:, :]))(ic), (("ACT", t_sil), ("PE", tok_h3)))
            if not KEO:
                continue
            w2w = [("DVE", cnt["DVE"])]
            if t_eo:
                w2w.append(("PE", t_eo))
            for ic in range(8):
                t_w2res = dma("pool", w2res[:, ic * 2048:(ic + 1) * 2048],
                              w2_t[ee, ic], tuple(w2w) if ic == 0 else ())
            last_g = max(t_gdve)
            if KDBG and blk == 0 and ee == 0:
                dma("pool", dbg_g[:, :], gbuf[:, :], (("DVE", last_g),))
            for tt in range(8):
                t = blk * 8 + tt
                ew = [("QPL", t_w2res), ("DVE", last_g)]
                if t_eodve:
                    ew.append(("DVE", t_eodve))
                tok_eo = None
                for ic in range(8):
                    for ds in range(4):
                        ppp = psA[ds // 2]
                        tok_eo = pe((lambda ic, ds, tt, ppp: lambda e:
                                     e.matmul(
                                         ppp[:, (ds % 2) * 512:
                                             ((ds % 2) + 1) * 512],
                                         gbuf[:, ic * 1024 + tt * 128:
                                              ic * 1024 + (tt + 1) * 128],
                                         w2res[:, ic * 2048 + ds * 512:
                                               ic * 2048 + (ds + 1) * 512],
                                         start=(ic == 0), stop=(ic == 7),
                                         skip_group_check=True))(
                                             ic, ds, tt, ppp),
                                    tuple(ew) if (ic == 0 and ds == 0) else ())
                t_eo = tok_eo
                accw = (("PE", tok_eo),)
                for half in range(2):
                    ppp = psA[half]
                    dst = eoacc[:, tt * 2048 + half * 1024:
                                tt * 2048 + (half + 1) * 1024]
                    cs = combsb[:, t * 8 + min(ee, 7):t * 8 + min(ee, 7) + 1]
                    if ee == 0:
                        t_eodve = dve((lambda dst, ppp, cs: lambda e:
                                       e.tensor_scalar(dst, ppp[:, :], cs,
                                                       None, ALU.mult))(
                                           dst, ppp, cs),
                                      accw if half == 0 else ())
                    elif ee < E:
                        dve((lambda ppp, cs: lambda e:
                             e.tensor_scalar(eotmp[:, :], ppp[:, :], cs,
                                             None, ALU.mult))(ppp, cs),
                            accw if half == 0 else ())
                        t_eodve = dve((lambda dst: lambda e:
                                       e.tensor_add(dst, dst,
                                                    eotmp[:, :]))(dst))
                    else:
                        t_eodve = dve((lambda dst, ppp: lambda e:
                                       e.tensor_add(dst, dst, ppp[:, :]))(
                                           dst, ppp),
                                      accw if half == 0 else ())
            if KDBG and blk == 0 and ee == 0 and tt == 7:
                dma("pool", dbg_eo0[:, :], eoacc[:, :], (("DVE", t_eodve),))
        if KDBG and blk == 0:
            dma("pool", dbg_eoF[:, :], eoacc[:, :], (("DVE", t_eodve),))
        t_p6prev = 0
        for tt in range(8 if KP6 else 0):
            t = blk * 8 + tt
            eot = eoacc[:, tt * 2048:(tt + 1) * 2048]
            f1 = dve((lambda eot: lambda e: e.reduce_sum(
                nsc[0][:, :], eot, axis=AX.X))(eot), (("DVE", t_eodve),))
            f2 = act((lambda eot: lambda e: e.activation(
                p6a[:, :], eot, AF.Square, accum_out=nsc[1][:, :]))(eot),
                (("DVE", t_eodve), ("QPL", t_w2res),
                 ("DVE", t_p6prev)))
            f3 = dve(lambda e: e.tensor_scalar_mul(nsc[2][:, :], nsc[0][:, :],
                                                   1.0 / D), (("DVE", f1),))
            f4 = dve(lambda e: e.tensor_scalar_mul(nsc[3][:, :], nsc[1][:, :],
                                                   1.0 / D), (("ACT", f2),))
            f5 = dve(lambda e: e.tensor_mul(nsc[4][:, :], nsc[2][:, :],
                                            nsc[2][:, :]), (("DVE", f3),))
            f6 = dve(lambda e: e.tensor_sub(nsc[5][:, :], nsc[3][:, :],
                                            nsc[4][:, :]),
                     (("DVE", f5), ("DVE", f4)))
            f7 = act(lambda e: e.activation(nsc[6][:, :], nsc[5][:, :], AF.Ln,
                                            bias=epsb[:, :]), (("DVE", f6),))
            f8 = act(lambda e: e.activation(nsc[7][:, :], nsc[6][:, :],
                                            AF.Exp, scale=-0.5),
                     (("ACT", f7),))
            t_x2r = dma("sp", p6x2[:, :], x2s[t * 128:(t + 1) * 128, :],
                        (("QPL", t_x2w[t]), ("DVE", t_eodve),
                         ("DVE", t_p6prev)))
            f9 = dve((lambda eot: lambda e: e.tensor_scalar(
                p6a[:, :], eot, nsc[2][:, :], None, ALU.subtract))(eot),
                (("DVE", f3), ("ACT", f2)))
            f10w = [("DVE", f9), ("DVE", f6), ("ACT", f8), ("QSP", t_x2r)]
            if t_outprev:
                f10w.append(("QPL", t_outprev))
            f10 = dve(lambda e: e.scalar_tensor_tensor(
                p6o[:, :], p6a[:, :], nsc[7][:, :], p6x2[:, :],
                ALU.mult, ALU.add), tuple(f10w))
            t_outprev = dma("pool", out[t * 128:(t + 1) * 128, :], p6o[:, :],
                            (("DVE", f10),))
            t_p6prev = f10

    return finish()



def _host_prepare(inputs):
    W_qkv = np.asarray(inputs["L0_attn_W_qkv"], np.float32)
    W_o = np.asarray(inputs["L0_attn_W_o"], np.float32)
    gate_w = np.asarray(inputs["L0_ffn_gate_w"], np.float32)
    w1 = np.asarray(inputs["L0_ffn_w1"], np.float32)
    w2 = np.asarray(inputs["L0_ffn_w2"], np.float32)
    w3 = np.asarray(inputs["L0_ffn_w3"], np.float32)
    ws1 = np.asarray(inputs["L0_ffn_ws1"], np.float32)
    ws2 = np.asarray(inputs["L0_ffn_ws2"], np.float32)
    ws3 = np.asarray(inputs["L0_ffn_ws3"], np.float32)

    shared = {}
    shared["wqkv_t"] = np.ascontiguousarray(
        W_qkv[:4096].reshape(32, 128, 16, 128).transpose(0, 3, 2, 1))
    shared["wv_t"] = np.ascontiguousarray(
        W_qkv[4096:].reshape(2048, 16, 128).transpose(1, 2, 0))
    shared["wo_t"] = np.ascontiguousarray(
        W_o.reshape(2048, 16, 128).transpose(1, 2, 0))
    shared["gate_t"] = np.ascontiguousarray(
        gate_w.reshape(8, 16, 128).transpose(1, 2, 0))
    w13 = np.zeros((2, NE, 8, 128, 16, 128), np.float32)
    for wi, warr, wsarr in ((0, w1, ws1), (1, w3, ws3)):
        a = warr.reshape(8, 8, 128, 16, 128)
        w13[wi, :8] = a.transpose(0, 1, 4, 3, 2)
        b = wsarr.reshape(8, 128, 16, 128)
        w13[wi, 8] = b.transpose(0, 3, 2, 1)
    shared["w13_t"] = np.ascontiguousarray(w13)
    w2t = np.zeros((NE, 8, 128, 2048), np.float32)
    a = w2.reshape(8, 4, 512, 8, 128)
    w2t[:8] = a.transpose(0, 3, 4, 1, 2).reshape(8, 8, 128, 2048)
    b = ws2.reshape(4, 512, 8, 128)
    w2t[8] = b.transpose(2, 3, 0, 1).reshape(8, 128, 2048)
    shared["w2_t"] = np.ascontiguousarray(w2t)
    tm = np.zeros((8, 128, 1024), np.float32)
    qq = np.arange(1024)[None, :]
    for ck in range(8):
        kk = ck * 128 + np.arange(128)[:, None]
        tm[ck] = np.where(qq >= kk, 0.0, NEG)
    shared["trimask"] = tm
    shared["ident"] = np.eye(128, dtype=np.float32)
    Pm = np.zeros((128, 128), np.float32)
    for j in range(64):
        Pm[2 * j, 2 * j + 1] = -1.0
        Pm[2 * j + 1, 2 * j] = 1.0
    shared["ropept"] = np.ascontiguousarray(Pm.T)

    invf = 1.0 / (10000.0 ** (np.arange(0, DH, 2, dtype=np.float64) / DH))
    pos = np.arange(L, dtype=np.float64)
    ang = pos[None, :] * np.repeat(invf, 2)[:, None]
    cos_all = np.cos(ang).astype(np.float32)
    sin_all = np.sin(ang).astype(np.float32)

    per_core = []
    x = np.asarray(inputs["L0_attn_x"], np.float32)
    for core in range(8):
        b, c = core // 4, core % 4
        lo, hi = _sub_ranges(core)
        rows = np.r_[1024 * lo:1024 * (lo + 1), 1024 * hi:1024 * (hi + 1)]
        d = dict(shared)
        d["x_own"] = np.ascontiguousarray(x[b].reshape(L, D)[rows])
        d["cos_t"] = np.ascontiguousarray(cos_all[:, rows])
        d["sin_t"] = np.ascontiguousarray(sin_all[:, rows])
        pk = np.zeros((128, NSLOT), np.float32)
        for j in range(3):
            if j >= c:
                pk[:, j] = KILL
        for j in range(7):
            if j >= 7 - c:
                pk[:, 4 + j] = KILL
        d["padkill"] = pk
        idx = np.zeros((128, 2 * NSLOT * H), np.int32)
        p = np.arange(128)

        def kt_of(s, j):
            if s == 0:
                return c if j == 3 else j
            return (7 - c) if j == 7 else j

        for s in range(2):
            nsl = 4 if s == 0 else 8
            for j in range(nsl):
                slot = j if s == 0 else 4 + j
                kt = kt_of(s, j)
                rho = 4 * b + (kt if kt <= 3 else 7 - kt)
                halfbit = 1 if kt > 3 else 0
                for h in range(H):
                    kr = rho * CROWS + h * 128 + p
                    vr = rho * CROWS + 2048 + h * 128 + p
                    idx[:, (0 * NSLOT + slot) * H + h] = kr * 2 + halfbit
                    idx[:, (1 * NSLOT + slot) * H + h] = vr * 2 + halfbit
        d["kvidx"] = idx
        per_core.append(d)
    return per_core


_NC_CACHE = {}


def kernel(**inputs):
    in_maps = _host_prepare(inputs)
    if "nc" not in _NC_CACHE:
        _NC_CACHE["nc"] = build_nc()
    nc = _NC_CACHE["nc"]
    res = run_bass_kernel_spmd(nc, in_maps, core_ids=list(range(8)))
    out = np.empty((B, L, D), np.float32)
    for core in range(8):
        b = core // 4
        lo, hi = _sub_ranges(core)
        o = np.asarray(res.results[core]["out"], np.float32)
        out[b, 1024 * lo:1024 * (lo + 1)] = o[:1024]
        out[b, 1024 * hi:1024 * (hi + 1)] = o[1024:]
    return out


# version 05
# speedup vs baseline: 1.1845x; 1.0161x over previous
import math
import os
import numpy as np
import ml_dtypes
import concourse.bass as bass
import concourse.mybir as mybir
from concourse.bass import IndirectOffsetOnAxis
from concourse.bass_utils import run_bass_kernel_spmd
from contextlib import ExitStack

F32 = mybir.dt.float32
BF16 = mybir.dt.bfloat16
I32 = mybir.dt.int32
AF = mybir.ActivationFunctionType
ALU = mybir.AluOpType
AX = mybir.AxisListType

D = 2048
H = 16
DH = 128
B = 2
L = 8192
TOK = 2048
NT = 16
NBLK = 2
E = 8
NE = 9
EPS = 1e-5
ATT_SCALE = 1.0 / math.sqrt(DH)
NEG = -1e30
KILL = -1e38
NSLOT = 12
CROWS = 4104


def _sub_ranges(core):
    c = core % 4
    return c, 7 - c


KPH = int(os.environ.get("KPH", "9"))
KP6 = int(os.environ.get("KP6", "1"))
KDBG = int(os.environ.get("KDBG", "0"))
KNE = int(os.environ.get("KNE", str(NE)))
KEO = int(os.environ.get("KEO", "1"))


def build_nc():
    nc = bass.Bass("TRN2", target_bir_lowering=False, debug=False)

    def din(name, shape, dt=F32):
        return nc.dram_tensor(name, shape, dt, kind="ExternalInput").ap()

    x_own = din("x_own", [TOK, D])
    wqkv_t = din("wqkv_t", [32, 128, 16, 128])
    wv_t = din("wv_t", [16, 128, 2048])
    wo_t = din("wo_t", [16, 128, 2048])
    gate_t = din("gate_t", [16, 128, 8])
    w13_t = din("w13_t", [2, NE, 8, 128, 16, 128])
    w2_t = din("w2_t", [NE, 8, 128, 2048])
    cos_t = din("cos_t", [128, TOK])
    sin_t = din("sin_t", [128, TOK])
    trimask = din("trimask", [8, 128, 1024])
    padkill_in = din("padkill", [128, NSLOT])
    kvidx = din("kvidx", [128, 2 * NSLOT * H], I32)
    ident_in = din("ident", [128, 128])
    ropept_in = din("ropept", [128, 128])

    out = nc.dram_tensor("out", [TOK, D], F32, kind="ExternalOutput").ap()
    if KDBG:
        dbg_qT = nc.dram_tensor("dbg_qT", [H * 128, TOK], F32,
                                kind="ExternalOutput").ap()
        dbg_ct = nc.dram_tensor("dbg_ct", [CROWS, D], F32,
                                kind="ExternalOutput").ap()
        dbg_yT = nc.dram_tensor("dbg_yT", [NT * 128, H * 128], F32,
                                kind="ExternalOutput").ap()
        dbg_x2 = nc.dram_tensor("dbg_x2", [TOK, D], F32,
                                kind="ExternalOutput").ap()
        dbg_xnT = nc.dram_tensor("dbg_xnT", [16 * 128, TOK], F32,
                                 kind="ExternalOutput").ap()
        dbg_comb = nc.dram_tensor("dbg_comb", [128, 128], F32,
                                  kind="ExternalOutput").ap()
        dbg_q0 = nc.dram_tensor("dbg_q0", [128, 1024], F32,
                                kind="ExternalOutput").ap()
        dbg_rot = nc.dram_tensor("dbg_rot", [128, 1024], F32,
                                 kind="ExternalOutput").ap()
        dbg_rt1 = nc.dram_tensor("dbg_rt1", [128, 1024], F32,
                                 kind="ExternalOutput").ap()
        dbg_rt2 = nc.dram_tensor("dbg_rt2", [128, 1024], F32,
                                 kind="ExternalOutput").ap()
        dbg_cos = nc.dram_tensor("dbg_cos", [128, 1024], F32,
                                 kind="ExternalOutput").ap()
        dbg_sin = nc.dram_tensor("dbg_sin", [128, 1024], F32,
                                 kind="ExternalOutput").ap()
        dbg_g = nc.dram_tensor("dbg_g", [128, 8192], F32,
                               kind="ExternalOutput").ap()
        dbg_eo0 = nc.dram_tensor("dbg_eo0", [128, 16384], F32,
                                 kind="ExternalOutput").ap()
        dbg_eoF = nc.dram_tensor("dbg_eoF", [128, 16384], F32,
                                 kind="ExternalOutput").ap()

    xnT = nc.dram_tensor("xnT", [16, 128, TOK], BF16, kind="Internal").ap()
    qT = nc.dram_tensor("qT", [H, 128, TOK], BF16, kind="Internal").ap()
    contrib = nc.dram_tensor("contrib", [CROWS, D], BF16, kind="Internal").ap()
    gathered = nc.dram_tensor("gathered", [8 * CROWS, D], BF16, kind="Internal",
                              addr_space="Shared").ap()
    yT = nc.dram_tensor("yT", [NT, 128, H * 128], BF16, kind="Internal").ap()
    x2s = nc.dram_tensor("x2s", [TOK, D], F32, kind="Internal").ap()

    st = ExitStack()
    e_ = st.enter_context

    xnTb = nc.alloc_sbuf_tensor("xnTb", [128, 16 * 1024], BF16)
    stg = nc.alloc_sbuf_tensor("stg", [128, 2048], BF16)
    combsb = nc.alloc_sbuf_tensor("combsb", [128, 128], F32)
    padkill = nc.alloc_sbuf_tensor("padkill_sb", [128, NSLOT], F32)
    sidx = nc.alloc_sbuf_tensor("sidx", [128, 2 * NSLOT * H], I32)
    gate_sb = nc.alloc_sbuf_tensor("gate_sb", [128, 128], BF16)
    identf = nc.alloc_sbuf_tensor("identf", [128, 128], F32)
    ropept = nc.alloc_sbuf_tensor("ropept_sb", [128, 128], BF16)
    onescol = nc.alloc_sbuf_tensor("onescol", [128, 1], BF16)
    onesrow = nc.alloc_sbuf_tensor("onesrow", [1, 128], BF16)
    qn2 = nc.alloc_sbuf_tensor("qn2", [1, 16], F32)
    kn2own = nc.alloc_sbuf_tensor("kn2own", [1, 16], F32)
    kn2bf = nc.alloc_sbuf_tensor("kn2bf", [1, 16], BF16)
    knrow = nc.alloc_sbuf_tensor("knrow", [1, 128], BF16)
    kn2all = nc.alloc_sbuf_tensor("kn2all", [1, 16], F32)
    prodn = nc.alloc_sbuf_tensor("prodn", [1, 16], F32)
    cvec = nc.alloc_sbuf_tensor("cvec", [1, 16], F32)
    cbf = nc.alloc_sbuf_tensor("cbf", [1, 16], BF16)
    negCall = nc.alloc_sbuf_tensor("negCall", [128, 16], F32)
    biastab = nc.alloc_sbuf_tensor("biastab", [128, NSLOT], F32)
    nsc = [nc.alloc_sbuf_tensor(f"nsc{i}", [128, 1], F32) for i in range(8)]
    n1a = nc.alloc_sbuf_tensor("n1a", [1, 1], F32)
    n1b = nc.alloc_sbuf_tensor("n1b", [1, 1], F32)
    lrow = nc.alloc_sbuf_tensor("lrow", [1, 1024], F32)
    linv = nc.alloc_sbuf_tensor("linv", [1, 1024], F32)
    linvbf = nc.alloc_sbuf_tensor("linvbf", [1, 1024], BF16)
    epsb = nc.alloc_sbuf_tensor("epsb", [128, 1], F32)

    ARENA_KB = 136
    arena = nc.alloc_sbuf_tensor("arena", [128, ARENA_KB * 1024],
                                 mybir.dt.uint8)
    abase = nc.lookup_mloc(arena).addr

    def at(name, shape, dt, off_kb):
        return nc.alloc_sbuf_tensor_at(name, shape, dt,
                                       offset=abase + off_kb * 1024)

    wvsb = at("wvsb", [128, 16 * 2048], BF16, 0)
    cossb = at("cossb", [128, 2048], F32, 64)
    sinsb = at("sinsb", [128, 2048], F32, 72)
    wqbuf = [at(f"wqbuf{i}", [128, 2048], BF16, 80 + 4 * i) for i in range(2)]
    ropet1 = at("ropet1", [128, 1024], F32, 88)
    ropet2 = at("ropet2", [128, 1024], F32, 92)
    qkbf = [at(f"qkbf{i}", [128, 1024], BF16, 96 + 2 * i) for i in range(2)]
    ropedbf = [at(f"ropedbf{i}", [128, 1024], BF16, 100 + 2 * i)
               for i in range(2)]
    sqbf = at("sqbf", [128, 1024], BF16, 104)
    voutb = [at(f"voutb{i}", [128, 2048], BF16, 106 + 4 * i) for i in range(2)]
    masksb = at("masksb", [128, 8 * 1024], F32, 0)
    qbuf2 = [at(f"qbuf2_{i}", [128, 1024], BF16, [32, 56][i])
             for i in range(2)]
    kbuf = [at(f"kbuf{i}", [128, 1024], BF16, 34 + 2 * i) for i in range(2)]
    vbuf = [at(f"vbuf{i}", [128, 1024], BF16, 38 + 2 * i) for i in range(2)]
    pbuf = [at(f"pbuf{i}", [128, 2048], BF16, 42 + 4 * i) for i in range(2)]
    ybuf = at("ybuf", [128, 1024], BF16, 50)
    linvb = at("linvb", [128, 1024], F32, 52)
    wosb = at("wosb", [128, 16 * 2048], BF16, 0)
    ytb = [at(f"ytb{i}", [128, 2048], BF16, 64 + 4 * i) for i in range(2)]
    xtile = [at(f"xtile{i}", [128, 2048], F32, 72 + 8 * i) for i in range(2)]
    xf = at("xf", [128, 2048], F32, 88)
    xnf = at("xnf", [128, 2048], F32, 96)
    aout = at("aout", [128, 2048], F32, 104)
    x2t = at("x2t", [128, 2048], F32, 112)
    eoacc = at("eoacc", [128, 8 * 2048], F32, 0)
    w2res = at("w2res", [128, 8 * 2048], BF16, 64)
    gbuf = at("gbuf", [128, 8 * 1024], BF16, 96)
    w1buf = [at(f"w1buf{i}", [128, 2048], BF16, 112 + 4 * i) for i in range(2)]
    w3buf = [at(f"w3buf{i}", [128, 2048], BF16, 120 + 4 * i) for i in range(2)]
    sgl = at("sgl", [128, 1024], F32, 128)
    eotmp = at("eotmp", [128, 1024], F32, 132)
    dbgbuf = at("dbgbuf", [128, 1024], F32, 114)
    p6a = at("p6a", [128, 2048], F32, 64)
    p6x2 = at("p6x2", [128, 2048], F32, 72)
    p6o = at("p6o", [128, 2048], F32, 80)

    pS = nc.alloc_psum_tensor("pS", [128, 2048], F32)
    pA = pS[:, 0:1024]
    pB = pS[:, 1024:2048]
    pC = nc.alloc_psum_tensor("pC", [128, 1024], F32)
    pD = nc.alloc_psum_tensor("pD", [128, 512], F32)
    pE = nc.alloc_psum_tensor("pE", [128, 512], F32)
    psA = [pA, pB]

    QSP = e_(nc.semaphore("QSP"))
    QPL = e_(nc.semaphore("QPL"))
    CC = e_(nc.semaphore("CC"))
    SPE = e_(nc.semaphore("SPE"))
    SACT = e_(nc.semaphore("SACT"))
    SDVE = e_(nc.semaphore("SDVE"))
    SEMS = {"QSP": QSP, "QPL": QPL, "CC": CC, "PE": SPE, "ACT": SACT,
            "DVE": SDVE}

    recs = []
    cnt = dict(QSP=0, QPL=0, CC=0, PE=0, ACT=0, DVE=0)

    def rec(eng, emit, waits=(), inc=None):
        ws = tuple((s, v) for (s, v) in waits if v)
        for s, v in ws:
            assert v <= cnt[s], f"forward wait {s} {v} > {cnt[s]}"
        if inc:
            cnt[inc[0]] += inc[1]
        recs.append((eng, ws, emit, inc))
        return cnt[inc[0]] if inc else None

    def dma(q, out_ap, in_ap, waits=(), **kw):
        sem = "QSP" if q == "sp" else "QPL"
        return rec(q, lambda e: e.dma_start(out=out_ap, in_=in_ap, **kw),
                   waits, (sem, 16))

    def pe(emit, waits=(), bump=True):
        return rec("pe", emit, waits, ("PE", 1) if bump else None)

    def act(emit, waits=()):
        return rec("act", emit, waits, ("ACT", 1))

    def dve(emit, waits=()):
        return rec("dve", emit, waits, ("DVE", 1))

    def barrier():
        snap = dict(cnt)
        own = {"sp": "QSP", "pool": "QPL", "pe": "PE", "act": "ACT",
               "dve": "DVE"}
        for engname in ("sp", "pool", "pe", "act", "dve"):
            waits = tuple((s, v) for s, v in snap.items()
                          if s != own[engname] and s != "CC" and v > 0)
            recs.append((engname, waits, None, None))

    def finish():
        if KDBG:
            dbw = (("QPL", cnt["QPL"]), ("QSP", cnt["QSP"]),
                   ("PE", cnt["PE"]), ("ACT", cnt["ACT"]),
                   ("DVE", cnt["DVE"]))
            pairs = [(dbg_qT, qT.rearrange("h p t -> (h p) t")),
                     (dbg_ct, contrib[:, :]),
                     (dbg_yT, yT.rearrange("t p c -> (t p) c")),
                     (dbg_x2, x2s[:, :]),
                     (dbg_xnT, xnT.rearrange("a p t -> (a p) t")),
                     (dbg_comb, combsb[:, :])]
            for oi, (dst, srcap) in enumerate(pairs):
                dma("pool", dst[:, :], srcap, dbw if oi == 0 else ())
        recs.append(("pool", (("QPL", cnt["QPL"]),), None, None))
        recs.append(("sp", (("QSP", cnt["QSP"]),), None, None))
        with nc.Block() as block:
            def runner(engname):
                def _run(eng):
                    for rengine, waits, emit, inc in recs:
                        if rengine != engname:
                            continue
                        for s_, v in waits:
                            eng.wait_ge(SEMS[s_], v)
                        if emit is None:
                            continue
                        ins = emit(eng)
                        if inc:
                            ins.then_inc(SEMS[inc[0]], inc[1])
                return _run

            block.sync(runner("sp"))
            block.gpsimd(runner("pool"))
            block.tensor(runner("pe"))
            block.scalar(runner("act"))
            block.vector(runner("dve"))
        st.close()
        return nc

    dma("sp", padkill[:, :], padkill_in[:, :])
    dma("sp", sidx[:, :], kvidx[:, :])
    t_ident = dma("sp", identf[:, :], ident_in[:, :])
    dma("pool", ropept[:, :], ropept_in[:, :])
    for dc in range(16):
        dma("pool", gate_sb[:, dc * 8:(dc + 1) * 8], gate_t[dc, :, :])
    t_sin = 0
    dve(lambda e: e.memset(onescol[:, :], 1.0))
    dve(lambda e: e.memset(epsb[:, :], EPS))
    dve(lambda e: e.memset(onesrow[:, :], 1.0))
    dve(lambda e: e.memset(qn2[:, :], 0.0))
    dve(lambda e: e.memset(kn2own[:, :], 0.0))

    def emit_ln_tile(x_src, tok_src, src_sem, stg_war_qpl, prev_dve=0):
        w = ((src_sem, tok_src),)
        wp = w + ((("DVE", prev_dve),) if prev_dve else ())
        t1 = dve(lambda e: e.reduce_sum(nsc[0][:, :], x_src[:, :], axis=AX.X),
                 w)
        t2 = act(lambda e: e.activation(xf[:, :], x_src[:, :], AF.Square,
                                        accum_out=nsc[1][:, :]), wp)
        t3 = dve(lambda e: e.tensor_scalar_mul(nsc[2][:, :], nsc[0][:, :],
                                               1.0 / D), (("DVE", t1),))
        t4 = dve(lambda e: e.tensor_scalar_mul(nsc[3][:, :], nsc[1][:, :],
                                               1.0 / D), (("ACT", t2),))
        t5 = dve(lambda e: e.tensor_mul(nsc[4][:, :], nsc[2][:, :],
                                        nsc[2][:, :]), (("DVE", t3),))
        t6 = dve(lambda e: e.tensor_sub(nsc[5][:, :], nsc[3][:, :],
                                        nsc[4][:, :]),
                 (("DVE", t5), ("DVE", t4)))
        t7 = act(lambda e: e.activation(nsc[6][:, :], nsc[5][:, :], AF.Ln,
                                        bias=epsb[:, :]), (("DVE", t6),))
        t8 = act(lambda e: e.activation(nsc[7][:, :], nsc[6][:, :], AF.Exp,
                                        scale=-0.5), (("ACT", t7),))
        t9 = dve(lambda e: e.tensor_scalar(xf[:, :], x_src[:, :], nsc[2][:, :],
                                           None, ALU.subtract),
                 (("DVE", t3), ("ACT", t2)))
        t10 = dve(lambda e: e.tensor_scalar(xnf[:, :], xf[:, :], nsc[7][:, :],
                                            None, ALU.mult),
                  (("DVE", t9), ("ACT", t8)))
        tok_stg = 0
        for half in range(2):
            tok_pe = None
            for j in range(8):
                dcb = half * 8 + j
                wj = ()
                if j == 0:
                    wj = (("DVE", t10),) if half == 0 else (("DVE", tok_stg),)
                tok_pe = pe(
                    (lambda dcb, j: lambda e: e.transpose(
                        pC[:, j * 128:(j + 1) * 128],
                        xnf[:, dcb * 128:(dcb + 1) * 128],
                        identf[:, :]))(dcb, j), wj)
            sw = [("PE", tok_pe)]
            if half == 0 and stg_war_qpl:
                sw.append(("QPL", stg_war_qpl))
            tok_stg = dve(
                (lambda half: lambda e: e.tensor_copy(
                    stg[:, half * 1024:(half + 1) * 1024], pC[:, :]))(half),
                tuple(sw))
        return tok_stg

    t_xuse = [0, 0]
    t_stg_dma = 0
    t_p0prev = 0
    recs.append(("pe", (("QSP", t_ident),), None, None))
    for t in range(NT):
        wr = [("DVE", t_xuse[t % 2])] if t_xuse[t % 2] else []
        tok_x = dma("sp", xtile[t % 2][:, :], x_own[t * 128:(t + 1) * 128, :],
                    tuple(wr))
        tok_stg = emit_ln_tile(xtile[t % 2], tok_x, "QSP", t_stg_dma,
                               prev_dve=t_p0prev)
        t_xuse[t % 2] = tok_stg
        t_p0prev = tok_stg
        t_stg_dma = dma(
            "pool",
            xnT[:, :, t * 128:(t + 1) * 128].rearrange("dc p tj -> p dc tj"),
            stg.rearrange("p (dc tj) -> p dc tj", dc=16),
            (("DVE", tok_stg),))
    xnT_done = t_stg_dma
    barrier()

    if KPH < 1:
        return finish()
    dma("pool", cossb[:, :], cos_t[:, :])
    t_sin = dma("pool", sinsb[:, :], sin_t[:, :])
    t_wv = 0
    for dc in range(16):
        t_wv = dma("pool", wvsb[:, dc * 2048:(dc + 1) * 2048], wv_t[dc, :, :])
    t_wq = [0, 0]
    t_qkev = [0, 0]
    t_rope = [0, 0]
    t_mmtok = [0, 0]
    t_spill = [0, 0]
    t_tsq = [0, 0]
    t_pCr = 0
    t_norm_mm = 0
    t_normred = 0
    t_xnTb = 0
    for blk in range(NBLK):
        for dc in range(16):
            t_xnTb = dma("sp", xnTb[:, dc * 1024:(dc + 1) * 1024],
                         xnT[dc, :, blk * 1024:(blk + 1) * 1024],
                         (("QPL", xnT_done),) if (dc == 0 and blk == 0) else
                         ((("PE", cnt["PE"]),) if dc == 0 else ()))
        for m in range(32):
            i = m % 2
            t_wq[i] = dma("pool", wqbuf[i][:, :],
                          wqkv_t[m].rearrange("p dc fj -> p (dc fj)"),
                          (("PE", t_mmtok[i]),) if t_mmtok[i] else ())
            mmw = [("QPL", t_wq[i]), ("QSP", t_xnTb)]
            if t_qkev[i]:
                mmw.append(("DVE", t_qkev[i]))
            tok_mm = None
            for dc in range(16):
                for ns in range(2):
                    tok_mm = pe((lambda i, dc, ns: lambda e: e.matmul(
                        psA[i][:, ns * 512:(ns + 1) * 512],
                        wqbuf[i][:, dc * 128:(dc + 1) * 128],
                        xnTb[:, dc * 1024 + ns * 512:dc * 1024 + (ns + 1) * 512],
                        start=(dc == 0), stop=(dc == 15),
                        skip_group_check=True))(i, dc, ns),
                        tuple(mmw) if (dc == 0 and ns == 0) else ())
            t_mmtok[i] = tok_mm
            ev_w = [("PE", tok_mm)]
            if t_rope[i]:
                ev_w.append(("DVE", t_rope[i]))
            t_qkev[i] = dve((lambda i: lambda e: e.tensor_copy(
                qkbf[i][:, :], psA[i][:, :]))(i), tuple(ev_w))
            rw = [("DVE", t_qkev[i])]
            if t_pCr:
                rw.append(("DVE", t_pCr))
            tok_rot = None
            for ns in range(2):
                tok_rot = pe((lambda i, ns: lambda e: e.matmul(
                    pC[:, ns * 512:(ns + 1) * 512], ropept[:, :],
                    qkbf[i][:, ns * 512:(ns + 1) * 512],
                    start=True, stop=True, skip_group_check=True))(i, ns),
                    tuple(rw) if ns == 0 else ())
            cosw = (("QPL", t_sin),) if (blk == 0 and m == 0) else ()
            t_pCr = dve((lambda blk: lambda e: e.tensor_mul(
                ropet1[:, :], pC[:, :],
                sinsb[:, blk * 1024:(blk + 1) * 1024]))(blk),
                (("PE", tok_rot),) + cosw)
            tr2 = dve((lambda i, blk: lambda e: e.tensor_mul(
                ropet2[:, :], qkbf[i][:, :],
                cossb[:, blk * 1024:(blk + 1) * 1024]))(i, blk),
                (("DVE", t_pCr),))
            if KDBG and blk == 0 and m == 0:
                dma("pool", dbg_q0[:, :], qkbf[i][:, :], (("DVE", tr2),))
                t_pCr = dve(lambda e: e.tensor_copy(dbgbuf[:, :], pC[:, :]),
                            (("DVE", t_pCr),))
                dma("pool", dbg_rot[:, :], dbgbuf[:, :],
                    (("DVE", cnt["DVE"]),))
                dma("pool", dbg_rt1[:, :], ropet1[:, :], (("DVE", tr2),))
                dma("pool", dbg_rt2[:, :], ropet2[:, :], (("DVE", tr2),))
                dma("pool", dbg_cos[:, :], cossb[:, 0:1024],
                    (("QPL", t_sin),))
                dma("pool", dbg_sin[:, :], sinsb[:, 0:1024],
                    (("QPL", t_sin),))
            rope_w = [("DVE", tr2)]
            rope_w.append(("QPL", cnt["QPL"])) if (KDBG and blk == 0 and
                                                   m == 0) else None
            if t_spill[i]:
                rope_w.append(("QPL", t_spill[i]))
            if t_tsq[i]:
                rope_w.append(("ACT", t_tsq[i]))
            t_rope[i] = dve((lambda i: lambda e: e.tensor_add(
                ropedbf[i][:, :], ropet1[:, :], ropet2[:, :]))(i),
                tuple(rope_w))
            sq_w = [("DVE", t_rope[i])]
            if t_norm_mm:
                sq_w.append(("PE", t_norm_mm))
            t_tsq[i] = act((lambda i: lambda e: e.activation(
                sqbf[:, :], ropedbf[i][:, :], AF.Square))(i), tuple(sq_w))
            nmw = [("ACT", t_tsq[i])]
            if t_normred:
                nmw.append(("DVE", t_normred))
            pe(lambda e: e.matmul(pD[0:1, :], onescol[:, :], sqbf[:, 0:512],
                                  start=True, stop=True,
                                  skip_group_check=True), tuple(nmw))
            t_norm_mm = pe(lambda e: e.matmul(
                pE[0:1, :], onescol[:, :], sqbf[:, 512:1024],
                start=True, stop=True, skip_group_check=True))
            ta = dve(lambda e: e.reduce_max(n1a[:, :], pD[0:1, :], axis=AX.X),
                     (("PE", t_norm_mm),))
            tb = dve(lambda e: e.reduce_max(n1b[:, :], pE[0:1, :], axis=AX.X),
                     (("DVE", ta),))
            tc = dve(lambda e: e.tensor_max(n1a[:, :], n1a[:, :], n1b[:, :]),
                     (("DVE", tb),))
            dst = qn2 if m < 16 else kn2own
            hh = m % 16
            t_normred = dve((lambda dst, hh: lambda e: e.tensor_max(
                dst[0:1, hh:hh + 1], dst[0:1, hh:hh + 1], n1a[0:1, :]))(
                    dst, hh), (("DVE", tc),))
            if m < 16:
                t_spill[i] = dma("pool",
                                 qT[m, :, blk * 1024:(blk + 1) * 1024],
                                 ropedbf[i][:, :], (("DVE", t_rope[i]),))
            else:
                t_spill[i] = dma("pool",
                                 contrib[(m - 16) * 128:(m - 15) * 128,
                                         blk * 1024:(blk + 1) * 1024],
                                 ropedbf[i][:, :], (("DVE", t_rope[i]),))
        t_vev = [0, 0]
        t_vsp = [0, 0]
        for vt in range(8):
            t = blk * 8 + vt
            i = vt % 2
            vmw = [("QPL", t_wv), ("QSP", t_xnTb),
                   ("DVE", max(t_qkev)), ("DVE", max(t_vev))]
            tok_mm = None
            for dc in range(16):
                for ns in range(4):
                    ppp = psA[ns // 2]
                    tok_mm = pe((lambda dc, ns, vt, ppp: lambda e: e.matmul(
                        ppp[:, (ns % 2) * 512:((ns % 2) + 1) * 512],
                        xnTb[:, dc * 1024 + vt * 128:dc * 1024 + (vt + 1) * 128],
                        wvsb[:, dc * 2048 + ns * 512:dc * 2048 + (ns + 1) * 512],
                        start=(dc == 0), stop=(dc == 15),
                        skip_group_check=True))(dc, ns, vt, ppp),
                        tuple(vmw) if (dc == 0 and ns == 0) else ())
            evw = [("PE", tok_mm)]
            if t_vsp[i]:
                evw.append(("QPL", t_vsp[i]))
            dve((lambda i: lambda e: e.tensor_copy(
                voutb[i][:, 0:1024], pA[:, :]))(i), tuple(evw))
            t_vev[i] = dve((lambda i: lambda e: e.tensor_copy(
                voutb[i][:, 1024:2048], pB[:, :]))(i))
            t_vsp[i] = dma(
                "pool",
                contrib[2048:4096, :].rearrange(
                    "(h p) (tl d) -> p h tl d", p=128, d=128)[:, :, t, :],
                voutb[i].rearrange("p (h d) -> p h d", h=16),
                (("DVE", t_vev[i]),))
        t_qkev = [cnt["DVE"], cnt["DVE"]]
    tkb = dve(lambda e: e.tensor_copy(kn2bf[:, :], kn2own[:, :]),
              (("DVE", t_normred),))
    dma("pool", contrib[4096:4097, 0:16], kn2bf[0:1, :], (("DVE", tkb),))
    barrier()

    if KPH < 2:
        return finish()
    rec("pool", lambda e: e.collective_compute(
        "AllGather", ALU.bypass, replica_groups=[list(range(8))],
        ins=[contrib[:, :].opt()], outs=[gathered[:, :].opt()]),
        ((("QPL", cnt["QPL"])),), ("CC", 1))
    cc_tok = cnt["CC"]
    pe(lambda e: e.matmul(pD[0:1, :], onescol[:, :], xnTb[:, 0:512],
                          start=True, stop=True, skip_group_check=True),
       (("DVE", cnt["DVE"]),))
    for _ in range(2000):
        pe(lambda e: e.matmul(pD[0:1, :], onescol[:, :], xnTb[:, 0:512],
                              start=True, stop=True, skip_group_check=True),
           (), bump=False)
    pe(lambda e: e.matmul(pD[0:1, :], onescol[:, :], xnTb[:, 0:512],
                          start=True, stop=True, skip_group_check=True))
    barrier()

    if KPH < 3:
        return finish()
    t_mask = 0
    for ck in range(8):
        t_mask = dma("sp", masksb[:, ck * 1024:(ck + 1) * 1024], trimask[ck])
    t_knr = 0
    for r in range(8):
        t_knr = dma("sp", knrow[0:1, r * 16:(r + 1) * 16],
                    gathered[4096 + r * CROWS:4097 + r * CROWS, 0:16],
                    (("CC", cc_tok),) if r == 0 else ())
    tk1 = dve(lambda e: e.reduce_max(
        kn2all[0:1, :],
        knrow[0:1, :].rearrange("a (r h) -> a h r", r=8), axis=AX.X),
        (("QSP", t_knr),))
    tk2 = dve(lambda e: e.tensor_mul(prodn[:, :], qn2[:, :], kn2all[:, :]),
              (("DVE", tk1),))
    tk3 = act(lambda e: e.activation(cvec[:, :], prodn[:, :], AF.Ln,
                                     scale=1.0 / DH), (("DVE", tk2),))
    tk4 = act(lambda e: e.activation(cvec[:, :], cvec[:, :], AF.Exp,
                                     scale=0.5), (("ACT", tk3),))
    tk5 = dve(lambda e: e.tensor_scalar_mul(cvec[:, :], cvec[:, :], 1.02),
              (("ACT", tk4),))
    tk6 = dve(lambda e: e.tensor_copy(cbf[:, :], cvec[:, :]), (("DVE", tk5),))
    tk7 = pe(lambda e: e.matmul(pD[:, 0:16], onesrow[:, :], cbf[0:1, :],
                                start=True, stop=True, skip_group_check=True),
             (("DVE", tk6),))
    tk8 = dve(lambda e: e.tensor_scalar_mul(negCall[:, :], pD[:, 0:16], -1.0),
              (("PE", tk7),))

    gview = gathered.rearrange("r c -> (r c)")[0:128 * 1024].rearrange(
        "(p c) -> p c", c=1024)

    t_kld = [0, 0]
    t_vld = [0, 0]
    t_pb = [0, 0]
    t_sps = 0
    t_ypc = 0
    t_yspill = 0
    t_pdpe = tk8
    t_quse = [0, 0]
    for h in range(H):
        t_bias = dve((lambda h: lambda e: e.tensor_scalar(
            biastab[:, :], padkill[:, :], negCall[:, h:h + 1], None,
            ALU.add))(h), (("DVE", tk8),))
        for s in range(2):
            qp = (h * 2 + s) % 2
            qw = [("PE", t_quse[qp])] if t_quse[qp] else []
            t_qld = dma("sp", qbuf2[qp][:, :],
                        qT[h, :, s * 1024:(s + 1) * 1024], tuple(qw))
            nslots = 4 if s == 0 else 8
            npairs = nslots * 4
            loaded = set()
            s_toks = {}

            def ensure_loaded(j, s=s, h=h):
                if j in loaded:
                    return
                loaded.add(j)
                slot = j if s == 0 else 4 + j
                jj = j % 2
                colk = slot * H + h
                colv = (NSLOT + slot) * H + h
                kw = [("CC", cc_tok)]
                if t_pb[jj]:
                    kw.append(("PE", t_pb[jj]))
                t_kld[jj] = rec("pool", (lambda jj, colk: lambda e:
                                e.indirect_dma_start(
                                    kbuf[jj][:, :], None, gview,
                                    IndirectOffsetOnAxis(
                                        sidx[:, colk:colk + 1], 0)))(jj, colk),
                                tuple(kw), ("QPL", 16))
                t_vld[jj] = rec("pool", (lambda jj, colv: lambda e:
                                e.indirect_dma_start(
                                    vbuf[jj][:, :], None, gview,
                                    IndirectOffsetOnAxis(
                                        sidx[:, colv:colv + 1], 0)))(jj, colv),
                                (), ("QPL", 16))

            def issue_S(pp, qp=qp):
                j = pp // 4
                ensure_loaded(j)
                jj = j % 2
                sw = [("QPL", t_kld[jj]), ("QSP", t_qld)]
                if t_sps:
                    sw.append(("ACT", t_sps))
                tok = None
                for c in range(2):
                    ck = (pp % 4) * 2 + c
                    for ns in range(2):
                        tok = pe((lambda jj, ck, c, ns, qp: lambda e: e.matmul(
                            pS[:, c * 1024 + ns * 512:
                               c * 1024 + (ns + 1) * 512],
                            kbuf[jj][:, ck * 128:(ck + 1) * 128],
                            qbuf2[qp][:, ns * 512:(ns + 1) * 512],
                            start=True, stop=True,
                            skip_group_check=True))(jj, ck, c, ns, qp),
                            tuple(sw) if (c == 0 and ns == 0) else ())
                s_toks[pp] = tok

            issue_S(0)
            tok_l = 0
            tok_pv = 0
            for pp in range(npairs):
                j = pp // 4
                jj = j % 2
                pj = pp % 2
                slot = j if s == 0 else 4 + j
                is_diag = (s == 0 and j == 3) or (s == 1 and j == 7)
                tok_s = s_toks.pop(pp)
                ew = [("PE", tok_s)]
                if is_diag:
                    ck0 = (pp % 4) * 2
                    tmsk = dve((lambda ck0: lambda e: e.tensor_add(
                        pS[:, :], pS[:, :],
                        masksb[:, ck0 * 1024:(ck0 + 2) * 1024]))(ck0),
                        (("PE", tok_s), ("QSP", t_mask)))
                    ew = [("DVE", tmsk)]
                pw = ew + [("DVE", t_bias)]
                if t_pb[pj]:
                    pw.append(("PE", t_pb[pj]))
                t_sps = act((lambda pj, slot: lambda e:
                             e.activation(
                                 pbuf[pj][:, :], pS[:, :], AF.Exp,
                                 bias=biastab[:, slot:slot + 1],
                                 scale=ATT_SCALE))(pj, slot),
                            tuple(pw))
                if pp + 1 < npairs:
                    issue_S(pp + 1)
                pvw = [("ACT", t_sps), ("QPL", t_vld[jj])]
                if pp == 0:
                    pvw.append(("DVE", t_ypc))
                st_f = (pp == 0)
                sp_f = (pp == npairs - 1)
                first_mm = True
                for c in range(2):
                    ck = (pp % 4) * 2 + c
                    for ns in range(2):
                        tok_pv = pe((lambda jj, ck, c, ns, st_f, sp_f, pj:
                                     lambda e: e.matmul(
                                         pC[:, ns * 512:(ns + 1) * 512],
                                         vbuf[jj][:, ck * 128:(ck + 1) * 128],
                                         pbuf[pj][:, c * 1024 + ns * 512:
                                                  c * 1024 + (ns + 1) * 512],
                                         start=(st_f and c == 0),
                                         stop=(sp_f and c == 1),
                                         skip_group_check=True))(
                                             jj, ck, c, ns, st_f, sp_f, pj),
                                    tuple(pvw) if first_mm else ())
                        first_mm = False
                lw = (("DVE", t_pdpe),) if pp == 0 else ()
                lps = [pD, pE]
                for c in range(2):
                    for ns in range(2):
                        tok_l = pe((lambda ns, c, st_f, sp_f, pj, lp:
                                    lambda e: e.matmul(
                                        lp[0:1, :], onescol[:, :],
                                        pbuf[pj][:, c * 1024 + ns * 512:
                                                 c * 1024 + (ns + 1) * 512],
                                        start=(st_f and c == 0),
                                        stop=(sp_f and c == 1),
                                        skip_group_check=True))(
                                            ns, c, st_f, sp_f, pj,
                                            lps[ns]),
                                   lw if (c == 0 and ns == 0) else (),
                                   bump=(c == 1 and ns == 1))
                t_pb[pj] = tok_l
            t_quse[qp] = tok_l
            e1 = dve(lambda e: e.tensor_copy(lrow[0:1, 0:512], pD[0:1, :]),
                     (("PE", tok_l),))
            e2 = dve(lambda e: e.tensor_copy(lrow[0:1, 512:1024], pE[0:1, :]),
                     (("DVE", e1),))
            e3 = dve(lambda e: e.reciprocal(linv[0:1, :], lrow[0:1, :]),
                     (("DVE", e2),))
            e4 = dve(lambda e: e.tensor_copy(linvbf[0:1, :], linv[0:1, :]),
                     (("DVE", e3),))
            bc = None
            for ns in range(2):
                bc = pe((lambda ns, lp: lambda e: e.matmul(
                    lp[:, :], onesrow[:, :],
                    linvbf[0:1, ns * 512:(ns + 1) * 512],
                    start=True, stop=True, skip_group_check=True))(
                        ns, [pD, pE][ns]),
                    (("DVE", e4),) if ns == 0 else ())
            e5a = dve(lambda e: e.tensor_copy(linvb[:, 0:512], pD[:, :]),
                      (("PE", bc),))
            e5 = dve(lambda e: e.tensor_copy(linvb[:, 512:1024], pE[:, :]),
                     (("DVE", e5a),))
            t_pdpe = e5
            yw = [("DVE", e5), ("PE", tok_pv)]
            if t_yspill:
                yw.append(("QPL", t_yspill))
            t_ypc = dve(lambda e: e.tensor_mul(ybuf[:, :], pC[:, :],
                                               linvb[:, :]), tuple(yw))
            t_yspill = dma(
                "pool",
                yT[:, :, h * 128:(h + 1) * 128].rearrange(
                    "tl p d -> p tl d")[:, s * 8:(s + 1) * 8, :],
                ybuf.rearrange("p (tl d) -> p tl d", d=128),
                (("DVE", t_ypc),))
    barrier()

    if KPH < 4:
        return finish()
    t_wo = 0
    for dc in range(16):
        t_wo = dma("pool", wosb[:, dc * 2048:(dc + 1) * 2048], wo_t[dc])
    t_yt = [0, 0]
    t_aev = 0
    t_gate_dve = 0
    xnT2_wtok = [0] * NT
    t_x2w = [0] * NT
    t_stg2_dma = 0
    t_p4prev = 0
    for t in range(NT):
        i = t % 2
        t_ytld = dma("sp", ytb[i][:, :], yT[t],
                     (("PE", t_yt[i]),) if t_yt[i] else ())
        t_xld = dma("sp", xtile[i][:, :], x_own[t * 128:(t + 1) * 128, :],
                    (("DVE", t_xuse[i]),) if t_xuse[i] else ())
        mw = [("QPL", t_wo), ("QSP", t_ytld)]
        if t_aev:
            mw.append(("DVE", t_aev))
        tok_mm = None
        for hh in range(16):
            for ns in range(4):
                ppp = psA[ns // 2]
                tok_mm = pe((lambda i, hh, ns, ppp: lambda e: e.matmul(
                    ppp[:, (ns % 2) * 512:((ns % 2) + 1) * 512],
                    ytb[i][:, hh * 128:(hh + 1) * 128],
                    wosb[:, hh * 2048 + ns * 512:hh * 2048 + (ns + 1) * 512],
                    start=(hh == 0), stop=(hh == 15),
                    skip_group_check=True))(i, hh, ns, ppp),
                    tuple(mw) if (hh == 0 and ns == 0) else ())
        t_yt[i] = tok_mm
        a1 = dve(lambda e: e.tensor_copy(aout[:, 0:1024], pA[:, :]),
                 (("PE", tok_mm),))
        t_aev = dve(lambda e: e.tensor_copy(aout[:, 1024:2048], pB[:, :]),
                    (("DVE", a1),))
        l1 = dve(lambda e: e.reduce_sum(nsc[0][:, :], aout[:, :], axis=AX.X),
                 (("DVE", t_aev),))
        l2 = act(lambda e: e.activation(xf[:, :], aout[:, :], AF.Square,
                                        accum_out=nsc[1][:, :]),
                 (("DVE", t_aev),))
        l3 = dve(lambda e: e.tensor_scalar_mul(nsc[2][:, :], nsc[0][:, :],
                                               1.0 / D), (("DVE", l1),))
        l4 = dve(lambda e: e.tensor_scalar_mul(nsc[3][:, :], nsc[1][:, :],
                                               1.0 / D), (("ACT", l2),))
        l5 = dve(lambda e: e.tensor_mul(nsc[4][:, :], nsc[2][:, :],
                                        nsc[2][:, :]), (("DVE", l3),))
        l6 = dve(lambda e: e.tensor_sub(nsc[5][:, :], nsc[3][:, :],
                                        nsc[4][:, :]),
                 (("DVE", l5), ("DVE", l4)))
        l7 = act(lambda e: e.activation(nsc[6][:, :], nsc[5][:, :], AF.Ln,
                                        bias=epsb[:, :]), (("DVE", l6),))
        l8 = act(lambda e: e.activation(nsc[7][:, :], nsc[6][:, :], AF.Exp,
                                        scale=-0.5), (("ACT", l7),))
        l9 = dve(lambda e: e.tensor_scalar(xf[:, :], aout[:, :], nsc[2][:, :],
                                           None, ALU.subtract),
                 (("DVE", l3), ("ACT", l2)))
        l10w = [("DVE", l9), ("ACT", l8), ("QSP", t_xld)]
        if t > 0:
            l10w.append(("QPL", t_x2w[t - 1]))
        l10 = dve((lambda i: lambda e: e.scalar_tensor_tensor(
            x2t[:, :], xf[:, :], nsc[7][:, :], xtile[i][:, :],
            ALU.mult, ALU.add))(i), tuple(l10w))
        t_xuse[i] = l10
        t_x2w[t] = dma("pool", x2s[t * 128:(t + 1) * 128, :], x2t[:, :],
                       (("DVE", l10),))
        tok_stg2 = emit_ln_tile(x2t, l10, "DVE", t_stg2_dma,
                                prev_dve=t_p4prev)
        t_stg2_dma = dma(
            "pool",
            xnT[:, :, t * 128:(t + 1) * 128].rearrange("dc p tj -> p dc tj"),
            stg.rearrange("p (dc tj) -> p dc tj", dc=16),
            (("DVE", tok_stg2),))
        xnT2_wtok[t] = t_stg2_dma
        t_p4prev = tok_stg2
        gw = [("DVE", tok_stg2)]
        if t_gate_dve:
            gw.append(("DVE", t_gate_dve))
        tok_g = None
        for dc in range(16):
            tok_g = pe((lambda dc: lambda e: e.matmul(
                pD[:, 0:8], stg[:, dc * 128:(dc + 1) * 128],
                gate_sb[:, dc * 8:(dc + 1) * 8],
                start=(dc == 0), stop=(dc == 15),
                skip_group_check=True))(dc),
                tuple(gw) if dc == 0 else ())
        g1 = dve(lambda e: e.reduce_max(nsc[0][:, :], pD[:, 0:8], axis=AX.X),
                 (("PE", tok_g),))
        g2 = dve(lambda e: e.tensor_scalar_mul(nsc[1][:, :], nsc[0][:, :],
                                               -1.0), (("DVE", g1),))
        g3 = act(lambda e: e.activation(xnf[:, 0:8], pD[:, 0:8], AF.Exp,
                                        bias=nsc[1][:, :]), (("DVE", g2),))
        g4 = dve(lambda e: e.reduce_sum(nsc[2][:, :], xnf[:, 0:8], axis=AX.X),
                 (("ACT", g3),))
        g5 = dve(lambda e: e.reciprocal(nsc[3][:, :], nsc[2][:, :]),
                 (("DVE", g4),))
        g6 = dve(lambda e: e.tensor_scalar(xnf[:, 8:16], xnf[:, 0:8],
                                           nsc[3][:, :], None, ALU.mult),
                 (("DVE", g5),))
        g7 = dve(lambda e: e.reduce_max(nsc[4][:, :], xnf[:, 8:16],
                                        axis=AX.X), (("DVE", g6),))
        g8 = dve(lambda e: e.tensor_scalar(xnf[:, 16:24], xnf[:, 8:16],
                                           nsc[4][:, :], None, ALU.is_ge),
                 (("DVE", g7),))
        g9 = dve(lambda e: e.scalar_tensor_tensor(
            xnf[:, 24:32], xnf[:, 16:24], KILL, xnf[:, 8:16],
            ALU.mult, ALU.add), (("DVE", g8),))
        g10 = dve(lambda e: e.reduce_max(nsc[5][:, :], xnf[:, 24:32],
                                         axis=AX.X), (("DVE", g9),))
        g11 = dve(lambda e: e.tensor_scalar(xnf[:, 16:24], xnf[:, 8:16],
                                            nsc[5][:, :], None, ALU.is_ge),
                  (("DVE", g10),))
        t_gate_dve = dve((lambda t: lambda e: e.tensor_mul(
            combsb[:, t * 8:(t + 1) * 8], xnf[:, 16:24], xnf[:, 8:16]))(t),
            (("DVE", g11),))
    barrier()

    if KPH < 5:
        return finish()
    t_gdve = [0, 0]
    t_hmm = [0, 0]
    t_w2res = 0
    t_eo = 0
    t_eodve = 0
    t_outprev = 0
    for blk in range(NBLK):
        for dc in range(16):
            t_xnTb = dma("sp", xnTb[:, dc * 1024:(dc + 1) * 1024],
                         xnT[dc, :, blk * 1024:(blk + 1) * 1024],
                         (("QPL", xnT2_wtok[NT - 1]), ("PE", cnt["PE"]))
                         if dc == 0 else ())
        w13_pref = {}

        def load_w13(ee, ic):
            key = (ee, ic)
            if key in w13_pref:
                return w13_pref[key]
            i = ic % 2
            w1w = [("PE", t_hmm[i])] if t_hmm[i] else []
            dma("pool", w1buf[i][:, :],
                w13_t[0, ee, ic].rearrange("p dc ij -> p (dc ij)"),
                tuple(w1w))
            t_w3 = dma("pool", w3buf[i][:, :],
                       w13_t[1, ee, ic].rearrange("p dc ij -> p (dc ij)"))
            w13_pref[key] = t_w3
            return t_w3

        for ee in range(KNE):
            t_sil = 0
            for ic in range(8):
                i = ic % 2
                t_w3 = load_w13(ee, ic)
                hw = [("QPL", t_w3), ("QSP", t_xnTb)]
                if t_sil:
                    hw.append(("ACT", t_sil))
                if ic == 0 and t_eodve:
                    hw.append(("DVE", t_eodve))
                tok_h = None
                for dc in range(16):
                    for ns in range(2):
                        tok_h = pe((lambda i, dc, ns: lambda e: e.matmul(
                            pA[:, ns * 512:(ns + 1) * 512],
                            w1buf[i][:, dc * 128:(dc + 1) * 128],
                            xnTb[:, dc * 1024 + ns * 512:
                                 dc * 1024 + (ns + 1) * 512],
                            start=(dc == 0), stop=(dc == 15),
                            skip_group_check=True))(i, dc, ns),
                            tuple(hw) if (dc == 0 and ns == 0) else ())
                h3w = [("DVE", t_gdve[1 - i])] if t_gdve[1 - i] else []
                if ic == 0 and t_eodve:
                    h3w.append(("DVE", t_eodve))
                tok_h3 = None
                for dc in range(16):
                    for ns in range(2):
                        tok_h3 = pe((lambda i, dc, ns: lambda e: e.matmul(
                            pB[:, ns * 512:(ns + 1) * 512],
                            w3buf[i][:, dc * 128:(dc + 1) * 128],
                            xnTb[:, dc * 1024 + ns * 512:
                                 dc * 1024 + (ns + 1) * 512],
                            start=(dc == 0), stop=(dc == 15),
                            skip_group_check=True))(i, dc, ns),
                            tuple(h3w) if (dc == 0 and ns == 0) else ())
                t_hmm[i] = tok_h3
                silw = [("PE", tok_h)]
                if t_gdve[1 - i]:
                    silw.append(("DVE", t_gdve[1 - i]))
                t_sil = act(lambda e: e.activation(sgl[:, :], pA[:, :],
                                                   AF.Silu), tuple(silw))
                t_gdve[i] = dve((lambda ic: lambda e: e.tensor_mul(
                    gbuf[:, ic * 1024:(ic + 1) * 1024], sgl[:, :],
                    pB[:, :]))(ic), (("ACT", t_sil), ("PE", tok_h3)))
            if not KEO:
                continue
            w2w = [("DVE", cnt["DVE"])]
            if t_eo:
                w2w.append(("PE", t_eo))
            for ic in range(8):
                t_w2res = dma("pool", w2res[:, ic * 2048:(ic + 1) * 2048],
                              w2_t[ee, ic], tuple(w2w) if ic == 0 else ())
            if ee + 1 < KNE:
                load_w13(ee + 1, 0)
                load_w13(ee + 1, 1)
            last_g = max(t_gdve)
            if KDBG and blk == 0 and ee == 0:
                dma("pool", dbg_g[:, :], gbuf[:, :], (("DVE", last_g),))
            for tt in range(8):
                t = blk * 8 + tt
                ew = [("QPL", t_w2res), ("DVE", last_g)]
                if t_eodve:
                    ew.append(("DVE", t_eodve))
                tok_eo = None
                for ic in range(8):
                    for ds in range(4):
                        ppp = psA[ds // 2]
                        tok_eo = pe((lambda ic, ds, tt, ppp: lambda e:
                                     e.matmul(
                                         ppp[:, (ds % 2) * 512:
                                             ((ds % 2) + 1) * 512],
                                         gbuf[:, ic * 1024 + tt * 128:
                                              ic * 1024 + (tt + 1) * 128],
                                         w2res[:, ic * 2048 + ds * 512:
                                               ic * 2048 + (ds + 1) * 512],
                                         start=(ic == 0), stop=(ic == 7),
                                         skip_group_check=True))(
                                             ic, ds, tt, ppp),
                                    tuple(ew) if (ic == 0 and ds == 0) else ())
                t_eo = tok_eo
                accw = (("PE", tok_eo),)
                for half in range(2):
                    ppp = psA[half]
                    dst = eoacc[:, tt * 2048 + half * 1024:
                                tt * 2048 + (half + 1) * 1024]
                    cs = combsb[:, t * 8 + min(ee, 7):t * 8 + min(ee, 7) + 1]
                    if ee == 0:
                        t_eodve = dve((lambda dst, ppp, cs: lambda e:
                                       e.tensor_scalar(dst, ppp[:, :], cs,
                                                       None, ALU.mult))(
                                           dst, ppp, cs),
                                      accw if half == 0 else ())
                    elif ee < E:
                        dve((lambda ppp, cs: lambda e:
                             e.tensor_scalar(eotmp[:, :], ppp[:, :], cs,
                                             None, ALU.mult))(ppp, cs),
                            accw if half == 0 else ())
                        t_eodve = dve((lambda dst: lambda e:
                                       e.tensor_add(dst, dst,
                                                    eotmp[:, :]))(dst))
                    else:
                        t_eodve = dve((lambda dst, ppp: lambda e:
                                       e.tensor_add(dst, dst, ppp[:, :]))(
                                           dst, ppp),
                                      accw if half == 0 else ())
            if KDBG and blk == 0 and ee == 0 and tt == 7:
                dma("pool", dbg_eo0[:, :], eoacc[:, :], (("DVE", t_eodve),))
        if KDBG and blk == 0:
            dma("pool", dbg_eoF[:, :], eoacc[:, :], (("DVE", t_eodve),))
        t_p6prev = 0
        for tt in range(8 if KP6 else 0):
            t = blk * 8 + tt
            eot = eoacc[:, tt * 2048:(tt + 1) * 2048]
            f1 = dve((lambda eot: lambda e: e.reduce_sum(
                nsc[0][:, :], eot, axis=AX.X))(eot), (("DVE", t_eodve),))
            f2 = act((lambda eot: lambda e: e.activation(
                p6a[:, :], eot, AF.Square, accum_out=nsc[1][:, :]))(eot),
                (("DVE", t_eodve), ("QPL", t_w2res),
                 ("DVE", t_p6prev)))
            f3 = dve(lambda e: e.tensor_scalar_mul(nsc[2][:, :], nsc[0][:, :],
                                                   1.0 / D), (("DVE", f1),))
            f4 = dve(lambda e: e.tensor_scalar_mul(nsc[3][:, :], nsc[1][:, :],
                                                   1.0 / D), (("ACT", f2),))
            f5 = dve(lambda e: e.tensor_mul(nsc[4][:, :], nsc[2][:, :],
                                            nsc[2][:, :]), (("DVE", f3),))
            f6 = dve(lambda e: e.tensor_sub(nsc[5][:, :], nsc[3][:, :],
                                            nsc[4][:, :]),
                     (("DVE", f5), ("DVE", f4)))
            f7 = act(lambda e: e.activation(nsc[6][:, :], nsc[5][:, :], AF.Ln,
                                            bias=epsb[:, :]), (("DVE", f6),))
            f8 = act(lambda e: e.activation(nsc[7][:, :], nsc[6][:, :],
                                            AF.Exp, scale=-0.5),
                     (("ACT", f7),))
            t_x2r = dma("sp", p6x2[:, :], x2s[t * 128:(t + 1) * 128, :],
                        (("QPL", t_x2w[t]), ("DVE", t_eodve),
                         ("DVE", t_p6prev)))
            f9 = dve((lambda eot: lambda e: e.tensor_scalar(
                p6a[:, :], eot, nsc[2][:, :], None, ALU.subtract))(eot),
                (("DVE", f3), ("ACT", f2)))
            f10w = [("DVE", f9), ("DVE", f6), ("ACT", f8), ("QSP", t_x2r)]
            if t_outprev:
                f10w.append(("QPL", t_outprev))
            f10 = dve(lambda e: e.scalar_tensor_tensor(
                p6o[:, :], p6a[:, :], nsc[7][:, :], p6x2[:, :],
                ALU.mult, ALU.add), tuple(f10w))
            t_outprev = dma("pool", out[t * 128:(t + 1) * 128, :], p6o[:, :],
                            (("DVE", f10),))
            t_p6prev = f10

    return finish()



def _host_prepare(inputs):
    W_qkv = np.asarray(inputs["L0_attn_W_qkv"], np.float32)
    W_o = np.asarray(inputs["L0_attn_W_o"], np.float32)
    gate_w = np.asarray(inputs["L0_ffn_gate_w"], np.float32)
    w1 = np.asarray(inputs["L0_ffn_w1"], np.float32)
    w2 = np.asarray(inputs["L0_ffn_w2"], np.float32)
    w3 = np.asarray(inputs["L0_ffn_w3"], np.float32)
    ws1 = np.asarray(inputs["L0_ffn_ws1"], np.float32)
    ws2 = np.asarray(inputs["L0_ffn_ws2"], np.float32)
    ws3 = np.asarray(inputs["L0_ffn_ws3"], np.float32)

    shared = {}
    shared["wqkv_t"] = np.ascontiguousarray(
        W_qkv[:4096].reshape(32, 128, 16, 128).transpose(0, 3, 2, 1))
    shared["wv_t"] = np.ascontiguousarray(
        W_qkv[4096:].reshape(2048, 16, 128).transpose(1, 2, 0))
    shared["wo_t"] = np.ascontiguousarray(
        W_o.reshape(2048, 16, 128).transpose(1, 2, 0))
    shared["gate_t"] = np.ascontiguousarray(
        gate_w.reshape(8, 16, 128).transpose(1, 2, 0))
    w13 = np.zeros((2, NE, 8, 128, 16, 128), np.float32)
    for wi, warr, wsarr in ((0, w1, ws1), (1, w3, ws3)):
        a = warr.reshape(8, 8, 128, 16, 128)
        w13[wi, :8] = a.transpose(0, 1, 4, 3, 2)
        b = wsarr.reshape(8, 128, 16, 128)
        w13[wi, 8] = b.transpose(0, 3, 2, 1)
    shared["w13_t"] = np.ascontiguousarray(w13)
    w2t = np.zeros((NE, 8, 128, 2048), np.float32)
    a = w2.reshape(8, 4, 512, 8, 128)
    w2t[:8] = a.transpose(0, 3, 4, 1, 2).reshape(8, 8, 128, 2048)
    b = ws2.reshape(4, 512, 8, 128)
    w2t[8] = b.transpose(2, 3, 0, 1).reshape(8, 128, 2048)
    shared["w2_t"] = np.ascontiguousarray(w2t)
    tm = np.zeros((8, 128, 1024), np.float32)
    qq = np.arange(1024)[None, :]
    for ck in range(8):
        kk = ck * 128 + np.arange(128)[:, None]
        tm[ck] = np.where(qq >= kk, 0.0, NEG)
    shared["trimask"] = tm
    shared["ident"] = np.eye(128, dtype=np.float32)
    Pm = np.zeros((128, 128), np.float32)
    for j in range(64):
        Pm[2 * j, 2 * j + 1] = -1.0
        Pm[2 * j + 1, 2 * j] = 1.0
    shared["ropept"] = np.ascontiguousarray(Pm.T)

    invf = 1.0 / (10000.0 ** (np.arange(0, DH, 2, dtype=np.float64) / DH))
    pos = np.arange(L, dtype=np.float64)
    ang = pos[None, :] * np.repeat(invf, 2)[:, None]
    cos_all = np.cos(ang).astype(np.float32)
    sin_all = np.sin(ang).astype(np.float32)

    per_core = []
    x = np.asarray(inputs["L0_attn_x"], np.float32)
    for core in range(8):
        b, c = core // 4, core % 4
        lo, hi = _sub_ranges(core)
        rows = np.r_[1024 * lo:1024 * (lo + 1), 1024 * hi:1024 * (hi + 1)]
        d = dict(shared)
        d["x_own"] = np.ascontiguousarray(x[b].reshape(L, D)[rows])
        d["cos_t"] = np.ascontiguousarray(cos_all[:, rows])
        d["sin_t"] = np.ascontiguousarray(sin_all[:, rows])
        pk = np.zeros((128, NSLOT), np.float32)
        for j in range(3):
            if j >= c:
                pk[:, j] = KILL
        for j in range(7):
            if j >= 7 - c:
                pk[:, 4 + j] = KILL
        d["padkill"] = pk
        idx = np.zeros((128, 2 * NSLOT * H), np.int32)
        p = np.arange(128)

        def kt_of(s, j):
            if s == 0:
                return c if j == 3 else j
            return (7 - c) if j == 7 else j

        for s in range(2):
            nsl = 4 if s == 0 else 8
            for j in range(nsl):
                slot = j if s == 0 else 4 + j
                kt = kt_of(s, j)
                rho = 4 * b + (kt if kt <= 3 else 7 - kt)
                halfbit = 1 if kt > 3 else 0
                for h in range(H):
                    kr = rho * CROWS + h * 128 + p
                    vr = rho * CROWS + 2048 + h * 128 + p
                    idx[:, (0 * NSLOT + slot) * H + h] = kr * 2 + halfbit
                    idx[:, (1 * NSLOT + slot) * H + h] = vr * 2 + halfbit
        d["kvidx"] = idx
        per_core.append(d)
    return per_core


_NC_CACHE = {}


def kernel(**inputs):
    in_maps = _host_prepare(inputs)
    if "nc" not in _NC_CACHE:
        _NC_CACHE["nc"] = build_nc()
    nc = _NC_CACHE["nc"]
    res = run_bass_kernel_spmd(nc, in_maps, core_ids=list(range(8)))
    out = np.empty((B, L, D), np.float32)
    for core in range(8):
        b = core // 4
        lo, hi = _sub_ranges(core)
        o = np.asarray(res.results[core]["out"], np.float32)
        out[b, 1024 * lo:1024 * (lo + 1)] = o[:1024]
        out[b, 1024 * hi:1024 * (hi + 1)] = o[1024:]
    return out


# version 06
# speedup vs baseline: 1.2744x; 1.0759x over previous
import math
import os
import numpy as np
import ml_dtypes
import concourse.bass as bass
import concourse.mybir as mybir
from concourse.bass import IndirectOffsetOnAxis
from concourse.bass_utils import run_bass_kernel_spmd
from contextlib import ExitStack

F32 = mybir.dt.float32
BF16 = mybir.dt.bfloat16
I32 = mybir.dt.int32
AF = mybir.ActivationFunctionType
ALU = mybir.AluOpType
AX = mybir.AxisListType

D = 2048
H = 16
DH = 128
B = 2
L = 8192
TOK = 2048
NT = 16
NBLK = 2
E = 8
NE = 9
EPS = 1e-5
ATT_SCALE = 1.0 / math.sqrt(DH)
NEG = -1e30
KILL = -1e38
NSLOT = 12
CROWS = 4104


def _sub_ranges(core):
    c = core % 4
    return c, 7 - c


KPH = int(os.environ.get("KPH", "9"))
KP6 = int(os.environ.get("KP6", "1"))
KDBG = int(os.environ.get("KDBG", "0"))
KNE = int(os.environ.get("KNE", str(NE)))
KEO = int(os.environ.get("KEO", "1"))


def build_nc():
    nc = bass.Bass("TRN2", target_bir_lowering=False, debug=False)

    def din(name, shape, dt=F32):
        return nc.dram_tensor(name, shape, dt, kind="ExternalInput").ap()

    x_own = din("x_own", [TOK, D])
    wqkv_t = din("wqkv_t", [32, 128, 16, 128])
    wv_t = din("wv_t", [16, 128, 2048])
    wo_t = din("wo_t", [16, 128, 2048])
    gate_t = din("gate_t", [16, 128, 8])
    w13_t = din("w13_t", [2, NE, 8, 128, 16, 128])
    w2_t = din("w2_t", [NE, 8, 128, 2048])
    cos_t = din("cos_t", [128, TOK])
    sin_t = din("sin_t", [128, TOK])
    trimask = din("trimask", [8, 128, 1024])
    padkill_in = din("padkill", [128, NSLOT])
    kvidx = din("kvidx", [128, 2 * NSLOT * H], I32)
    ident_in = din("ident", [128, 128])
    ropept_in = din("ropept", [128, 128])

    out = nc.dram_tensor("out", [TOK, D], F32, kind="ExternalOutput").ap()
    if KDBG:
        dbg_qT = nc.dram_tensor("dbg_qT", [H * 128, TOK], F32,
                                kind="ExternalOutput").ap()
        dbg_ct = nc.dram_tensor("dbg_ct", [CROWS, D], F32,
                                kind="ExternalOutput").ap()
        dbg_yT = nc.dram_tensor("dbg_yT", [NT * 128, H * 128], F32,
                                kind="ExternalOutput").ap()
        dbg_x2 = nc.dram_tensor("dbg_x2", [TOK, D], F32,
                                kind="ExternalOutput").ap()
        dbg_xnT = nc.dram_tensor("dbg_xnT", [16 * 128, TOK], F32,
                                 kind="ExternalOutput").ap()
        dbg_comb = nc.dram_tensor("dbg_comb", [128, 128], F32,
                                  kind="ExternalOutput").ap()
        dbg_q0 = nc.dram_tensor("dbg_q0", [128, 1024], F32,
                                kind="ExternalOutput").ap()
        dbg_rot = nc.dram_tensor("dbg_rot", [128, 1024], F32,
                                 kind="ExternalOutput").ap()
        dbg_rt1 = nc.dram_tensor("dbg_rt1", [128, 1024], F32,
                                 kind="ExternalOutput").ap()
        dbg_rt2 = nc.dram_tensor("dbg_rt2", [128, 1024], F32,
                                 kind="ExternalOutput").ap()
        dbg_cos = nc.dram_tensor("dbg_cos", [128, 1024], F32,
                                 kind="ExternalOutput").ap()
        dbg_sin = nc.dram_tensor("dbg_sin", [128, 1024], F32,
                                 kind="ExternalOutput").ap()
        dbg_g = nc.dram_tensor("dbg_g", [128, 8192], F32,
                               kind="ExternalOutput").ap()
        dbg_eo0 = nc.dram_tensor("dbg_eo0", [128, 16384], F32,
                                 kind="ExternalOutput").ap()
        dbg_eoF = nc.dram_tensor("dbg_eoF", [128, 16384], F32,
                                 kind="ExternalOutput").ap()

    xnT = nc.dram_tensor("xnT", [16, 128, TOK], BF16, kind="Internal").ap()
    qT = nc.dram_tensor("qT", [H, 128, TOK], BF16, kind="Internal").ap()
    contrib = nc.dram_tensor("contrib", [CROWS, D], BF16, kind="Internal").ap()
    gathered = nc.dram_tensor("gathered", [8 * CROWS, D], BF16, kind="Internal",
                              addr_space="Shared").ap()
    yT = nc.dram_tensor("yT", [NT, 128, H * 128], BF16, kind="Internal").ap()
    x2s = nc.dram_tensor("x2s", [TOK, D], F32, kind="Internal").ap()

    st = ExitStack()
    e_ = st.enter_context

    xnTb = nc.alloc_sbuf_tensor("xnTb", [128, 16 * 1024], BF16)
    stg = nc.alloc_sbuf_tensor("stg", [128, 2048], BF16)
    combsb = nc.alloc_sbuf_tensor("combsb", [128, 128], F32)
    padkill = nc.alloc_sbuf_tensor("padkill_sb", [128, NSLOT], F32)
    sidx = nc.alloc_sbuf_tensor("sidx", [128, 2 * NSLOT * H], I32)
    gate_sb = nc.alloc_sbuf_tensor("gate_sb", [128, 128], BF16)
    identf = nc.alloc_sbuf_tensor("identf", [128, 128], F32)
    ropept = nc.alloc_sbuf_tensor("ropept_sb", [128, 128], BF16)
    onescol = nc.alloc_sbuf_tensor("onescol", [128, 1], BF16)
    onesrow = nc.alloc_sbuf_tensor("onesrow", [1, 128], BF16)
    qn2 = nc.alloc_sbuf_tensor("qn2", [1, 16], F32)
    kn2own = nc.alloc_sbuf_tensor("kn2own", [1, 16], F32)
    kn2bf = nc.alloc_sbuf_tensor("kn2bf", [1, 16], BF16)
    knrow = nc.alloc_sbuf_tensor("knrow", [1, 128], BF16)
    kn2all = nc.alloc_sbuf_tensor("kn2all", [1, 16], F32)
    prodn = nc.alloc_sbuf_tensor("prodn", [1, 16], F32)
    cvec = nc.alloc_sbuf_tensor("cvec", [1, 16], F32)
    cbf = nc.alloc_sbuf_tensor("cbf", [1, 16], BF16)
    negCall = nc.alloc_sbuf_tensor("negCall", [128, 16], F32)
    biastab = nc.alloc_sbuf_tensor("biastab", [128, NSLOT], F32)
    nsc = [nc.alloc_sbuf_tensor(f"nsc{i}", [128, 1], F32) for i in range(8)]
    n1a = nc.alloc_sbuf_tensor("n1a", [1, 1], F32)
    n1b = nc.alloc_sbuf_tensor("n1b", [1, 1], F32)
    lrow = nc.alloc_sbuf_tensor("lrow", [1, 1024], F32)
    linv = nc.alloc_sbuf_tensor("linv", [1, 1024], F32)
    linvbf = nc.alloc_sbuf_tensor("linvbf", [1, 1024], BF16)
    epsb = nc.alloc_sbuf_tensor("epsb", [128, 1], F32)

    ARENA_KB = 136
    arena = nc.alloc_sbuf_tensor("arena", [128, ARENA_KB * 1024],
                                 mybir.dt.uint8)
    abase = nc.lookup_mloc(arena).addr

    def at(name, shape, dt, off_kb):
        return nc.alloc_sbuf_tensor_at(name, shape, dt,
                                       offset=abase + off_kb * 1024)

    wvsb = at("wvsb", [128, 16 * 2048], BF16, 0)
    cossb = at("cossb", [128, 2048], F32, 64)
    sinsb = at("sinsb", [128, 2048], F32, 72)
    wqbuf = [at(f"wqbuf{i}", [128, 2048], BF16, 80 + 4 * i) for i in range(2)]
    ropet1 = at("ropet1", [128, 1024], F32, 88)
    ropet2 = at("ropet2", [128, 1024], F32, 92)
    qkbf = [at(f"qkbf{i}", [128, 1024], BF16, 96 + 2 * i) for i in range(2)]
    ropedbf = [at(f"ropedbf{i}", [128, 1024], BF16, 100 + 2 * i)
               for i in range(2)]
    sqbf = at("sqbf", [128, 1024], BF16, 104)
    voutb = [at(f"voutb{i}", [128, 2048], BF16, 106 + 4 * i) for i in range(2)]
    masksb = at("masksb", [128, 8 * 1024], F32, 0)
    qbuf2 = [at(f"qbuf2_{i}", [128, 1024], BF16, [32, 56][i])
             for i in range(2)]
    kbuf = [at(f"kbuf{i}", [128, 1024], BF16, 34 + 2 * i) for i in range(2)]
    vbuf = [at(f"vbuf{i}", [128, 1024], BF16, 38 + 2 * i) for i in range(2)]
    pbuf = [at(f"pbuf{i}", [128, 2048], BF16, 42 + 4 * i) for i in range(2)]
    ybuf = at("ybuf", [128, 1024], BF16, 50)
    linvb = at("linvb", [128, 1024], F32, 52)
    wosb = at("wosb", [128, 16 * 2048], BF16, 0)
    ytb = [at(f"ytb{i}", [128, 2048], BF16, 64 + 4 * i) for i in range(2)]
    xtile = [at(f"xtile{i}", [128, 2048], F32, 72 + 8 * i) for i in range(2)]
    xf = at("xf", [128, 2048], F32, 88)
    xnf = at("xnf", [128, 2048], F32, 96)
    aout = at("aout", [128, 2048], F32, 104)
    x2t = at("x2t", [128, 2048], F32, 112)
    eoacc = at("eoacc", [128, 8 * 2048], F32, 0)
    w2res = at("w2res", [128, 8 * 2048], BF16, 64)
    gbuf = at("gbuf", [128, 8 * 1024], BF16, 96)
    w1buf = [at(f"w1buf{i}", [128, 2048], BF16, 112 + 4 * i) for i in range(2)]
    w3buf = [at(f"w3buf{i}", [128, 2048], BF16, 120 + 4 * i) for i in range(2)]
    sgl = at("sgl", [128, 1024], F32, 128)
    eotmp = at("eotmp", [128, 1024], F32, 132)
    dbgbuf = at("dbgbuf", [128, 1024], F32, 114)
    p6a = at("p6a", [128, 2048], F32, 64)
    p6x2 = at("p6x2", [128, 2048], F32, 72)
    p6o = at("p6o", [128, 2048], F32, 80)

    pS = nc.alloc_psum_tensor("pS", [128, 2048], F32)
    pA = pS[:, 0:1024]
    pB = pS[:, 1024:2048]
    pC = nc.alloc_psum_tensor("pC", [128, 1024], F32)
    pD = nc.alloc_psum_tensor("pD", [128, 512], F32)
    pE = nc.alloc_psum_tensor("pE", [128, 512], F32)
    psA = [pA, pB]

    QSP = e_(nc.semaphore("QSP"))
    QPL = e_(nc.semaphore("QPL"))
    CC = e_(nc.semaphore("CC"))
    SPE = e_(nc.semaphore("SPE"))
    SACT = e_(nc.semaphore("SACT"))
    SDVE = e_(nc.semaphore("SDVE"))
    SEMS = {"QSP": QSP, "QPL": QPL, "CC": CC, "PE": SPE, "ACT": SACT,
            "DVE": SDVE}

    recs = []
    cnt = dict(QSP=0, QPL=0, CC=0, PE=0, ACT=0, DVE=0)

    def rec(eng, emit, waits=(), inc=None):
        ws = tuple((s, v) for (s, v) in waits if v)
        for s, v in ws:
            assert v <= cnt[s], f"forward wait {s} {v} > {cnt[s]}"
        if inc:
            cnt[inc[0]] += inc[1]
        recs.append((eng, ws, emit, inc))
        return cnt[inc[0]] if inc else None

    def dma(q, out_ap, in_ap, waits=(), **kw):
        sem = "QSP" if q == "sp" else "QPL"
        return rec(q, lambda e: e.dma_start(out=out_ap, in_=in_ap, **kw),
                   waits, (sem, 16))

    def pe(emit, waits=(), bump=True):
        return rec("pe", emit, waits, ("PE", 1) if bump else None)

    def act(emit, waits=()):
        return rec("act", emit, waits, ("ACT", 1))

    def dve(emit, waits=()):
        return rec("dve", emit, waits, ("DVE", 1))

    def barrier():
        snap = dict(cnt)
        own = {"sp": "QSP", "pool": "QPL", "pe": "PE", "act": "ACT",
               "dve": "DVE"}
        for engname in ("sp", "pool", "pe", "act", "dve"):
            waits = tuple((s, v) for s, v in snap.items()
                          if s != own[engname] and s != "CC" and v > 0)
            recs.append((engname, waits, None, None))

    def finish():
        if KDBG:
            dbw = (("QPL", cnt["QPL"]), ("QSP", cnt["QSP"]),
                   ("PE", cnt["PE"]), ("ACT", cnt["ACT"]),
                   ("DVE", cnt["DVE"]))
            pairs = [(dbg_qT, qT.rearrange("h p t -> (h p) t")),
                     (dbg_ct, contrib[:, :]),
                     (dbg_yT, yT.rearrange("t p c -> (t p) c")),
                     (dbg_x2, x2s[:, :]),
                     (dbg_xnT, xnT.rearrange("a p t -> (a p) t")),
                     (dbg_comb, combsb[:, :])]
            for oi, (dst, srcap) in enumerate(pairs):
                dma("pool", dst[:, :], srcap, dbw if oi == 0 else ())
        recs.append(("pool", (("QPL", cnt["QPL"]),), None, None))
        recs.append(("sp", (("QSP", cnt["QSP"]),), None, None))
        with nc.Block() as block:
            def runner(engname):
                def _run(eng):
                    for rengine, waits, emit, inc in recs:
                        if rengine != engname:
                            continue
                        for s_, v in waits:
                            eng.wait_ge(SEMS[s_], v)
                        if emit is None:
                            continue
                        ins = emit(eng)
                        if inc:
                            ins.then_inc(SEMS[inc[0]], inc[1])
                return _run

            block.sync(runner("sp"))
            block.gpsimd(runner("pool"))
            block.tensor(runner("pe"))
            block.scalar(runner("act"))
            block.vector(runner("dve"))
        st.close()
        return nc

    dma("sp", padkill[:, :], padkill_in[:, :])
    dma("sp", sidx[:, :], kvidx[:, :])
    t_ident = dma("sp", identf[:, :], ident_in[:, :])
    dma("pool", ropept[:, :], ropept_in[:, :])
    for dc in range(16):
        dma("pool", gate_sb[:, dc * 8:(dc + 1) * 8], gate_t[dc, :, :])
    t_sin = 0
    dve(lambda e: e.memset(onescol[:, :], 1.0))
    dve(lambda e: e.memset(epsb[:, :], EPS))
    dve(lambda e: e.memset(onesrow[:, :], 1.0))
    dve(lambda e: e.memset(qn2[:, :], 0.0))
    dve(lambda e: e.memset(kn2own[:, :], 0.0))

    def emit_ln_tile(x_src, tok_src, src_sem, stg_war_qpl, prev_dve=0):
        w = ((src_sem, tok_src),)
        wp = w + ((("DVE", prev_dve),) if prev_dve else ())
        t1 = dve(lambda e: e.reduce_sum(nsc[0][:, :], x_src[:, :], axis=AX.X),
                 w)
        t2 = act(lambda e: e.activation(xf[:, :], x_src[:, :], AF.Square,
                                        accum_out=nsc[1][:, :]), wp)
        t3 = dve(lambda e: e.tensor_scalar_mul(nsc[2][:, :], nsc[0][:, :],
                                               1.0 / D), (("DVE", t1),))
        t4 = dve(lambda e: e.tensor_scalar_mul(nsc[3][:, :], nsc[1][:, :],
                                               1.0 / D), (("ACT", t2),))
        t5 = dve(lambda e: e.tensor_mul(nsc[4][:, :], nsc[2][:, :],
                                        nsc[2][:, :]), (("DVE", t3),))
        t6 = dve(lambda e: e.tensor_sub(nsc[5][:, :], nsc[3][:, :],
                                        nsc[4][:, :]),
                 (("DVE", t5), ("DVE", t4)))
        t7 = act(lambda e: e.activation(nsc[6][:, :], nsc[5][:, :], AF.Ln,
                                        bias=epsb[:, :]), (("DVE", t6),))
        t8 = act(lambda e: e.activation(nsc[7][:, :], nsc[6][:, :], AF.Exp,
                                        scale=-0.5), (("ACT", t7),))
        t9 = dve(lambda e: e.tensor_scalar(xf[:, :], x_src[:, :], nsc[2][:, :],
                                           None, ALU.subtract),
                 (("DVE", t3), ("ACT", t2)))
        t10 = dve(lambda e: e.tensor_scalar(xnf[:, :], xf[:, :], nsc[7][:, :],
                                            None, ALU.mult),
                  (("DVE", t9), ("ACT", t8)))
        tok_stg = 0
        for half in range(2):
            tok_pe = None
            for j in range(8):
                dcb = half * 8 + j
                wj = ()
                if j == 0:
                    wj = (("DVE", t10),) if half == 0 else (("DVE", tok_stg),)
                tok_pe = pe(
                    (lambda dcb, j: lambda e: e.transpose(
                        pC[:, j * 128:(j + 1) * 128],
                        xnf[:, dcb * 128:(dcb + 1) * 128],
                        identf[:, :]))(dcb, j), wj)
            sw = [("PE", tok_pe)]
            if half == 0 and stg_war_qpl:
                sw.append(("QPL", stg_war_qpl))
            tok_stg = dve(
                (lambda half: lambda e: e.tensor_copy(
                    stg[:, half * 1024:(half + 1) * 1024], pC[:, :]))(half),
                tuple(sw))
        return tok_stg

    t_xuse = [0, 0]
    t_stg_dma = 0
    t_p0prev = 0
    recs.append(("pe", (("QSP", t_ident),), None, None))
    for t in range(NT):
        wr = [("DVE", t_xuse[t % 2])] if t_xuse[t % 2] else []
        tok_x = dma("sp", xtile[t % 2][:, :], x_own[t * 128:(t + 1) * 128, :],
                    tuple(wr))
        tok_stg = emit_ln_tile(xtile[t % 2], tok_x, "QSP", t_stg_dma,
                               prev_dve=t_p0prev)
        t_xuse[t % 2] = tok_stg
        t_p0prev = tok_stg
        t_stg_dma = dma(
            "pool",
            xnT[:, :, t * 128:(t + 1) * 128].rearrange("dc p tj -> p dc tj"),
            stg.rearrange("p (dc tj) -> p dc tj", dc=16),
            (("DVE", tok_stg),))
    xnT_done = t_stg_dma
    barrier()

    if KPH < 1:
        return finish()
    dma("pool", cossb[:, :], cos_t[:, :])
    t_sin = dma("pool", sinsb[:, :], sin_t[:, :])
    t_wv = 0
    for dc in range(16):
        t_wv = dma("pool", wvsb[:, dc * 2048:(dc + 1) * 2048], wv_t[dc, :, :])
    t_wq = [0, 0]
    t_qkev = [0, 0]
    t_rope = [0, 0]
    t_mmtok = [0, 0]
    t_spill = [0, 0]
    t_tsq = [0, 0]
    t_pCr = 0
    t_norm_mm = 0
    t_normred = 0
    t_xnTb = 0
    for blk in range(NBLK):
        for dc in range(16):
            t_xnTb = dma("sp", xnTb[:, dc * 1024:(dc + 1) * 1024],
                         xnT[dc, :, blk * 1024:(blk + 1) * 1024],
                         (("QPL", xnT_done),) if (dc == 0 and blk == 0) else
                         ((("PE", cnt["PE"]),) if dc == 0 else ()))
        for m in range(32):
            i = m % 2
            t_wq[i] = dma("pool", wqbuf[i][:, :],
                          wqkv_t[m].rearrange("p dc fj -> p (dc fj)"),
                          (("PE", t_mmtok[i]),) if t_mmtok[i] else ())
            mmw = [("QPL", t_wq[i]), ("QSP", t_xnTb)]
            if t_qkev[i]:
                mmw.append(("DVE", t_qkev[i]))
            tok_mm = None
            for dc in range(16):
                for ns in range(2):
                    tok_mm = pe((lambda i, dc, ns: lambda e: e.matmul(
                        psA[i][:, ns * 512:(ns + 1) * 512],
                        wqbuf[i][:, dc * 128:(dc + 1) * 128],
                        xnTb[:, dc * 1024 + ns * 512:dc * 1024 + (ns + 1) * 512],
                        start=(dc == 0), stop=(dc == 15),
                        skip_group_check=True))(i, dc, ns),
                        tuple(mmw) if (dc == 0 and ns == 0) else ())
            t_mmtok[i] = tok_mm
            ev_w = [("PE", tok_mm)]
            if t_rope[i]:
                ev_w.append(("DVE", t_rope[i]))
            t_qkev[i] = dve((lambda i: lambda e: e.tensor_copy(
                qkbf[i][:, :], psA[i][:, :]))(i), tuple(ev_w))
            rw = [("DVE", t_qkev[i])]
            if t_pCr:
                rw.append(("DVE", t_pCr))
            tok_rot = None
            for ns in range(2):
                tok_rot = pe((lambda i, ns: lambda e: e.matmul(
                    pC[:, ns * 512:(ns + 1) * 512], ropept[:, :],
                    qkbf[i][:, ns * 512:(ns + 1) * 512],
                    start=True, stop=True, skip_group_check=True))(i, ns),
                    tuple(rw) if ns == 0 else ())
            cosw = (("QPL", t_sin),) if (blk == 0 and m == 0) else ()
            t_pCr = dve((lambda blk: lambda e: e.tensor_mul(
                ropet1[:, :], pC[:, :],
                sinsb[:, blk * 1024:(blk + 1) * 1024]))(blk),
                (("PE", tok_rot),) + cosw)
            tr2 = dve((lambda i, blk: lambda e: e.tensor_mul(
                ropet2[:, :], qkbf[i][:, :],
                cossb[:, blk * 1024:(blk + 1) * 1024]))(i, blk),
                (("DVE", t_pCr),))
            if KDBG and blk == 0 and m == 0:
                dma("pool", dbg_q0[:, :], qkbf[i][:, :], (("DVE", tr2),))
                t_pCr = dve(lambda e: e.tensor_copy(dbgbuf[:, :], pC[:, :]),
                            (("DVE", t_pCr),))
                dma("pool", dbg_rot[:, :], dbgbuf[:, :],
                    (("DVE", cnt["DVE"]),))
                dma("pool", dbg_rt1[:, :], ropet1[:, :], (("DVE", tr2),))
                dma("pool", dbg_rt2[:, :], ropet2[:, :], (("DVE", tr2),))
                dma("pool", dbg_cos[:, :], cossb[:, 0:1024],
                    (("QPL", t_sin),))
                dma("pool", dbg_sin[:, :], sinsb[:, 0:1024],
                    (("QPL", t_sin),))
            rope_w = [("DVE", tr2)]
            rope_w.append(("QPL", cnt["QPL"])) if (KDBG and blk == 0 and
                                                   m == 0) else None
            if t_spill[i]:
                rope_w.append(("QPL", t_spill[i]))
            if t_tsq[i]:
                rope_w.append(("ACT", t_tsq[i]))
            t_rope[i] = dve((lambda i: lambda e: e.tensor_add(
                ropedbf[i][:, :], ropet1[:, :], ropet2[:, :]))(i),
                tuple(rope_w))
            sq_w = [("DVE", t_rope[i])]
            if t_norm_mm:
                sq_w.append(("PE", t_norm_mm))
            t_tsq[i] = act((lambda i: lambda e: e.activation(
                sqbf[:, :], ropedbf[i][:, :], AF.Square))(i), tuple(sq_w))
            nmw = [("ACT", t_tsq[i])]
            if t_normred:
                nmw.append(("DVE", t_normred))
            pe(lambda e: e.matmul(pD[0:1, :], onescol[:, :], sqbf[:, 0:512],
                                  start=True, stop=True,
                                  skip_group_check=True), tuple(nmw))
            t_norm_mm = pe(lambda e: e.matmul(
                pE[0:1, :], onescol[:, :], sqbf[:, 512:1024],
                start=True, stop=True, skip_group_check=True))
            ta = dve(lambda e: e.reduce_max(n1a[:, :], pD[0:1, :], axis=AX.X),
                     (("PE", t_norm_mm),))
            tb = dve(lambda e: e.reduce_max(n1b[:, :], pE[0:1, :], axis=AX.X),
                     (("DVE", ta),))
            tc = dve(lambda e: e.tensor_max(n1a[:, :], n1a[:, :], n1b[:, :]),
                     (("DVE", tb),))
            dst = qn2 if m < 16 else kn2own
            hh = m % 16
            t_normred = dve((lambda dst, hh: lambda e: e.tensor_max(
                dst[0:1, hh:hh + 1], dst[0:1, hh:hh + 1], n1a[0:1, :]))(
                    dst, hh), (("DVE", tc),))
            if m < 16:
                t_spill[i] = dma("pool",
                                 qT[m, :, blk * 1024:(blk + 1) * 1024],
                                 ropedbf[i][:, :], (("DVE", t_rope[i]),))
            else:
                t_spill[i] = dma("pool",
                                 contrib[(m - 16) * 128:(m - 15) * 128,
                                         blk * 1024:(blk + 1) * 1024],
                                 ropedbf[i][:, :], (("DVE", t_rope[i]),))
        t_vev = [0, 0]
        t_vsp = [0, 0]
        for vt in range(8):
            t = blk * 8 + vt
            i = vt % 2
            vmw = [("QPL", t_wv), ("QSP", t_xnTb),
                   ("DVE", max(t_qkev)), ("DVE", max(t_vev))]
            tok_mm = None
            for dc in range(16):
                for ns in range(4):
                    ppp = psA[ns // 2]
                    tok_mm = pe((lambda dc, ns, vt, ppp: lambda e: e.matmul(
                        ppp[:, (ns % 2) * 512:((ns % 2) + 1) * 512],
                        xnTb[:, dc * 1024 + vt * 128:dc * 1024 + (vt + 1) * 128],
                        wvsb[:, dc * 2048 + ns * 512:dc * 2048 + (ns + 1) * 512],
                        start=(dc == 0), stop=(dc == 15),
                        skip_group_check=True))(dc, ns, vt, ppp),
                        tuple(vmw) if (dc == 0 and ns == 0) else ())
            evw = [("PE", tok_mm)]
            if t_vsp[i]:
                evw.append(("QPL", t_vsp[i]))
            dve((lambda i: lambda e: e.tensor_copy(
                voutb[i][:, 0:1024], pA[:, :]))(i), tuple(evw))
            t_vev[i] = dve((lambda i: lambda e: e.tensor_copy(
                voutb[i][:, 1024:2048], pB[:, :]))(i))
            t_vsp[i] = dma(
                "pool",
                contrib[2048:4096, :].rearrange(
                    "(h p) (tl d) -> p h tl d", p=128, d=128)[:, :, t, :],
                voutb[i].rearrange("p (h d) -> p h d", h=16),
                (("DVE", t_vev[i]),))
        t_qkev = [cnt["DVE"], cnt["DVE"]]
    tkb = dve(lambda e: e.tensor_copy(kn2bf[:, :], kn2own[:, :]),
              (("DVE", t_normred),))
    dma("pool", contrib[4096:4097, 0:16], kn2bf[0:1, :], (("DVE", tkb),))
    barrier()

    if KPH < 2:
        return finish()
    rec("pool", lambda e: e.collective_compute(
        "AllGather", ALU.bypass, replica_groups=[list(range(8))],
        ins=[contrib[:, :].opt()], outs=[gathered[:, :].opt()]),
        ((("QPL", cnt["QPL"])),), ("CC", 1))
    cc_tok = cnt["CC"]
    barrier()

    if KPH < 3:
        return finish()
    t_mask = 0
    for ck in range(8):
        t_mask = dma("sp", masksb[:, ck * 1024:(ck + 1) * 1024], trimask[ck])
    t_knr = 0
    for r in range(8):
        t_knr = dma("sp", knrow[0:1, r * 16:(r + 1) * 16],
                    gathered[4096 + r * CROWS:4097 + r * CROWS, 0:16],
                    (("CC", cc_tok),) if r == 0 else ())
    tk1 = dve(lambda e: e.reduce_max(
        kn2all[0:1, :],
        knrow[0:1, :].rearrange("a (r h) -> a h r", r=8), axis=AX.X),
        (("QSP", t_knr),))
    tk2 = dve(lambda e: e.tensor_mul(prodn[:, :], qn2[:, :], kn2all[:, :]),
              (("DVE", tk1),))
    tk3 = act(lambda e: e.activation(cvec[:, :], prodn[:, :], AF.Ln,
                                     scale=1.0 / DH), (("DVE", tk2),))
    tk4 = act(lambda e: e.activation(cvec[:, :], cvec[:, :], AF.Exp,
                                     scale=0.5), (("ACT", tk3),))
    tk5 = dve(lambda e: e.tensor_scalar_mul(cvec[:, :], cvec[:, :], 1.02),
              (("ACT", tk4),))
    tk6 = dve(lambda e: e.tensor_copy(cbf[:, :], cvec[:, :]), (("DVE", tk5),))
    tk7 = pe(lambda e: e.matmul(pD[:, 0:16], onesrow[:, :], cbf[0:1, :],
                                start=True, stop=True, skip_group_check=True),
             (("DVE", tk6),))
    tk8 = dve(lambda e: e.tensor_scalar_mul(negCall[:, :], pD[:, 0:16], -1.0),
              (("PE", tk7),))

    gview = gathered.rearrange("r c -> (r c)")[0:128 * 1024].rearrange(
        "(p c) -> p c", c=1024)

    t_kld = [0, 0]
    t_vld = [0, 0]
    t_pb = [0, 0]
    t_sps = 0
    t_ypc = 0
    t_yspill = 0
    t_pdpe = tk8
    t_quse = [0, 0]
    for h in range(H):
        t_bias = dve((lambda h: lambda e: e.tensor_scalar(
            biastab[:, :], padkill[:, :], negCall[:, h:h + 1], None,
            ALU.add))(h), (("DVE", tk8),))
        for s in range(2):
            qp = (h * 2 + s) % 2
            qw = [("PE", t_quse[qp])] if t_quse[qp] else []
            t_qld = dma("sp", qbuf2[qp][:, :],
                        qT[h, :, s * 1024:(s + 1) * 1024], tuple(qw))
            nslots = 4 if s == 0 else 8
            npairs = nslots * 4
            loaded = set()
            s_toks = {}

            def ensure_loaded(j, s=s, h=h):
                if j in loaded:
                    return
                loaded.add(j)
                slot = j if s == 0 else 4 + j
                jj = j % 2
                colk = slot * H + h
                colv = (NSLOT + slot) * H + h
                kw = [("CC", cc_tok)]
                if t_pb[jj]:
                    kw.append(("PE", t_pb[jj]))
                t_kld[jj] = rec("pool", (lambda jj, colk: lambda e:
                                e.indirect_dma_start(
                                    kbuf[jj][:, :], None, gview,
                                    IndirectOffsetOnAxis(
                                        sidx[:, colk:colk + 1], 0)))(jj, colk),
                                tuple(kw), ("QPL", 16))
                t_vld[jj] = rec("pool", (lambda jj, colv: lambda e:
                                e.indirect_dma_start(
                                    vbuf[jj][:, :], None, gview,
                                    IndirectOffsetOnAxis(
                                        sidx[:, colv:colv + 1], 0)))(jj, colv),
                                (), ("QPL", 16))

            def issue_S(pp, qp=qp):
                j = pp // 4
                ensure_loaded(j)
                jj = j % 2
                sw = [("QPL", t_kld[jj]), ("QSP", t_qld)]
                if t_sps:
                    sw.append(("ACT", t_sps))
                tok = None
                for c in range(2):
                    ck = (pp % 4) * 2 + c
                    for ns in range(2):
                        tok = pe((lambda jj, ck, c, ns, qp: lambda e: e.matmul(
                            pS[:, c * 1024 + ns * 512:
                               c * 1024 + (ns + 1) * 512],
                            kbuf[jj][:, ck * 128:(ck + 1) * 128],
                            qbuf2[qp][:, ns * 512:(ns + 1) * 512],
                            start=True, stop=True,
                            skip_group_check=True))(jj, ck, c, ns, qp),
                            tuple(sw) if (c == 0 and ns == 0) else ())
                s_toks[pp] = tok

            issue_S(0)
            tok_l = 0
            tok_pv = 0
            for pp in range(npairs):
                j = pp // 4
                jj = j % 2
                pj = pp % 2
                slot = j if s == 0 else 4 + j
                is_diag = (s == 0 and j == 3) or (s == 1 and j == 7)
                tok_s = s_toks.pop(pp)
                ew = [("PE", tok_s)]
                if is_diag:
                    ck0 = (pp % 4) * 2
                    tmsk = dve((lambda ck0: lambda e: e.tensor_add(
                        pS[:, :], pS[:, :],
                        masksb[:, ck0 * 1024:(ck0 + 2) * 1024]))(ck0),
                        (("PE", tok_s), ("QSP", t_mask)))
                    ew = [("DVE", tmsk)]
                pw = ew + [("DVE", t_bias)]
                if t_pb[pj]:
                    pw.append(("PE", t_pb[pj]))
                t_sps = act((lambda pj, slot: lambda e:
                             e.activation(
                                 pbuf[pj][:, :], pS[:, :], AF.Exp,
                                 bias=biastab[:, slot:slot + 1],
                                 scale=ATT_SCALE))(pj, slot),
                            tuple(pw))
                if pp + 1 < npairs:
                    issue_S(pp + 1)
                pvw = [("ACT", t_sps), ("QPL", t_vld[jj])]
                if pp == 0:
                    pvw.append(("DVE", t_ypc))
                st_f = (pp == 0)
                sp_f = (pp == npairs - 1)
                first_mm = True
                for c in range(2):
                    ck = (pp % 4) * 2 + c
                    for ns in range(2):
                        tok_pv = pe((lambda jj, ck, c, ns, st_f, sp_f, pj:
                                     lambda e: e.matmul(
                                         pC[:, ns * 512:(ns + 1) * 512],
                                         vbuf[jj][:, ck * 128:(ck + 1) * 128],
                                         pbuf[pj][:, c * 1024 + ns * 512:
                                                  c * 1024 + (ns + 1) * 512],
                                         start=(st_f and c == 0),
                                         stop=(sp_f and c == 1),
                                         skip_group_check=True))(
                                             jj, ck, c, ns, st_f, sp_f, pj),
                                    tuple(pvw) if first_mm else ())
                        first_mm = False
                lw = (("DVE", t_pdpe),) if pp == 0 else ()
                lps = [pD, pE]
                for c in range(2):
                    for ns in range(2):
                        tok_l = pe((lambda ns, c, st_f, sp_f, pj, lp:
                                    lambda e: e.matmul(
                                        lp[0:1, :], onescol[:, :],
                                        pbuf[pj][:, c * 1024 + ns * 512:
                                                 c * 1024 + (ns + 1) * 512],
                                        start=(st_f and c == 0),
                                        stop=(sp_f and c == 1),
                                        skip_group_check=True))(
                                            ns, c, st_f, sp_f, pj,
                                            lps[ns]),
                                   lw if (c == 0 and ns == 0) else (),
                                   bump=(c == 1 and ns == 1))
                t_pb[pj] = tok_l
            t_quse[qp] = tok_l
            e1 = dve(lambda e: e.tensor_copy(lrow[0:1, 0:512], pD[0:1, :]),
                     (("PE", tok_l),))
            e2 = dve(lambda e: e.tensor_copy(lrow[0:1, 512:1024], pE[0:1, :]),
                     (("DVE", e1),))
            e3 = dve(lambda e: e.reciprocal(linv[0:1, :], lrow[0:1, :]),
                     (("DVE", e2),))
            e4 = dve(lambda e: e.tensor_copy(linvbf[0:1, :], linv[0:1, :]),
                     (("DVE", e3),))
            bc = None
            for ns in range(2):
                bc = pe((lambda ns, lp: lambda e: e.matmul(
                    lp[:, :], onesrow[:, :],
                    linvbf[0:1, ns * 512:(ns + 1) * 512],
                    start=True, stop=True, skip_group_check=True))(
                        ns, [pD, pE][ns]),
                    (("DVE", e4),) if ns == 0 else ())
            e5a = dve(lambda e: e.tensor_copy(linvb[:, 0:512], pD[:, :]),
                      (("PE", bc),))
            e5 = dve(lambda e: e.tensor_copy(linvb[:, 512:1024], pE[:, :]),
                     (("DVE", e5a),))
            t_pdpe = e5
            yw = [("DVE", e5), ("PE", tok_pv)]
            if t_yspill:
                yw.append(("QPL", t_yspill))
            t_ypc = dve(lambda e: e.tensor_mul(ybuf[:, :], pC[:, :],
                                               linvb[:, :]), tuple(yw))
            t_yspill = dma(
                "pool",
                yT[:, :, h * 128:(h + 1) * 128].rearrange(
                    "tl p d -> p tl d")[:, s * 8:(s + 1) * 8, :],
                ybuf.rearrange("p (tl d) -> p tl d", d=128),
                (("DVE", t_ypc),))
    barrier()

    if KPH < 4:
        return finish()
    t_wo = 0
    for dc in range(16):
        t_wo = dma("pool", wosb[:, dc * 2048:(dc + 1) * 2048], wo_t[dc])
    t_yt = [0, 0]
    t_aev = 0
    t_gate_dve = 0
    xnT2_wtok = [0] * NT
    t_x2w = [0] * NT
    t_stg2_dma = 0
    t_p4prev = 0
    for t in range(NT):
        i = t % 2
        t_ytld = dma("sp", ytb[i][:, :], yT[t],
                     (("PE", t_yt[i]),) if t_yt[i] else ())
        t_xld = dma("sp", xtile[i][:, :], x_own[t * 128:(t + 1) * 128, :],
                    (("DVE", t_xuse[i]),) if t_xuse[i] else ())
        mw = [("QPL", t_wo), ("QSP", t_ytld)]
        if t_aev:
            mw.append(("DVE", t_aev))
        tok_mm = None
        for hh in range(16):
            for ns in range(4):
                ppp = psA[ns // 2]
                tok_mm = pe((lambda i, hh, ns, ppp: lambda e: e.matmul(
                    ppp[:, (ns % 2) * 512:((ns % 2) + 1) * 512],
                    ytb[i][:, hh * 128:(hh + 1) * 128],
                    wosb[:, hh * 2048 + ns * 512:hh * 2048 + (ns + 1) * 512],
                    start=(hh == 0), stop=(hh == 15),
                    skip_group_check=True))(i, hh, ns, ppp),
                    tuple(mw) if (hh == 0 and ns == 0) else ())
        t_yt[i] = tok_mm
        a1 = dve(lambda e: e.tensor_copy(aout[:, 0:1024], pA[:, :]),
                 (("PE", tok_mm),))
        t_aev = dve(lambda e: e.tensor_copy(aout[:, 1024:2048], pB[:, :]),
                    (("DVE", a1),))
        l1 = dve(lambda e: e.reduce_sum(nsc[0][:, :], aout[:, :], axis=AX.X),
                 (("DVE", t_aev),))
        l2 = act(lambda e: e.activation(xf[:, :], aout[:, :], AF.Square,
                                        accum_out=nsc[1][:, :]),
                 (("DVE", t_aev),))
        l3 = dve(lambda e: e.tensor_scalar_mul(nsc[2][:, :], nsc[0][:, :],
                                               1.0 / D), (("DVE", l1),))
        l4 = dve(lambda e: e.tensor_scalar_mul(nsc[3][:, :], nsc[1][:, :],
                                               1.0 / D), (("ACT", l2),))
        l5 = dve(lambda e: e.tensor_mul(nsc[4][:, :], nsc[2][:, :],
                                        nsc[2][:, :]), (("DVE", l3),))
        l6 = dve(lambda e: e.tensor_sub(nsc[5][:, :], nsc[3][:, :],
                                        nsc[4][:, :]),
                 (("DVE", l5), ("DVE", l4)))
        l7 = act(lambda e: e.activation(nsc[6][:, :], nsc[5][:, :], AF.Ln,
                                        bias=epsb[:, :]), (("DVE", l6),))
        l8 = act(lambda e: e.activation(nsc[7][:, :], nsc[6][:, :], AF.Exp,
                                        scale=-0.5), (("ACT", l7),))
        l9 = dve(lambda e: e.tensor_scalar(xf[:, :], aout[:, :], nsc[2][:, :],
                                           None, ALU.subtract),
                 (("DVE", l3), ("ACT", l2)))
        l10w = [("DVE", l9), ("ACT", l8), ("QSP", t_xld)]
        if t > 0:
            l10w.append(("QPL", t_x2w[t - 1]))
        l10 = dve((lambda i: lambda e: e.scalar_tensor_tensor(
            x2t[:, :], xf[:, :], nsc[7][:, :], xtile[i][:, :],
            ALU.mult, ALU.add))(i), tuple(l10w))
        t_xuse[i] = l10
        t_x2w[t] = dma("pool", x2s[t * 128:(t + 1) * 128, :], x2t[:, :],
                       (("DVE", l10),))
        tok_stg2 = emit_ln_tile(x2t, l10, "DVE", t_stg2_dma,
                                prev_dve=t_p4prev)
        t_stg2_dma = dma(
            "pool",
            xnT[:, :, t * 128:(t + 1) * 128].rearrange("dc p tj -> p dc tj"),
            stg.rearrange("p (dc tj) -> p dc tj", dc=16),
            (("DVE", tok_stg2),))
        xnT2_wtok[t] = t_stg2_dma
        t_p4prev = tok_stg2
        gw = [("DVE", tok_stg2)]
        if t_gate_dve:
            gw.append(("DVE", t_gate_dve))
        tok_g = None
        for dc in range(16):
            tok_g = pe((lambda dc: lambda e: e.matmul(
                pD[:, 0:8], stg[:, dc * 128:(dc + 1) * 128],
                gate_sb[:, dc * 8:(dc + 1) * 8],
                start=(dc == 0), stop=(dc == 15),
                skip_group_check=True))(dc),
                tuple(gw) if dc == 0 else ())
        g1 = dve(lambda e: e.reduce_max(nsc[0][:, :], pD[:, 0:8], axis=AX.X),
                 (("PE", tok_g),))
        g2 = dve(lambda e: e.tensor_scalar_mul(nsc[1][:, :], nsc[0][:, :],
                                               -1.0), (("DVE", g1),))
        g3 = act(lambda e: e.activation(xnf[:, 0:8], pD[:, 0:8], AF.Exp,
                                        bias=nsc[1][:, :]), (("DVE", g2),))
        g4 = dve(lambda e: e.reduce_sum(nsc[2][:, :], xnf[:, 0:8], axis=AX.X),
                 (("ACT", g3),))
        g5 = dve(lambda e: e.reciprocal(nsc[3][:, :], nsc[2][:, :]),
                 (("DVE", g4),))
        g6 = dve(lambda e: e.tensor_scalar(xnf[:, 8:16], xnf[:, 0:8],
                                           nsc[3][:, :], None, ALU.mult),
                 (("DVE", g5),))
        g7 = dve(lambda e: e.reduce_max(nsc[4][:, :], xnf[:, 8:16],
                                        axis=AX.X), (("DVE", g6),))
        g8 = dve(lambda e: e.tensor_scalar(xnf[:, 16:24], xnf[:, 8:16],
                                           nsc[4][:, :], None, ALU.is_ge),
                 (("DVE", g7),))
        g9 = dve(lambda e: e.scalar_tensor_tensor(
            xnf[:, 24:32], xnf[:, 16:24], KILL, xnf[:, 8:16],
            ALU.mult, ALU.add), (("DVE", g8),))
        g10 = dve(lambda e: e.reduce_max(nsc[5][:, :], xnf[:, 24:32],
                                         axis=AX.X), (("DVE", g9),))
        g11 = dve(lambda e: e.tensor_scalar(xnf[:, 16:24], xnf[:, 8:16],
                                            nsc[5][:, :], None, ALU.is_ge),
                  (("DVE", g10),))
        t_gate_dve = dve((lambda t: lambda e: e.tensor_mul(
            combsb[:, t * 8:(t + 1) * 8], xnf[:, 16:24], xnf[:, 8:16]))(t),
            (("DVE", g11),))
    barrier()

    if KPH < 5:
        return finish()
    t_gdve = [0, 0]
    t_hmm = [0, 0]
    t_w2res = 0
    t_eo = 0
    t_eodve = 0
    t_outprev = 0
    for blk in range(NBLK):
        for dc in range(16):
            t_xnTb = dma("sp", xnTb[:, dc * 1024:(dc + 1) * 1024],
                         xnT[dc, :, blk * 1024:(blk + 1) * 1024],
                         (("QPL", xnT2_wtok[NT - 1]), ("PE", cnt["PE"]))
                         if dc == 0 else ())
        w13_pref = {}

        def load_w13(ee, ic):
            key = (ee, ic)
            if key in w13_pref:
                return w13_pref[key]
            i = ic % 2
            w1w = [("PE", t_hmm[i])] if t_hmm[i] else []
            dma("pool", w1buf[i][:, :],
                w13_t[0, ee, ic].rearrange("p dc ij -> p (dc ij)"),
                tuple(w1w))
            t_w3 = dma("pool", w3buf[i][:, :],
                       w13_t[1, ee, ic].rearrange("p dc ij -> p (dc ij)"))
            w13_pref[key] = t_w3
            return t_w3

        for ee in range(KNE):
            t_sil = 0
            for ic in range(8):
                i = ic % 2
                t_w3 = load_w13(ee, ic)
                hw = [("QPL", t_w3), ("QSP", t_xnTb)]
                if t_sil:
                    hw.append(("ACT", t_sil))
                if ic == 0 and t_eodve:
                    hw.append(("DVE", t_eodve))
                tok_h = None
                for dc in range(16):
                    for ns in range(2):
                        tok_h = pe((lambda i, dc, ns: lambda e: e.matmul(
                            pA[:, ns * 512:(ns + 1) * 512],
                            w1buf[i][:, dc * 128:(dc + 1) * 128],
                            xnTb[:, dc * 1024 + ns * 512:
                                 dc * 1024 + (ns + 1) * 512],
                            start=(dc == 0), stop=(dc == 15),
                            skip_group_check=True))(i, dc, ns),
                            tuple(hw) if (dc == 0 and ns == 0) else ())
                h3w = [("DVE", t_gdve[1 - i])] if t_gdve[1 - i] else []
                if ic == 0 and t_eodve:
                    h3w.append(("DVE", t_eodve))
                tok_h3 = None
                for dc in range(16):
                    for ns in range(2):
                        tok_h3 = pe((lambda i, dc, ns: lambda e: e.matmul(
                            pB[:, ns * 512:(ns + 1) * 512],
                            w3buf[i][:, dc * 128:(dc + 1) * 128],
                            xnTb[:, dc * 1024 + ns * 512:
                                 dc * 1024 + (ns + 1) * 512],
                            start=(dc == 0), stop=(dc == 15),
                            skip_group_check=True))(i, dc, ns),
                            tuple(h3w) if (dc == 0 and ns == 0) else ())
                t_hmm[i] = tok_h3
                silw = [("PE", tok_h)]
                if t_gdve[1 - i]:
                    silw.append(("DVE", t_gdve[1 - i]))
                t_sil = act(lambda e: e.activation(sgl[:, :], pA[:, :],
                                                   AF.Silu), tuple(silw))
                t_gdve[i] = dve((lambda ic: lambda e: e.tensor_mul(
                    gbuf[:, ic * 1024:(ic + 1) * 1024], sgl[:, :],
                    pB[:, :]))(ic), (("ACT", t_sil), ("PE", tok_h3)))
            if not KEO:
                continue
            w2w = [("DVE", cnt["DVE"])]
            if t_eo:
                w2w.append(("PE", t_eo))
            for ic in range(8):
                t_w2res = dma("pool", w2res[:, ic * 2048:(ic + 1) * 2048],
                              w2_t[ee, ic], tuple(w2w) if ic == 0 else ())
            if ee + 1 < KNE:
                load_w13(ee + 1, 0)
                load_w13(ee + 1, 1)
            last_g = max(t_gdve)
            if KDBG and blk == 0 and ee == 0:
                dma("pool", dbg_g[:, :], gbuf[:, :], (("DVE", last_g),))
            eo_banks = [[pS[:, k * 512:(k + 1) * 512] for k in range(4)],
                        [pC[:, 0:512], pC[:, 512:1024], pD[:, :], pE[:, :]]]
            t_eoacc = [t_eodve, t_eodve]
            for tt in range(8):
                t = blk * 8 + tt
                tp_ = tt % 2
                bks = eo_banks[tp_]
                ew = [("QPL", t_w2res), ("DVE", last_g)]
                if t_eoacc[tp_]:
                    ew.append(("DVE", t_eoacc[tp_]))
                if tt < 2 and ee == 0:
                    ew.append(("DVE", cnt["DVE"]))
                tok_eo = None
                for ic in range(8):
                    for ds in range(4):
                        tok_eo = pe((lambda ic, ds, tt, bk: lambda e:
                                     e.matmul(
                                         bk,
                                         gbuf[:, ic * 1024 + tt * 128:
                                              ic * 1024 + (tt + 1) * 128],
                                         w2res[:, ic * 2048 + ds * 512:
                                               ic * 2048 + (ds + 1) * 512],
                                         start=(ic == 0), stop=(ic == 7),
                                         skip_group_check=True))(
                                             ic, ds, tt, bks[ds]),
                                    tuple(ew) if (ic == 0 and ds == 0) else ())
                t_eo = tok_eo
                accw = (("PE", tok_eo),)
                for ds in range(4):
                    ppp = bks[ds]
                    dst = eoacc[:, tt * 2048 + ds * 512:
                                tt * 2048 + (ds + 1) * 512]
                    cs = combsb[:, t * 8 + min(ee, 7):t * 8 + min(ee, 7) + 1]
                    if ee == 0:
                        t_eoacc[tp_] = dve((lambda dst, ppp, cs: lambda e:
                                            e.tensor_scalar(dst, ppp, cs,
                                                            None, ALU.mult))(
                                                dst, ppp, cs),
                                           accw if ds == 0 else ())
                    elif ee < E:
                        dve((lambda ppp, cs, ds: lambda e:
                             e.tensor_scalar(eotmp[:, (ds % 2) * 512:(ds % 2 + 1) * 512],
                                             ppp, cs, None, ALU.mult))(
                                 ppp, cs, ds),
                            accw if ds == 0 else ())
                        t_eoacc[tp_] = dve((lambda dst, ds: lambda e:
                                            e.tensor_add(
                                                dst, dst,
                                                eotmp[:, (ds % 2) * 512:
                                                      (ds % 2 + 1) * 512]))(
                                                dst, ds))
                    else:
                        t_eoacc[tp_] = dve((lambda dst, ppp: lambda e:
                                            e.tensor_add(dst, dst, ppp))(
                                                dst, ppp),
                                           accw if ds == 0 else ())
            t_eodve = max(t_eoacc)
            if KDBG and blk == 0 and ee == 0:
                dma("pool", dbg_eo0[:, :], eoacc[:, :], (("DVE", t_eodve),))
        if KDBG and blk == 0:
            dma("pool", dbg_eoF[:, :], eoacc[:, :], (("DVE", t_eodve),))
        t_p6prev = 0
        for tt in range(8 if KP6 else 0):
            t = blk * 8 + tt
            eot = eoacc[:, tt * 2048:(tt + 1) * 2048]
            f1 = dve((lambda eot: lambda e: e.reduce_sum(
                nsc[0][:, :], eot, axis=AX.X))(eot), (("DVE", t_eodve),))
            f2 = act((lambda eot: lambda e: e.activation(
                p6a[:, :], eot, AF.Square, accum_out=nsc[1][:, :]))(eot),
                (("DVE", t_eodve), ("QPL", t_w2res),
                 ("DVE", t_p6prev)))
            f3 = dve(lambda e: e.tensor_scalar_mul(nsc[2][:, :], nsc[0][:, :],
                                                   1.0 / D), (("DVE", f1),))
            f4 = dve(lambda e: e.tensor_scalar_mul(nsc[3][:, :], nsc[1][:, :],
                                                   1.0 / D), (("ACT", f2),))
            f5 = dve(lambda e: e.tensor_mul(nsc[4][:, :], nsc[2][:, :],
                                            nsc[2][:, :]), (("DVE", f3),))
            f6 = dve(lambda e: e.tensor_sub(nsc[5][:, :], nsc[3][:, :],
                                            nsc[4][:, :]),
                     (("DVE", f5), ("DVE", f4)))
            f7 = act(lambda e: e.activation(nsc[6][:, :], nsc[5][:, :], AF.Ln,
                                            bias=epsb[:, :]), (("DVE", f6),))
            f8 = act(lambda e: e.activation(nsc[7][:, :], nsc[6][:, :],
                                            AF.Exp, scale=-0.5),
                     (("ACT", f7),))
            t_x2r = dma("sp", p6x2[:, :], x2s[t * 128:(t + 1) * 128, :],
                        (("QPL", t_x2w[t]), ("DVE", t_eodve),
                         ("DVE", t_p6prev)))
            f9 = dve((lambda eot: lambda e: e.tensor_scalar(
                p6a[:, :], eot, nsc[2][:, :], None, ALU.subtract))(eot),
                (("DVE", f3), ("ACT", f2)))
            f10w = [("DVE", f9), ("DVE", f6), ("ACT", f8), ("QSP", t_x2r)]
            if t_outprev:
                f10w.append(("QPL", t_outprev))
            f10 = dve(lambda e: e.scalar_tensor_tensor(
                p6o[:, :], p6a[:, :], nsc[7][:, :], p6x2[:, :],
                ALU.mult, ALU.add), tuple(f10w))
            t_outprev = dma("pool", out[t * 128:(t + 1) * 128, :], p6o[:, :],
                            (("DVE", f10),))
            t_p6prev = f10

    return finish()



def _host_prepare(inputs):
    W_qkv = np.asarray(inputs["L0_attn_W_qkv"], np.float32)
    W_o = np.asarray(inputs["L0_attn_W_o"], np.float32)
    gate_w = np.asarray(inputs["L0_ffn_gate_w"], np.float32)
    w1 = np.asarray(inputs["L0_ffn_w1"], np.float32)
    w2 = np.asarray(inputs["L0_ffn_w2"], np.float32)
    w3 = np.asarray(inputs["L0_ffn_w3"], np.float32)
    ws1 = np.asarray(inputs["L0_ffn_ws1"], np.float32)
    ws2 = np.asarray(inputs["L0_ffn_ws2"], np.float32)
    ws3 = np.asarray(inputs["L0_ffn_ws3"], np.float32)

    shared = {}
    shared["wqkv_t"] = np.ascontiguousarray(
        W_qkv[:4096].reshape(32, 128, 16, 128).transpose(0, 3, 2, 1))
    shared["wv_t"] = np.ascontiguousarray(
        W_qkv[4096:].reshape(2048, 16, 128).transpose(1, 2, 0))
    shared["wo_t"] = np.ascontiguousarray(
        W_o.reshape(2048, 16, 128).transpose(1, 2, 0))
    shared["gate_t"] = np.ascontiguousarray(
        gate_w.reshape(8, 16, 128).transpose(1, 2, 0))
    w13 = np.zeros((2, NE, 8, 128, 16, 128), np.float32)
    for wi, warr, wsarr in ((0, w1, ws1), (1, w3, ws3)):
        a = warr.reshape(8, 8, 128, 16, 128)
        w13[wi, :8] = a.transpose(0, 1, 4, 3, 2)
        b = wsarr.reshape(8, 128, 16, 128)
        w13[wi, 8] = b.transpose(0, 3, 2, 1)
    shared["w13_t"] = np.ascontiguousarray(w13)
    w2t = np.zeros((NE, 8, 128, 2048), np.float32)
    a = w2.reshape(8, 4, 512, 8, 128)
    w2t[:8] = a.transpose(0, 3, 4, 1, 2).reshape(8, 8, 128, 2048)
    b = ws2.reshape(4, 512, 8, 128)
    w2t[8] = b.transpose(2, 3, 0, 1).reshape(8, 128, 2048)
    shared["w2_t"] = np.ascontiguousarray(w2t)
    tm = np.zeros((8, 128, 1024), np.float32)
    qq = np.arange(1024)[None, :]
    for ck in range(8):
        kk = ck * 128 + np.arange(128)[:, None]
        tm[ck] = np.where(qq >= kk, 0.0, NEG)
    shared["trimask"] = tm
    shared["ident"] = np.eye(128, dtype=np.float32)
    Pm = np.zeros((128, 128), np.float32)
    for j in range(64):
        Pm[2 * j, 2 * j + 1] = -1.0
        Pm[2 * j + 1, 2 * j] = 1.0
    shared["ropept"] = np.ascontiguousarray(Pm.T)

    invf = 1.0 / (10000.0 ** (np.arange(0, DH, 2, dtype=np.float64) / DH))
    pos = np.arange(L, dtype=np.float64)
    ang = pos[None, :] * np.repeat(invf, 2)[:, None]
    cos_all = np.cos(ang).astype(np.float32)
    sin_all = np.sin(ang).astype(np.float32)

    per_core = []
    x = np.asarray(inputs["L0_attn_x"], np.float32)
    for core in range(8):
        b, c = core // 4, core % 4
        lo, hi = _sub_ranges(core)
        rows = np.r_[1024 * lo:1024 * (lo + 1), 1024 * hi:1024 * (hi + 1)]
        d = dict(shared)
        d["x_own"] = np.ascontiguousarray(x[b].reshape(L, D)[rows])
        d["cos_t"] = np.ascontiguousarray(cos_all[:, rows])
        d["sin_t"] = np.ascontiguousarray(sin_all[:, rows])
        pk = np.zeros((128, NSLOT), np.float32)
        for j in range(3):
            if j >= c:
                pk[:, j] = KILL
        for j in range(7):
            if j >= 7 - c:
                pk[:, 4 + j] = KILL
        d["padkill"] = pk
        idx = np.zeros((128, 2 * NSLOT * H), np.int32)
        p = np.arange(128)

        def kt_of(s, j):
            if s == 0:
                return c if j == 3 else j
            return (7 - c) if j == 7 else j

        for s in range(2):
            nsl = 4 if s == 0 else 8
            for j in range(nsl):
                slot = j if s == 0 else 4 + j
                kt = kt_of(s, j)
                rho = 4 * b + (kt if kt <= 3 else 7 - kt)
                halfbit = 1 if kt > 3 else 0
                for h in range(H):
                    kr = rho * CROWS + h * 128 + p
                    vr = rho * CROWS + 2048 + h * 128 + p
                    idx[:, (0 * NSLOT + slot) * H + h] = kr * 2 + halfbit
                    idx[:, (1 * NSLOT + slot) * H + h] = vr * 2 + halfbit
        d["kvidx"] = idx
        per_core.append(d)
    return per_core


_NC_CACHE = {}


def kernel(**inputs):
    in_maps = _host_prepare(inputs)
    if "nc" not in _NC_CACHE:
        _NC_CACHE["nc"] = build_nc()
    nc = _NC_CACHE["nc"]
    res = run_bass_kernel_spmd(nc, in_maps, core_ids=list(range(8)))
    out = np.empty((B, L, D), np.float32)
    for core in range(8):
        b = core // 4
        lo, hi = _sub_ranges(core)
        o = np.asarray(res.results[core]["out"], np.float32)
        out[b, 1024 * lo:1024 * (lo + 1)] = o[:1024]
        out[b, 1024 * hi:1024 * (hi + 1)] = o[1024:]
    return out
